# version 11
# baseline (speedup 1.0000x reference)
import numpy as np
import ml_dtypes
import concourse.bass as bass
import concourse.mybir as mybir
from concourse.bass_utils import run_bass_kernel_spmd

F32 = mybir.dt.float32
BF16 = mybir.dt.bfloat16
I32 = mybir.dt.int32
AF = mybir.ActivationFunctionType
ALU = mybir.AluOpType
bf16 = ml_dtypes.bfloat16

D = 1024
DI = 1536
L = 8192
B = 4
NCORE = 8
TPC = 4096


class Prog:

    def __init__(self, nc):
        self.nc = nc
        self.seq = {"tensor": [], "vector": [], "scalar": [], "sync": [], "gpsimd": []}
        self.sems = {}

    def sem(self, name):
        if name not in self.sems:
            self.sems[name] = [self.nc.alloc_semaphore(name), 0]
        return self.sems[name][0]

    def bump(self, name, amount):
        self.sem(name)
        self.sems[name][1] += amount
        return self.sems[name][1]

    def val(self, name):
        self.sem(name)
        return self.sems[name][1]

    def emit(self, engine, fn):
        self.seq[engine].append(fn)

    def build(self):
        nc = self.nc
        with nc.Block() as blk:
            @blk.sync
            def _(e):
                for fn in self.seq["sync"]:
                    fn(e)

            @blk.gpsimd
            def _(e):
                for fn in self.seq["gpsimd"]:
                    fn(e)

            @blk.tensor
            def _(e):
                for fn in self.seq["tensor"]:
                    fn(e)

            @blk.vector
            def _(e):
                for fn in self.seq["vector"]:
                    fn(e)

            @blk.scalar
            def _(e):
                for fn in self.seq["scalar"]:
                    fn(e)
        return nc


class PsumPool:

    def __init__(self, nc, n, width, name="ps"):
        self.tiles = [nc.alloc_psum_tensor(f"{name}{i}", [128, width], F32) for i in range(n)]
        self.free_at = [None] * n
        self.idx = 0
        self.n = n

    def take(self):
        i = self.idx % self.n
        self.idx += 1
        return i, self.tiles[i], self.free_at[i]

    def release(self, i, sem_name, value):
        self.free_at[i] = (sem_name, value)


def build_mamba_A(chunk=128, tokens=TPC, debug=False, snap_chunk=None):
    import os
    GUARD = int(os.environ.get("GUARD", "0"))
    nch = tokens // chunk
    nc = bass.Bass("TRN2", target_bir_lowering=False)
    xh = nc.dram_tensor("xh", [8, 128, TPC + 2], F32, kind="ExternalInput")
    winT = nc.dram_tensor("winT", [8, 128, 3072], F32, kind="ExternalInput")
    dtwT = nc.dram_tensor("dtwT", [12, 128, 1536], F32, kind="ExternalInput")
    woutT = nc.dram_tensor("woutT", [12, 128, 1024], F32, kind="ExternalInput")
    cvec = nc.dram_tensor("cvec", [128, 12, 8], F32, kind="ExternalInput")
    out_main = nc.dram_tensor("out_main", [8, 128, TPC], BF16, kind="ExternalOutput")
    gmat = nc.dram_tensor("gmat", [12, 128, TPC], BF16, kind="ExternalOutput")
    totals = nc.dram_tensor("totals", [128, 12], F32, kind="ExternalOutput")
    if snap_chunk is not None:
        snap_g = nc.dram_tensor("snap_g", [128, 12 * chunk], F32, kind="ExternalOutput")
        snap_ae = nc.dram_tensor("snap_ae", [128, 12 * chunk], F32, kind="ExternalOutput")
        snap_sz = nc.dram_tensor("snap_sz", [128, 12 * chunk], F32, kind="ExternalOutput")
        sg_t = nc.alloc_sbuf_tensor("sg_t", [128, 12 * chunk], F32)
        sa_t = nc.alloc_sbuf_tensor("sa_t", [128, 12 * chunk], F32)
        ss_t = nc.alloc_sbuf_tensor("ss_t", [128, 12 * chunk], F32)
    if debug:
        dbg = {}
        for nm, wdt in [("d_xi", 12 * (chunk + 2)), ("d_xc", 12 * chunk), ("d_sz", 12 * chunk),
                        ("d_dt", 12 * chunk), ("d_ae", 12 * chunk), ("d_cm", 12 * chunk),
                        ("d_yb", 12 * chunk)]:
            dbg[nm] = nc.dram_tensor(nm, [128, wdt], F32, kind="ExternalOutput")

    w_in = nc.alloc_sbuf_tensor("w_in", [128, 8 * 3072], BF16)
    w_dt = nc.alloc_sbuf_tensor("w_dt", [128, 12 * 1536], BF16)
    w_out = nc.alloc_sbuf_tensor("w_out", [128, 12 * 1024], BF16)
    cv = nc.alloc_sbuf_tensor("cv", [128, 12 * 8], F32)

    W = chunk
    WH = chunk + 2
    xb = [nc.alloc_sbuf_tensor(f"xb{i}", [128, 8 * WH], BF16) for i in range(2)]
    xf = [nc.alloc_sbuf_tensor(f"xf{i}", [128, 8 * WH], F32) for i in range(2)]
    xi_t = nc.alloc_sbuf_tensor("xi", [128, 12 * WH], F32)
    dt_t = nc.alloc_sbuf_tensor("dt", [128, 12 * W], F32)
    sp_t = nc.alloc_sbuf_tensor("sp", [128, 12 * W], F32)
    ae_t = nc.alloc_sbuf_tensor("ae", [128, 12 * W], F32)
    xc_t = nc.alloc_sbuf_tensor("xc", [128, 12 * W], F32)
    w1_t = nc.alloc_sbuf_tensor("w1s", [128, 12 * W], F32)
    cm_t = nc.alloc_sbuf_tensor("cm", [128, 12 * W], F32)
    szb = nc.alloc_sbuf_tensor("szb", [128, 12 * W], F32)
    xcb = nc.alloc_sbuf_tensor("xcb", [128, 12 * W], BF16)
    yb = nc.alloc_sbuf_tensor("yb", [128, 12 * W], BF16)
    gbs = [nc.alloc_sbuf_tensor(f"gb{i}", [128, 12 * W], BF16) for i in range(2)]
    obs = [nc.alloc_sbuf_tensor(f"ob{i}", [128, 8 * W], BF16) for i in range(2)]
    ones = nc.alloc_sbuf_tensor("ones", [128, W], F32)
    carry = nc.alloc_sbuf_tensor("carry", [128, 16], F32)
    rs_t = nc.alloc_sbuf_tensor("rs", [128, 16], F32)

    P = Prog(nc)
    pool = PsumPool(nc, 7, WH)
    ps_dummy = nc.alloc_psum_tensor("ps_dummy", [128, 320], F32)
    for s in ["s_x", "s_xb", "s_wt", "s_cv", "s_mm", "s_evs", "s_evv", "s_ph", "s_phs", "s_st", "s_xc", "s_cy"]:
        P.sem(s)

    def g_pre(g):
        g.dma_start(w_in[:].rearrange("p (k m) -> p k m", k=8),
                    winT.ap().transpose([1, 0, 2])).then_inc(P.sem("s_wt"), 16)
        g.wait_ge(P.sem("s_wt"), 16)
        g.dma_start(w_dt[:].rearrange("p (k m) -> p k m", k=12),
                    dtwT.ap().transpose([1, 0, 2])).then_inc(P.sem("s_wt"), 16)
        g.wait_ge(P.sem("s_wt"), 32)
        g.dma_start(w_out[:].rearrange("p (k m) -> p k m", k=12),
                    woutT.ap().transpose([1, 0, 2])).then_inc(P.sem("s_wt"), 16)
    P.emit("gpsimd", g_pre)
    P.bump("s_wt", 48)

    def sy_pre(sy):
        sy.dma_start(cv[:], cvec.ap().rearrange("p a b -> p (a b)")).then_inc(P.sem("s_cv"), 16)
    P.emit("sync", sy_pre)
    P.bump("s_cv", 16)
    WT_FULL = P.val("s_wt")

    wbar_t = nc.alloc_sbuf_tensor("wbar_t", [128, 64], BF16)

    def s_wbar(s):
        s.wait_ge(P.sem("s_wt"), WT_FULL)
        ins = None
        for k in range(8):
            ins = s.activation(wbar_t[:], w_in[:, k * 3072 + 3008:k * 3072 + 3072], AF.Copy)
        for k in range(12):
            ins = s.activation(wbar_t[:], w_dt[:, k * 1536 + 1472:k * 1536 + 1536], AF.Copy)
        for k in range(12):
            ins = s.activation(wbar_t[:], w_out[:, k * 1024 + 960:k * 1024 + 1024], AF.Copy)
        ins.then_inc(P.sem("s_phs"), 1)
    P.emit("scalar", s_wbar)
    WBARA = P.bump("s_phs", 1)

    def v_pre(v):
        for _i in range(100):
            v.memset(ones[:], 1.0).then_inc(P.sem("s_spin"), 1)
            v.wait_ge(P.sem("s_spin"), _i + 1)
        v.memset(ones[:], 1.0)
        v.memset(carry[:], 0.0).then_inc(P.sem("s_ph"), 1)
    P.emit("vector", v_pre)
    P.bump("s_ph", 1)

    def xload(g, c, wait_xc, idx):
        if wait_xc is not None:
            g.wait_ge(P.sem("s_xc"), wait_xc)
        g.dma_start(
            xf[c % 2][:].rearrange("p (k m) -> p k m", k=8),
            xh.ap()[:, :, c * W:c * W + WH].transpose([1, 0, 2]),
        ).then_inc(P.sem("s_x"), 16)

    for c in range(min(2, nch)):
        P.emit("sync", lambda g, c=c: xload(g, c, None, c))
        P.bump("s_x", 16)

        def v_cast0(v, c=c):
            v.wait_ge(P.sem("s_x"), 16 * (c + 1))
            v.tensor_copy(xb[c % 2][:], xf[c % 2][:]).then_inc(P.sem("s_xb"), 1)
        P.emit("vector", v_cast0)
        P.bump("s_xb", 1)

    ph_y_hist = []

    def emit_drain():
        def t_drain(t):
            t.matmul(ps_dummy[:, 0:320], w_in[:, 0:128], xb[0][:, 0:320],
                     start=True, stop=True).then_inc(P.sem("s_mm"), 1)
        P.emit("tensor", t_drain)
        P.bump("s_mm", 1)

    for c in range(nch):
        xcur = xb[c % 2]
        gb = gbs[c % 2]
        ob = obs[c % 2]
        for m in range(24):
            bk, pt, freed = pool.take()
            x_need = (c + 1) * 16

            gphy = (ph_y_hist[-1] if ph_y_hist else 0) if (GUARD & 2 and m == 0) else None

            def t_mm(t, c=c, m=m, pt=pt, freed=freed, x_need=x_need, xcur=xcur, gphy=gphy):
                if c == 0 and m == 0:
                    t.wait_ge(P.sem("s_wt"), WT_FULL)
                    t.wait_ge(P.sem("s_cv"), 16)
                    t.wait_ge(P.sem("s_phs"), WBARA)
                if gphy is not None:
                    t.wait_ge(P.sem("s_ph"), gphy)
                if m == 0:
                    t.wait_ge(P.sem("s_xb"), c + 1)
                if freed is not None:
                    t.wait_ge(P.sem(freed[0]), freed[1])
                for k in range(8):
                    ins = t.matmul(
                        pt[:, :WH],
                        w_in[:, k * 3072 + m * 128: k * 3072 + m * 128 + 128],
                        xcur[:, k * WH:(k + 1) * WH],
                        start=(k == 0), stop=(k == 7),
                    )
                ins.then_inc(P.sem("s_mm"), 1)
            P.emit("tensor", t_mm)
            need = P.bump("s_mm", 1)
            if m == 23:
                evs_all = P.val("s_evs") + 1

                def v_xcons(v, evs_all=evs_all):
                    v.wait_ge(P.sem("s_evs"), evs_all)
                    v.engine_nop().then_inc(P.sem("s_xc"), 1)
                P.emit("vector", v_xcons)
                P.bump("s_xc", 1)

            if m < 12:
                gph = P.val("s_ph") if GUARD & 1 else None

                def s_ev(s, m=m, pt=pt, need=need, gph=gph):
                    if gph is not None:
                        s.wait_ge(P.sem("s_ph"), gph)
                    s.wait_ge(P.sem("s_mm"), need + 1)
                    s.activation(xi_t[:, m * WH:(m + 1) * WH], pt[:, :WH], AF.Copy).then_inc(
                        P.sem("s_evs"), 1)
                P.emit("scalar", s_ev)
                pool.release(bk, "s_evs", P.bump("s_evs", 1))
            else:
                prev_ph_y = ph_y_hist[-1] if (m == 12 and ph_y_hist) else None

                def s_ev(s, m=m, pt=pt, need=need, prev_ph_y=prev_ph_y):
                    if prev_ph_y is not None:
                        s.wait_ge(P.sem("s_ph"), prev_ph_y)
                    s.wait_ge(P.sem("s_mm"), need + 1)
                    s.activation(szb[:, (m - 12) * W:(m - 11) * W], pt[:, 1:1 + W],
                                 AF.Silu).then_inc(P.sem("s_evs"), 1)
                P.emit("scalar", s_ev)
                pool.release(bk, "s_evs", P.bump("s_evs", 1))

        emit_drain()

        if c + 2 < nch:
            P.emit("sync", lambda g, c2=c + 2, w=c + 1: xload(g, c2, w, c2))
            P.bump("s_x", 16)
            x_exp = P.val("s_x")

            def v_cast(v, c2=c + 2, x_exp=x_exp):
                v.wait_ge(P.sem("s_x"), x_exp)
                v.tensor_copy(xb[c2 % 2][:], xf[c2 % 2][:]).then_inc(P.sem("s_xb"), 1)
            P.emit("vector", v_cast)
            P.bump("s_xb", 1)

        evs_need = P.val("s_evs")

        gmm = P.val("s_mm") if GUARD & 4 else None

        def v_conv(v, evs_need=evs_need, gmm=gmm):
            if gmm is not None:
                v.wait_ge(P.sem("s_mm"), gmm)
            v.wait_ge(P.sem("s_evs"), evs_need)
            for j in range(12):
                o = j * WH
                a = w1_t[:, j * W:(j + 1) * W]
                bsl = cm_t[:, j * W:(j + 1) * W]
                v.tensor_scalar_mul(a, xi_t[:, o:o + W], cv[:, j * 8:j * 8 + 1])
                v.scalar_tensor_tensor(bsl, xi_t[:, o + 1:o + 1 + W],
                                       cv[:, j * 8 + 1:j * 8 + 2], a, ALU.mult, ALU.add)
                ins = v.scalar_tensor_tensor(a, xi_t[:, o + 2:o + 2 + W],
                                             cv[:, j * 8 + 2:j * 8 + 3], bsl, ALU.mult, ALU.add)
            ins.then_inc(P.sem("s_ph"), 1)
        P.emit("vector", v_conv)
        ph_conv = P.bump("s_ph", 1)

        gmm2 = P.val("s_mm") if GUARD & 8 else None

        def s_conv(s, ph_conv=ph_conv, gmm2=gmm2):
            if gmm2 is not None:
                s.wait_ge(P.sem("s_mm"), gmm2)
            s.wait_ge(P.sem("s_ph"), ph_conv)
            for j in range(12):
                s.activation(xc_t[:, j * W:(j + 1) * W], w1_t[:, j * W:(j + 1) * W],
                             AF.Silu, bias=cv[:, j * 8 + 3:j * 8 + 4])
            s.activation(xcb[:], xc_t[:], AF.Copy).then_inc(P.sem("s_phs"), 1)
        P.emit("scalar", s_conv)
        phs_xc = P.bump("s_phs", 1)

        for m in range(12):
            bk, pt, freed = pool.take()

            def t_dt(t, m=m, pt=pt, freed=freed, phs_xc=phs_xc):
                if m == 0:
                    t.wait_ge(P.sem("s_phs"), phs_xc)
                if freed is not None:
                    t.wait_ge(P.sem(freed[0]), freed[1])
                for k in range(12):
                    ins = t.matmul(
                        pt[:, :W],
                        w_dt[:, k * 1536 + m * 128: k * 1536 + m * 128 + 128],
                        xcb[:, k * W:(k + 1) * W],
                        start=(k == 0), stop=(k == 11),
                    )
                ins.then_inc(P.sem("s_mm"), 1)
            P.emit("tensor", t_dt)
            need = P.bump("s_mm", 1)

            def s_dtev(s, m=m, pt=pt, need=need):
                s.wait_ge(P.sem("s_mm"), need + 1)
                s.tensor_scalar_add(dt_t[:, m * W:(m + 1) * W], pt[:, :W],
                                    cv[:, m * 8 + 4:m * 8 + 5]).then_inc(P.sem("s_evv"), 1)
            P.emit("vector", s_dtev)
            pool.release(bk, "s_evv", P.bump("s_evv", 1))

        emit_drain()
        evs_dt = P.val("s_evv")

        def s_softplus(s, evs_dt=evs_dt):
            s.wait_ge(P.sem("s_evv"), evs_dt)
            s.activation(sp_t[:], dt_t[:], AF.Relu)
            s.activation(dt_t[:], dt_t[:], AF.Abs)
            s.activation(dt_t[:], dt_t[:], AF.Exp, scale=-1.0)
            s.activation(dt_t[:], dt_t[:], AF.Ln, bias=1.0).then_inc(P.sem("s_phs"), 1)
        P.emit("scalar", s_softplus)
        phs_sp = P.bump("s_phs", 1)

        def v_dt(v, phs_sp=phs_sp):
            v.wait_ge(P.sem("s_phs"), phs_sp)
            v.tensor_add(dt_t[:], dt_t[:], sp_t[:]).then_inc(P.sem("s_ph"), 1)
        P.emit("vector", v_dt)
        ph_dt = P.bump("s_ph", 1)

        def s_aexp(s, ph_dt=ph_dt):
            s.wait_ge(P.sem("s_ph"), ph_dt)
            for j in range(12):
                ins = s.activation(ae_t[:, j * W:(j + 1) * W], dt_t[:, j * W:(j + 1) * W],
                                   AF.Exp, scale=cv[:, j * 8 + 5:j * 8 + 6])
            ins.then_inc(P.sem("s_phs"), 1)
        P.emit("scalar", s_aexp)
        phs_ae = P.bump("s_phs", 1)

        if snap_chunk == c:
            def v_snap(v, phs_ae=phs_ae):
                v.wait_ge(P.sem("s_phs"), phs_ae)
                v.tensor_copy(sg_t[:], xc_t[:])
                v.tensor_copy(sa_t[:], ae_t[:])
                ins = v.tensor_copy(ss_t[:], dt_t[:])
                ins.then_inc(P.sem("s_ph"), 1)
            P.emit("vector", v_snap)
            snap_ph = P.bump("s_ph", 1)

            def sy_snap(sy, snap_ph=snap_ph):
                sy.wait_ge(P.sem("s_ph"), snap_ph)
                sy.dma_start(snap_g.ap(), sg_t[:]).then_inc(P.sem("s_st"), 16)
                sy.dma_start(snap_ae.ap(), sa_t[:]).then_inc(P.sem("s_st"), 16)
                sy.dma_start(snap_sz.ap(), ss_t[:]).then_inc(P.sem("s_st"), 16)
            P.emit("sync", sy_snap)
            P.bump("s_st", 48)

        st_prev = P.val("s_st")

        cyw = P.val("s_cy")

        def v_wy(v, phs_ae=phs_ae, st_prev=st_prev, cyw=cyw, gb=gb):
            v.wait_ge(P.sem("s_st"), st_prev)
            v.wait_ge(P.sem("s_phs"), phs_ae)
            v.wait_ge(P.sem("s_cy"), cyw)
            v.tensor_mul(w1_t[:], xc_t[:], dt_t[:])
            v.tensor_mul(dt_t[:], w1_t[:], ae_t[:])
            for j in range(12):
                v.reduce_sum(rs_t[:, j:j + 1], dt_t[:, j * W:(j + 1) * W],
                             axis=mybir.AxisListType.X)
            for j in range(12):
                v.tensor_tensor_scan(
                    cm_t[:, j * W:(j + 1) * W], ones[:, :W], dt_t[:, j * W:(j + 1) * W],
                    carry[:, j:j + 1], ALU.mult, ALU.add)
            v.tensor_mul(w1_t[:], cm_t[:], ae_t[:])
            for j in range(12):
                v.scalar_tensor_tensor(
                    dt_t[:, j * W:(j + 1) * W], xc_t[:, j * W:(j + 1) * W],
                    cv[:, j * 8 + 6:j * 8 + 7], w1_t[:, j * W:(j + 1) * W],
                    ALU.mult, ALU.add)
            v.tensor_mul(yb[:], dt_t[:], szb[:])
            ins = v.tensor_mul(gb[:], ae_t[:], szb[:])
            ins.then_inc(P.sem("s_ph"), 1)
        P.emit("vector", v_wy)
        ph_y = P.bump("s_ph", 1)
        ph_y_hist.append(ph_y)



        def g_carry(g, ph_y=ph_y):
            g.wait_ge(P.sem("s_ph"), ph_y)
            g.tensor_add(carry[:, 0:12], carry[:, 0:12], rs_t[:, 0:12]).then_inc(
                P.sem("s_cy"), 1)
        P.emit("gpsimd", g_carry)
        P.bump("s_cy", 1)

        for m in range(8):
            bk, pt, freed = pool.take()

            def t_wo(t, m=m, pt=pt, freed=freed, ph_y=ph_y):
                if m == 0:
                    t.wait_ge(P.sem("s_ph"), ph_y)
                if freed is not None:
                    t.wait_ge(P.sem(freed[0]), freed[1])
                for k in range(12):
                    ins = t.matmul(
                        pt[:, :W],
                        w_out[:, k * 1024 + m * 128: k * 1024 + m * 128 + 128],
                        yb[:, k * W:(k + 1) * W],
                        start=(k == 0), stop=(k == 11),
                    )
                ins.then_inc(P.sem("s_mm"), 1)
            P.emit("tensor", t_wo)
            need = P.bump("s_mm", 1)

            def v_oev(v, m=m, pt=pt, need=need, ob=ob):
                v.wait_ge(P.sem("s_mm"), need + 1)
                v.tensor_copy(ob[:, m * W:(m + 1) * W], pt[:, :W]).then_inc(P.sem("s_evv"), 1)
            P.emit("vector", v_oev)
            pool.release(bk, "s_evv", P.bump("s_evv", 1))

        emit_drain()
        evv_o = P.val("s_evv")

        def sy_store(sy, c=c, evv_o=evv_o, ph_y=ph_y, gb=gb, ob=ob):
            sy.wait_ge(P.sem("s_evv"), evv_o)
            sy.dma_start(
                out_main.ap()[:, :, c * W:(c + 1) * W].transpose([1, 0, 2]),
                ob[:].rearrange("p (k m) -> p k m", k=8),
            ).then_inc(P.sem("s_st"), 16)
            sy.wait_ge(P.sem("s_ph"), ph_y)
            sy.dma_start(
                gmat.ap()[:, :, c * W:(c + 1) * W].transpose([1, 0, 2]),
                gb[:].rearrange("p (k m) -> p k m", k=12),
            ).then_inc(P.sem("s_st"), 16)
        P.emit("sync", sy_store)
        P.bump("s_st", 32)

    if debug:
        ph_last_d = P.val("s_ph")

        def sy_dbg(sy, ph_last_d=ph_last_d):
            sy.wait_ge(P.sem("s_ph"), ph_last_d)
            sy.dma_start(dbg["d_xi"].ap(), xi_t[:]).then_inc(P.sem("s_st"), 16)
            sy.dma_start(dbg["d_xc"].ap(), xc_t[:]).then_inc(P.sem("s_st"), 16)
            sy.dma_start(dbg["d_sz"].ap(), szb[:]).then_inc(P.sem("s_st"), 16)
            sy.dma_start(dbg["d_dt"].ap(), dt_t[:]).then_inc(P.sem("s_st"), 16)
            sy.dma_start(dbg["d_ae"].ap(), ae_t[:]).then_inc(P.sem("s_st"), 16)
            sy.dma_start(dbg["d_cm"].ap(), cm_t[:]).then_inc(P.sem("s_st"), 16)
        P.emit("sync", sy_dbg)

        def g_dbg(g, ph_last_d=ph_last_d):
            g.wait_ge(P.sem("s_ph"), ph_last_d)
            g.dma_start(dbg["d_yb"].ap(), yb[:]).then_inc(P.sem("s_st"), 16)
        P.emit("gpsimd", g_dbg)
        P.bump("s_st", 7 * 16)

    def t_fin(t):
        t.matmul(ps_dummy[:, 0:2], w_in[:, 0:128], xb[0][:, 0:2],
                 start=True, stop=True).then_inc(P.sem("s_mm"), 1)
    P.emit("tensor", t_fin)
    P.bump("s_mm", 1)

    def s_fin(s):
        s.activation(ones[:, 0:2], ones[:, 0:2], AF.Copy).then_inc(P.sem("s_evs"), 1)
    P.emit("scalar", s_fin)
    P.bump("s_evs", 1)

    def v_fin(v):
        v.tensor_copy(carry[:, 14:15], carry[:, 14:15]).then_inc(P.sem("s_evv"), 1)
    P.emit("vector", v_fin)
    P.bump("s_evv", 1)

    ph_last = P.val("s_ph")
    st_final = P.val("s_st") + 16

    cy_last = P.val("s_cy")

    def sy_tot(sy, ph_last=ph_last, st_final=st_final, cy_last=cy_last):
        sy.wait_ge(P.sem("s_cy"), cy_last)
        sy.wait_ge(P.sem("s_ph"), ph_last)
        sy.dma_start(totals.ap()[:, :], carry[:, 0:12]).then_inc(P.sem("s_st"), 16)
        sy.wait_ge(P.sem("s_st"), st_final)
    P.emit("sync", sy_tot)
    P.bump("s_st", 16)

    return P.build()


def host_prep_mamba_A(inputs):
    x = np.asarray(inputs["L0_attn_x"])
    W_in = np.asarray(inputs["L0_attn_W_in"])
    dt_w = np.asarray(inputs["L0_attn_dt_w"])
    W_out = np.asarray(inputs["L0_attn_W_out"])
    conv_w = np.asarray(inputs["L0_attn_conv_w"])
    conv_b = np.asarray(inputs["L0_attn_conv_b"])
    dt_b = np.asarray(inputs["L0_attn_dt_b"])
    A = np.asarray(inputs["L0_attn_A"])
    Dp = np.asarray(inputs["L0_attn_Dp"])

    winT = np.ascontiguousarray(W_in.T).reshape(8, 128, 3072)
    dtwT = np.ascontiguousarray(dt_w.T).reshape(12, 128, 1536)
    woutT = np.ascontiguousarray(W_out.T).reshape(12, 128, 1024)
    cvec = np.zeros((128, 12, 8), np.float32)
    for idx, arr in enumerate([conv_w[:, 0, 0], conv_w[:, 0, 1], conv_w[:, 0, 2],
                               conv_b, dt_b, A, Dp]):
        cvec[:, :, idx] = arr.reshape(12, 128).T
    in_maps = []
    for core in range(NCORE):
        b, h = core // 2, core % 2
        x_cm = np.ascontiguousarray(x[b].T)
        xhm = np.zeros((1024, TPC + 2), np.float32)
        t0 = h * TPC
        xhm[:, 1:TPC + 1] = x_cm[:, t0:t0 + TPC]
        if t0 > 0:
            xhm[:, 0] = x_cm[:, t0 - 1]
        if t0 + TPC < L:
            xhm[:, TPC + 1] = x_cm[:, t0 + TPC]
        in_maps.append({
            "xh": np.ascontiguousarray(xhm.reshape(8, 128, TPC + 2)),
            "winT": winT, "dtwT": dtwT, "woutT": woutT,
            "cvec": cvec,
        })
    return in_maps


def _col128(a):
    return np.ascontiguousarray(np.asarray(a, np.float32).reshape(-1, 128).T)


def _conv_consts():
    R = 128
    W128 = np.exp(-2j * np.pi * np.outer(np.arange(R), np.arange(R)) / R)
    twn = np.exp(-2j * np.pi / 16384)
    F1c = W128[:64, :]
    f1s = np.zeros((2, 128, 128), np.float32)
    f1s[0, 0:64] = F1c.real
    f1s[0, 64:128] = -F1c.imag
    f1s[1, 0:64] = F1c.imag
    f1s[1, 64:128] = F1c.real
    f1r = np.zeros((2, 64, 128), np.float32)
    f1r[0] = F1c.real
    f1r[1] = F1c.imag
    wsh = np.zeros((3, 128, 192), np.float32)
    WI2 = np.conj(W128[:, :64])
    for a, (w2p, wi2p) in enumerate(((W128.real, WI2.real), (W128.imag, WI2.imag),
                                     (-W128.imag, -WI2.imag))):
        wsh[a, :, 0:128] = w2p
        wsh[a, :, 128:192] = wi2p
    ang = 2.0 * np.pi * np.outer(np.arange(128), np.arange(128)) / 16384.0
    twcm = np.zeros((128, 256), np.float32)
    twcm[:, 0:128] = np.cos(ang)
    twcm[:, 128:256] = -np.sin(ang)
    tw2cm = np.zeros((128, 256), np.float32)
    tw2cm[:, 0:128] = (np.cos(ang) / 16384.0).T.astype(np.float32).T
    tw2cm[:, 0:128] = np.cos(ang).T / 16384.0
    tw2cm[:, 128:256] = np.sin(ang).T / 16384.0
    WI1 = np.conj(W128)
    WI1p = WI1[::-1, :]
    WI1p0 = WI1[(128 - np.arange(128)) % 128, :]
    wi1c = np.zeros((9, 128, 128), np.float32)
    wi1c[0] = WI1.real; wi1c[1] = WI1.imag; wi1c[2] = -WI1.imag
    wi1c[3] = WI1p.real; wi1c[4] = WI1p.imag; wi1c[5] = -WI1p.imag
    wi1c[6] = WI1p0.real; wi1c[7] = WI1p0.imag; wi1c[8] = -WI1p0.imag
    prevc = np.zeros((2, 128, 128), np.float32)
    for j in range(128):
        prevc[0, 127 - j, j] = 1.0
        prevc[1, (128 - j) % 128, j] = 1.0
    t_rep = np.ascontiguousarray(
        np.broadcast_to(np.linspace(0, 1, L, dtype=np.float32)[None, :], (65, L)))
    idn = np.eye(128, dtype=np.float32)
    return dict(f1s=f1s, f1r=f1r, wsh=wsh, twc=twcm, tw2c=tw2cm, wi1c=wi1c,
                prevc=prevc, t_rep=t_rep, idn=idn)


def _ffn_maps(inp, pfx, xins):
    base = {
        'q1': np.ascontiguousarray(inp[pfx + 'fc1_q'].reshape(8, 128, 4096)),
        'q2': np.ascontiguousarray(inp[pfx + 'fc2_q'].reshape(32, 128, 1024)),
        'sc1': np.ascontiguousarray(np.broadcast_to(
            inp[pfx + 'fc1_scales'][:, None, :], (8, 128, 4096))),
        'sc2': np.ascontiguousarray(np.broadcast_to(
            inp[pfx + 'fc2_scales'][:, None, :], (32, 128, 1024))),
        'b1': _col128(inp[pfx + 'fc1_bias']), 'b2': _col128(inp[pfx + 'fc2_bias']),
        'gam': _col128(inp[pfx + 'ln_gamma']), 'bet': _col128(inp[pfx + 'ln_beta']),
    }
    return [dict(base, xin=np.ascontiguousarray(xi)) for xi in xins]


def kernel(**inputs):
    inp = {k: np.asarray(v) for k, v in inputs.items()}
    cores = list(range(NCORE))

    resA = run_bass_kernel_spmd(build_mamba_A(), host_prep_mamba_A(inp), core_ids=cores)

    x = inp['L0_attn_x']
    woutT = np.ascontiguousarray(inp['L0_attn_W_out'].T).reshape(12, 128, 1024)
    mapsB1 = []
    for core in cores:
        b, h = core // 2, core % 2
        x_cm = np.ascontiguousarray(x[b].T[:, h * TPC:(h + 1) * TPC]).reshape(8, 128, TPC)
        rA = resA.results[core]
        tprev = np.zeros((128, 12), np.float32) if h == 0 \
            else np.ascontiguousarray(resA.results[core - 1]['totals'])
        mapsB1.append({'xres': x_cm, 'rhs': rA['gmat'], 'base': rA['out_main'],
                       'tprev': tprev, 'wT': woutT})
    resB1 = run_bass_kernel_spmd(
        build_mm_res_ln(nk=12, scale_rhs=True, has_base=True, xres_bf16=False, chunk=512),
        mapsB1, core_ids=cores)

    resB2 = run_bass_kernel_spmd(
        build_ffn(chunk=256),
        _ffn_maps(inp, 'L0_ffn_', [resB1.results[c]['x1'] for c in cores]),
        core_ids=cores)

    cc = _conv_consts()
    W_in1 = inp['L1_attn_W_in']
    f_w1 = inp['L1_attn_f_w1']; f_b1 = inp['L1_attn_f_b1']
    f_w2 = inp['L1_attn_f_w2']; f_b2 = inp['L1_attn_f_b2']
    x2f_b = []
    for b in range(B):
        x2f_b.append(np.ascontiguousarray(np.concatenate(
            [resB2.results[2 * b]['x2'], resB2.results[2 * b + 1]['x2']], axis=2)))
    mapsC = []
    for core in cores:
        b, cg = core // 2, core % 2
        sel = np.concatenate([np.arange(cg * 512, cg * 512 + 512) + g * 1024
                              for g in range(3)])
        winT1 = np.ascontiguousarray(W_in1.T[:, sel]).reshape(8, 128, 1536)
        selv = np.arange(cg * 512, cg * 512 + 512)
        w2aug = np.zeros((2, 65, 512), np.float32)
        for o in range(2):
            w2aug[o, 0:64] = f_w2[o][selv, :].T
            w2aug[o, 64] = f_b2[o][selv]
        mapsC.append(dict(cc, x2f=x2f_b[b], winT1=winT1,
                          fw1=np.ascontiguousarray(f_w1[:, :, 0].T),
                          fb1=np.ascontiguousarray(f_b1.T), w2aug=w2aug))
    resC = run_bass_kernel_spmd(build_conv(), mapsC, core_ids=cores)

    wT1 = np.ascontiguousarray(inp['L1_attn_W_out'].T).reshape(8, 128, 1024)
    mapsD = []
    for core in cores:
        b, h = core // 2, core % 2
        yk = np.concatenate([resC.results[2 * b]['y_out'],
                             resC.results[2 * b + 1]['y_out']], axis=0)
        mapsD.append({'xres': resB2.results[core]['x2'],
                      'rhs': np.ascontiguousarray(yk[:, :, h * TPC:(h + 1) * TPC]),
                      'wT': wT1})
    resD = run_bass_kernel_spmd(
        build_mm_res_ln(nk=8, scale_rhs=False, has_base=False, xres_bf16=True, chunk=512),
        mapsD, core_ids=cores)

    resE = run_bass_kernel_spmd(
        build_ffn(chunk=256, out_f32=True),
        _ffn_maps(inp, 'L1_ffn_', [resD.results[c]['x1'] for c in cores]),
        core_ids=cores)

    out = np.empty((B, L, D), np.float32)
    for core in cores:
        b, h = core // 2, core % 2
        out[b, h * TPC:(h + 1) * TPC, :] = \
            resE.results[core]['x2'].reshape(D, TPC).T
    return out


def emit_drain_fn(P, ps_dummy, wsb, xsb, wait_ph=None):
    def t_drain(t, wait_ph=wait_ph):
        if wait_ph is not None:
            t.wait_ge(P.sem("s_ph"), wait_ph)
        t.matmul(ps_dummy[:, 0:320], wsb[:, 0:128], xsb[:, 0:320],
                 start=True, stop=True).then_inc(P.sem("s_mm"), 1)
    P.emit("tensor", t_drain)
    P.bump("s_mm", 1)


def build_mm_res_ln(nk, scale_rhs, has_base, xres_bf16, chunk=256):
    nch = TPC // chunk
    W = chunk
    XDT = BF16 if xres_bf16 else F32
    nc = bass.Bass("TRN2", target_bir_lowering=False)
    xres = nc.dram_tensor("xres", [8, 128, TPC], XDT, kind="ExternalInput")
    rhs_d = nc.dram_tensor("rhs", [nk, 128, TPC], BF16, kind="ExternalInput")
    wT = nc.dram_tensor("wT", [nk, 128, 1024], F32, kind="ExternalInput")
    if has_base:
        base_d = nc.dram_tensor("base", [8, 128, TPC], BF16, kind="ExternalInput")
    if scale_rhs:
        tprev = nc.dram_tensor("tprev", [128, nk], F32, kind="ExternalInput")
    x1_d = nc.dram_tensor("x1", [8, 128, TPC], BF16, kind="ExternalOutput")

    w_sb = nc.alloc_sbuf_tensor("w_sb", [128, nk * 1024], BF16)
    xr = [nc.alloc_sbuf_tensor(f"xr{i}", [128, 8 * W], XDT) for i in range(2)]
    rh = [nc.alloc_sbuf_tensor(f"rh{i}", [128, nk * W], BF16) for i in range(2)]
    bs = [nc.alloc_sbuf_tensor(f"bs{i}", [128, 8 * W], BF16) for i in range(2)] if has_base else None
    gp_t = nc.alloc_sbuf_tensor("gp", [128, nk * W], BF16) if scale_rhs else None
    r_t = nc.alloc_sbuf_tensor("r_t", [128, 8 * W], F32)
    sq_t = nc.alloc_sbuf_tensor("sq", [128, 8 * W], F32)
    row_t = nc.alloc_sbuf_tensor("row_t", [1, 4 * W], F32)
    x1b = [nc.alloc_sbuf_tensor(f"x1b{i}", [128, 8 * W], BF16) for i in range(2)]
    onc_t = nc.alloc_sbuf_tensor("onc_t", [128, 2], F32)
    onr_t = nc.alloc_sbuf_tensor("onr_t", [1, 128], F32)
    eps_t = nc.alloc_sbuf_tensor("eps_t", [1, 2], F32)
    if scale_rhs:
        tp_t = nc.alloc_sbuf_tensor("tp_t", [128, nk], F32)

    P = Prog(nc)
    pool = PsumPool(nc, 4 if W <= 256 else 3, W)
    ps_s12 = nc.alloc_psum_tensor("ps_s12", [1, 2 * W], F32)
    ps_s1 = ps_s12[0:1, 0:W]
    ps_s2 = ps_s12[0:1, W:2 * W]
    ps_bc = nc.alloc_psum_tensor("ps_bc", [128, 2 * W], F32)
    ps_dummy = nc.alloc_psum_tensor("ps_dum", [128, 320], F32)
    for sname in ["s_in", "s_wt", "s_mm", "s_ev", "s_ph", "s_phs", "s_st"]:
        P.sem(sname)

    def g_pre(g):
        g.dma_start(w_sb[:].rearrange("p (k m) -> p k m", k=nk),
                    wT.ap().transpose([1, 0, 2])).then_inc(P.sem("s_wt"), 16)
    P.emit("gpsimd", g_pre)
    P.bump("s_wt", 16)

    wbar_t = nc.alloc_sbuf_tensor("wbar_t", [128, 64], BF16)

    def s_wbar(s, nk=nk):
        s.wait_ge(P.sem("s_wt"), 16)
        ins = None
        for k in range(nk):
            ins = s.activation(wbar_t[:], w_sb[:, k * 1024 + 960:k * 1024 + 1024], AF.Copy)
        ins.then_inc(P.sem("s_phs"), 1)
    P.emit("scalar", s_wbar)
    WBARB = P.bump("s_phs", 1)

    def v_pre(v):
        for _i in range(100):
            v.memset(r_t[:, 0:512], 0.0).then_inc(P.sem("s_spin"), 1)
            v.wait_ge(P.sem("s_spin"), _i + 1)
        v.memset(onc_t[:], 1.0)
        v.memset(eps_t[:], 1e-5)
        ins = v.memset(onr_t[:], 1.0)
        ins.then_inc(P.sem("s_ph"), 1)
    P.emit("vector", v_pre)
    P.bump("s_ph", 1)

    def load(sy, c):
        sl = slice(c * W, (c + 1) * W)
        sy.dma_start(xr[c % 2][:].rearrange("p (k m) -> p k m", k=8),
                     xres.ap()[:, :, sl].transpose([1, 0, 2])).then_inc(P.sem("s_in"), 16)
        sy.dma_start(rh[c % 2][:].rearrange("p (k m) -> p k m", k=nk),
                     rhs_d.ap()[:, :, sl].transpose([1, 0, 2])).then_inc(P.sem("s_in"), 16)
        if has_base:
            sy.dma_start(bs[c % 2][:].rearrange("p (k m) -> p k m", k=8),
                         base_d.ap()[:, :, sl].transpose([1, 0, 2])).then_inc(P.sem("s_in"), 16)
    NLD = 3 if has_base else 2

    if scale_rhs:
        def sy_tp(sy):
            sy.dma_start(tp_t[:], tprev.ap()).then_inc(P.sem("s_in"), 16)
        P.emit("sync", sy_tp)
        P.bump("s_in", 16)
    IN0 = P.val("s_in")

    for c in range(min(2, nch)):
        P.emit("sync", lambda sy, c=c: load(sy, c))
        P.bump("s_in", 16 * NLD)

    for c in range(nch):
        in_need = IN0 + 16 * NLD * (c + 1)
        ph_gp = None
        if scale_rhs:
            def v_gp(v, c=c, in_need=in_need):
                v.wait_ge(P.sem("s_in"), in_need)
                for j in range(nk):
                    ins = v.tensor_scalar_mul(gp_t[:, j * W:(j + 1) * W],
                                              rh[c % 2][:, j * W:(j + 1) * W], tp_t[:, j:j + 1])
                ins.then_inc(P.sem("s_ph"), 1)
            P.emit("vector", v_gp)
            ph_gp = P.bump("s_ph", 1)

        for m in range(8):
            bk, pt, freed = pool.take()

            def t_mm(t, c=c, m=m, pt=pt, freed=freed, ph_gp=ph_gp, in_need=in_need):
                if m == 0:
                    t.wait_ge(P.sem("s_wt"), 16)
                    if c == 0:
                        t.wait_ge(P.sem("s_phs"), WBARB)
                        t.wait_ge(P.sem("s_ph"), 1)
                    if ph_gp is not None:
                        t.wait_ge(P.sem("s_ph"), ph_gp)
                    t.wait_ge(P.sem("s_in"), in_need)
                if freed is not None:
                    t.wait_ge(P.sem(freed[0]), freed[1])
                src = gp_t if scale_rhs else rh[c % 2]
                for k in range(nk):
                    ins = t.matmul(
                        pt[:, :W],
                        w_sb[:, k * 1024 + m * 128:k * 1024 + m * 128 + 128],
                        src[:, k * W:(k + 1) * W],
                        start=(k == 0), stop=(k == nk - 1),
                    )
                ins.then_inc(P.sem("s_mm"), 1)
            P.emit("tensor", t_mm)
            need = P.bump("s_mm", 1)

            def v_ev(v, c=c, m=m, pt=pt, need=need):
                v.wait_ge(P.sem("s_mm"), need + 1)
                sl = slice(m * W, (m + 1) * W)
                if has_base:
                    v.tensor_add(r_t[:, sl], pt[:, :W], bs[c % 2][:, sl])
                    ins = v.tensor_add(r_t[:, sl], r_t[:, sl], xr[c % 2][:, sl])
                else:
                    ins = v.tensor_add(r_t[:, sl], pt[:, :W], xr[c % 2][:, sl])
                ins.then_inc(P.sem("s_ev"), 1)
            P.emit("vector", v_ev)
            pool.release(bk, "s_ev", P.bump("s_ev", 1))
        emit_drain_fn(P, ps_dummy, w_sb, rh[0])

        ev_need = P.val("s_ev")

        def s_sq(s, ev_need=ev_need):
            s.wait_ge(P.sem("s_ev"), ev_need)
            ins = s.activation(sq_t[:], r_t[:], AF.Square)
            ins.then_inc(P.sem("s_phs"), 1)
        P.emit("scalar", s_sq)
        phs_sq = P.bump("s_phs", 1)

        def t_stat(t, phs_sq=phs_sq):
            t.wait_ge(P.sem("s_phs"), phs_sq)
            for k in range(8):
                t.matmul(ps_s1, onc_t[:, 0:1], r_t[:, k * W:(k + 1) * W],
                         start=(k == 0), stop=(k == 7))
            for k in range(8):
                ins = t.matmul(ps_s2, onc_t[:, 1:2], sq_t[:, k * W:(k + 1) * W],
                               start=(k == 0), stop=(k == 7))
            ins.then_inc(P.sem("s_mm"), 1)
        P.emit("tensor", t_stat)
        stat_need = P.bump("s_mm", 1)
        emit_drain_fn(P, ps_dummy, w_sb, rh[0])

        def v_stat(v, stat_need=stat_need):
            v.wait_ge(P.sem("s_mm"), stat_need + 1)
            v.tensor_scalar_mul(row_t[0:1, 2 * W:3 * W], ps_s1, 1.0 / 1024.0)
            ins = v.tensor_scalar_mul(row_t[0:1, 3 * W:4 * W], ps_s2, 1.0 / 1024.0)
            ins.then_inc(P.sem("s_ph"), 1)
        P.emit("vector", v_stat)
        ph_cp = P.bump("s_ph", 1)

        def s_stat(s, ph_cp=ph_cp):
            s.wait_ge(P.sem("s_ph"), ph_cp)
            ins = s.activation(row_t[0:1, 0:W], row_t[0:1, 2 * W:3 * W], AF.Square)
            ins.then_inc(P.sem("s_phs"), 1)
        P.emit("scalar", s_stat)
        phs_mu2 = P.bump("s_phs", 1)

        def v_var(v, phs_mu2=phs_mu2):
            v.wait_ge(P.sem("s_phs"), phs_mu2)
            ins = v.tensor_sub(row_t[0:1, W:2 * W], row_t[0:1, 3 * W:4 * W], row_t[0:1, 0:W])
            ins.then_inc(P.sem("s_ph"), 1)
        P.emit("vector", v_var)
        ph_var = P.bump("s_ph", 1)

        def s_rstd(s, ph_var=ph_var):
            s.wait_ge(P.sem("s_ph"), ph_var)
            s.activation(row_t[0:1, W:2 * W], row_t[0:1, W:2 * W], AF.Ln, bias=eps_t[0:1, 0:1])
            ins = s.activation(row_t[0:1, W:2 * W], row_t[0:1, W:2 * W], AF.Exp, scale=-0.5)
            ins.then_inc(P.sem("s_phs"), 1)
        P.emit("scalar", s_rstd)
        phs_rs = P.bump("s_phs", 1)

        def v_mmr(v, phs_rs=phs_rs):
            v.wait_ge(P.sem("s_phs"), phs_rs)
            v.tensor_mul(row_t[0:1, 0:W], row_t[0:1, 2 * W:3 * W], row_t[0:1, W:2 * W])
            ins = v.tensor_scalar_mul(row_t[0:1, 0:W], row_t[0:1, 0:W], -1.0)
            ins.then_inc(P.sem("s_ph"), 1)
        P.emit("vector", v_mmr)
        ph_mm = P.bump("s_ph", 1)

        def t_bc(t, ph_mm=ph_mm):
            t.wait_ge(P.sem("s_ph"), ph_mm)
            if W > 256:
                t.matmul(ps_bc[:, 0:W], onr_t[0:1, 0:128],
                         row_t[0:1, 0:W], start=True, stop=True)
                ins = t.matmul(ps_bc[:, W:2 * W], onr_t[0:1, 0:128],
                               row_t[0:1, W:2 * W], start=True, stop=True)
            else:
                ins = t.matmul(ps_bc[:, 0:2 * W], onr_t[0:1, 0:128],
                               row_t[0:1, 0:2 * W], start=True, stop=True)
            ins.then_inc(P.sem("s_mm"), 1)
        P.emit("tensor", t_bc)
        bc_need = P.bump("s_mm", 1)
        emit_drain_fn(P, ps_dummy, w_sb, rh[0])

        st_guard = 16 * max(0, c - 1)

        def v_app(v, c=c, bc_need=bc_need, st_guard=st_guard):
            v.wait_ge(P.sem("s_mm"), bc_need + 1)
            v.wait_ge(P.sem("s_st"), st_guard)
            for k in range(8):
                sl = slice(k * W, (k + 1) * W)
                v.tensor_mul(r_t[:, sl], r_t[:, sl], ps_bc[:, W:2 * W])
                ins = v.tensor_add(x1b[c % 2][:, sl], r_t[:, sl], ps_bc[:, 0:W])
            ins.then_inc(P.sem("s_ph"), 1)
        P.emit("vector", v_app)
        ph_x1 = P.bump("s_ph", 1)

        def sy_st(sy, c=c, ph_x1=ph_x1):
            sy.wait_ge(P.sem("s_ph"), ph_x1)
            sy.dma_start(x1_d.ap()[:, :, c * W:(c + 1) * W].transpose([1, 0, 2]),
                         x1b[c % 2][:].rearrange("p (k m) -> p k m", k=8)).then_inc(P.sem("s_st"), 16)
        P.emit("sync", sy_st)
        P.bump("s_st", 16)

        if c + 2 < nch:
            ev_done = P.val("s_ev")

            def sy_ld(sy, c2=c + 2, ev_done=ev_done):
                sy.wait_ge(P.sem("s_ev"), ev_done)
                load(sy, c2)
            P.emit("sync", sy_ld)
            P.bump("s_in", 16 * NLD)

    st_final = P.val("s_st")

    def sy_fin(sy, st_final=st_final):
        sy.wait_ge(P.sem("s_st"), st_final)
    P.emit("sync", sy_fin)
    return P.build()


def build_ffn(chunk=128, tokens=TPC, cut=None, dq_g=8, dq_g2=32, out_f32=False):
    nch = tokens // chunk
    W = chunk
    nc = bass.Bass("TRN2", target_bir_lowering=False)
    xin = nc.dram_tensor("xin", [8, 128, TPC], BF16, kind="ExternalInput")
    q1 = nc.dram_tensor("q1", [8, 128, 4096], I32, kind="ExternalInput")
    q2 = nc.dram_tensor("q2", [32, 128, 1024], I32, kind="ExternalInput")
    sc1 = nc.dram_tensor("sc1", [8, 128, 4096], F32, kind="ExternalInput")
    sc2 = nc.dram_tensor("sc2", [32, 128, 1024], F32, kind="ExternalInput")
    b1 = nc.dram_tensor("b1", [128, 32], F32, kind="ExternalInput")
    b2 = nc.dram_tensor("b2", [128, 8], F32, kind="ExternalInput")
    gam = nc.dram_tensor("gam", [128, 8], F32, kind="ExternalInput")
    bet = nc.dram_tensor("bet", [128, 8], F32, kind="ExternalInput")
    x2_d = nc.dram_tensor("x2", [8, 128, TPC], F32 if out_f32 else BF16, kind="ExternalOutput")

    w1 = nc.alloc_sbuf_tensor("w1", [128, 8 * 4096], BF16)
    w2 = nc.alloc_sbuf_tensor("w2", [128, 32 * 1024], BF16)
    qt = [nc.alloc_sbuf_tensor(f"qt{i}", [128, 1024], I32) for i in range(3)]
    scb = [nc.alloc_sbuf_tensor(f"scb{i}", [128, 512], F32) for i in range(4)]
    xb = [nc.alloc_sbuf_tensor(f"xb{i}", [128, 8 * W], BF16) for i in range(2)]
    hb = nc.alloc_sbuf_tensor("hb", [128, 8 * W], BF16)
    yb = nc.alloc_sbuf_tensor("yb", [128, 32 * W], BF16)
    r_t = nc.alloc_sbuf_tensor("r_t", [128, 8 * W], F32)
    sq_t = nc.alloc_sbuf_tensor("sq", [128, 8 * W], F32)
    dq_t = nc.alloc_sbuf_tensor("dq", [128, 512], F32)
    row_t = nc.alloc_sbuf_tensor("row_t", [1, 4 * W], F32)
    if out_f32 and W >= 256:
        _x2b1 = nc.alloc_sbuf_tensor("x2b0", [128, 8 * W], F32)
        x2b = [_x2b1, _x2b1]
        x2b_single = True
    else:
        x2b = [nc.alloc_sbuf_tensor(f"x2b{i}", [128, 8 * W], F32 if out_f32 else BF16)
               for i in range(2)]
        x2b_single = False
    onc_t = nc.alloc_sbuf_tensor("onc_t", [128, 2], F32)
    oncb_t = nc.alloc_sbuf_tensor("oncb_t", [128, 2], BF16)
    onr_t = nc.alloc_sbuf_tensor("onr_t", [1, 128], F32)
    eps_t = nc.alloc_sbuf_tensor("eps_t", [1, 2], F32)
    dr_w = nc.alloc_sbuf_tensor("dr_w", [128, 128], BF16)
    dr_x = nc.alloc_sbuf_tensor("dr_x", [128, 320], BF16)
    cvec = nc.alloc_sbuf_tensor("cvec", [128, 64], F32)

    P = Prog(nc)
    pool = PsumPool(nc, 4, max(W, 256))
    ps_s1 = nc.alloc_psum_tensor("ps_s1", [1, W], F32)
    ps_s2 = nc.alloc_psum_tensor("ps_s2", [1, W], F32)
    ps_bc = nc.alloc_psum_tensor("ps_bc", [128, 512], F32)
    ps_dummy = nc.alloc_psum_tensor("ps_dum", [128, 320], F32)
    for sname in ["s_in", "s_q", "s_wt", "s_mm", "s_ev", "s_evs", "s_ph", "s_phs", "s_st", "s_dq"]:
        P.sem(sname)

    def sy_pre(sy):
        sy.dma_start(cvec[:, 0:32], b1.ap()).then_inc(P.sem("s_wt"), 16)
        sy.dma_start(cvec[:, 32:40], b2.ap()).then_inc(P.sem("s_wt"), 16)
        sy.dma_start(cvec[:, 40:48], gam.ap()).then_inc(P.sem("s_wt"), 16)
        sy.dma_start(cvec[:, 48:56], bet.ap()).then_inc(P.sem("s_wt"), 16)
    P.emit("sync", sy_pre)
    WT0 = P.bump("s_wt", 64)

    def v_pre(v):
        for _i in range(100):
            v.memset(r_t[:, 0:512], 0.0).then_inc(P.sem("s_spin"), 1)
            v.wait_ge(P.sem("s_spin"), _i + 1)
        v.memset(onc_t[:], 1.0)
        v.memset(oncb_t[:], 1.0)
        v.memset(eps_t[:], 1e-5)
        v.memset(onr_t[:], 1.0).then_inc(P.sem("s_ph"), 1)
        v.memset(dr_w[:], 1.0).then_inc(P.sem("s_ph"), 1)
        v.memset(dr_x[:], 1.0).then_inc(P.sem("s_ph"), 1)
    P.emit("vector", v_pre)
    P.bump("s_ph", 3)

    def g_nop(g):
        g.engine_nop()
    P.emit("gpsimd", g_nop)

    def s_half2(s):
        s.wait_ge(P.sem("s_wt"), WT0)
        ins = s.mul(cvec[:, 56:64], cvec[:, 32:40], 0.5)
        ins.then_inc(P.sem("s_phs"), 1)
    P.emit("scalar", s_half2)
    P.bump("s_phs", 1)

    if cut == 'consts':
        cdbg = nc.dram_tensor("cdbg", [128, 64], F32, kind="ExternalOutput")

        def sy_cutc(sy):
            sy.wait_ge(P.sem("s_phs"), 1)
            sy.dma_start(cdbg.ap(), cvec[:]).then_inc(P.sem("s_st"), 16)
            sy.wait_ge(P.sem("s_st"), 16)
        P.emit("sync", sy_cutc)
        P.bump("s_st", 16)
        return P.build()

    slabs = []
    for qsrc, scsrc, wdst, ntile, ncol in ((q1, sc1, w1, dq_g, 4096), (q2, sc2, w2, dq_g2, 1024)):
        for g in range(ntile):
            for slab in range(ncol // 1024):
                slabs.append((qsrc, scsrc, wdst, g, slab, ncol))
    NSLAB = len(slabs)
    SQ_TOTAL = NSLAB * 48 + 32
    sidx = 0
    for j, (qsrc, scsrc, wdst, g, slab, ncol) in enumerate(slabs):
        qpar = j % 3
        dqq = max(0, 2 * j - 4)

        def sy_q(sy, qsrc=qsrc, g=g, slab=slab, dqq=dqq, qpar=qpar):
            sy.wait_ge(P.sem("s_dq"), dqq)
            sy.dma_start(qt[qpar][:], qsrc.ap()[g, :, slab * 1024:(slab + 1) * 1024]
                         ).then_inc(P.sem("s_q"), 16)
        P.emit("sync", sy_q)
        P.bump("s_q", 16)

        for ss in range(2):
            col0 = slab * 1024 + ss * 512
            spar = (2 * j + ss) % 4
            dqs = max(0, 2 * j + ss - 3)

            def sy_sc(sy, scsrc=scsrc, g=g, col0=col0, dqs=dqs, spar=spar):
                sy.wait_ge(P.sem("s_dq"), dqs)
                sy.dma_start(scb[spar][:], scsrc.ap()[g, :, col0:col0 + 512]
                             ).then_inc(P.sem("s_q"), 16)
            P.emit("sync", sy_sc)
            P.bump("s_q", 16)

            need = 48 * j + 32 + 16 * ss
            need_ahead = min(SQ_TOTAL, need + 32)

            def v_dq(v, g=g, ss=ss, col0=col0, need_ahead=need_ahead,
                     wdst=wdst, ncol=ncol, qpar=qpar, spar=spar):
                v.wait_ge(P.sem("s_q"), need_ahead)
                v.tensor_scalar_add(dq_t[:], qt[qpar][:, (col0 % 1024):(col0 % 1024) + 512], -128.0)
                ins = v.tensor_mul(wdst[:, g * ncol + col0:g * ncol + col0 + 512],
                                   dq_t[:], scb[spar][:])
                ins.then_inc(P.sem("s_dq"), 1)
            P.emit("vector", v_dq)
            P.bump("s_dq", 1)

    for d in range(2):
        qpar = (NSLAB + d) % 3
        dqq = 2 * (NSLAB + d) - 4

        def sy_dummy(sy, qpar=qpar, dqq=dqq):
            sy.wait_ge(P.sem("s_dq"), min(dqq, 2 * NSLAB))
            sy.dma_start(qt[qpar][:], q1.ap()[0, :, 0:1024]).then_inc(P.sem("s_q"), 16)
        P.emit("sync", sy_dummy)
        P.bump("s_q", 16)

    DQ_ALL = P.val("s_dq")

    def s_wbar(s, DQ_ALL=DQ_ALL):
        s.wait_ge(P.sem("s_dq"), DQ_ALL)
        for g in range(dq_g):
            s.activation(hb[:, 0:64], w1[:, g * 4096 + 4032:g * 4096 + 4096], AF.Copy)
        for g in range(dq_g2):
            s.activation(hb[:, 0:64], w2[:, g * 1024 + 960:g * 1024 + 1024], AF.Copy)
        ins = s.activation(hb[:, 0:64], hb[:, 0:64], AF.Copy)
        ins.then_inc(P.sem("s_phs"), 1)
    P.emit("scalar", s_wbar)
    WBAR = P.bump("s_phs", 1)

    if cut == 'dq':
        w1dbg = nc.dram_tensor("w1dbg", [128, 4096], BF16, kind="ExternalOutput")

        def sy_cut(sy):
            sy.wait_ge(P.sem("s_dq"), DQ_ALL)
            sy.dma_start(w1dbg.ap(), w1[:, 0:4096]).then_inc(P.sem("s_st"), 16)
            sy.wait_ge(P.sem("s_st"), 16)
        P.emit("sync", sy_cut)
        P.bump("s_st", 16)
        return P.build()

    def load(sy, c):
        sy.dma_start(xb[c % 2][:].rearrange("p (k m) -> p k m", k=8),
                     xin.ap()[:, :, c * W:(c + 1) * W].transpose([1, 0, 2])).then_inc(P.sem("s_in"), 16)
    for c in range(min(2, nch)):
        P.emit("sync", lambda sy, c=c: load(sy, c))
        P.bump("s_in", 16)

    for c in range(nch):
        in_need = 16 * (c + 1)
        xc = xb[c % 2]
        def s_sq(s, c=c, in_need=in_need, xc=xc):
            s.wait_ge(P.sem("s_in"), in_need)
            if c == 0:
                s.wait_ge(P.sem("s_ph"), 3)
            ins = s.activation(sq_t[:], xc[:], AF.Square)
            ins.then_inc(P.sem("s_phs"), 1)
        P.emit("scalar", s_sq)
        phs_sq = P.bump("s_phs", 1)

        def t_stat(t, c=c, phs_sq=phs_sq, xc=xc):
            t.wait_ge(P.sem("s_phs"), phs_sq)
            for k in range(8):
                t.matmul(ps_s1[0:1, :W], oncb_t[:, 0:1], xc[:, k * W:(k + 1) * W],
                         start=(k == 0), stop=(k == 7))
            for k in range(8):
                ins = t.matmul(ps_s2[0:1, :W], onc_t[:, 1:2], sq_t[:, k * W:(k + 1) * W],
                               start=(k == 0), stop=(k == 7))
            ins.then_inc(P.sem("s_mm"), 1)
        P.emit("tensor", t_stat)
        stat_need = P.bump("s_mm", 1)
        emit_drain_fn(P, ps_dummy, dr_w, dr_x, wait_ph=3)

        def v_stat(v, stat_need=stat_need):
            v.wait_ge(P.sem("s_mm"), stat_need + 1)
            v.tensor_scalar_mul(row_t[0:1, 2 * W:3 * W], ps_s1[0:1, :W], 1.0 / 1024.0)
            ins = v.tensor_scalar_mul(row_t[0:1, 3 * W:4 * W], ps_s2[0:1, :W], 1.0 / 1024.0)
            ins.then_inc(P.sem("s_ph"), 1)
        P.emit("vector", v_stat)
        ph_cp = P.bump("s_ph", 1)

        def s_stat(s, ph_cp=ph_cp):
            s.wait_ge(P.sem("s_ph"), ph_cp)
            ins = s.activation(row_t[0:1, 0:W], row_t[0:1, 2 * W:3 * W], AF.Square)
            ins.then_inc(P.sem("s_phs"), 1)
        P.emit("scalar", s_stat)
        phs_mu2 = P.bump("s_phs", 1)

        def v_var(v, phs_mu2=phs_mu2):
            v.wait_ge(P.sem("s_phs"), phs_mu2)
            ins = v.tensor_sub(row_t[0:1, W:2 * W], row_t[0:1, 3 * W:4 * W], row_t[0:1, 0:W])
            ins.then_inc(P.sem("s_ph"), 1)
        P.emit("vector", v_var)
        ph_var = P.bump("s_ph", 1)

        def s_rstd(s, ph_var=ph_var):
            s.wait_ge(P.sem("s_ph"), ph_var)
            s.activation(row_t[0:1, W:2 * W], row_t[0:1, W:2 * W], AF.Ln, bias=eps_t[0:1, 0:1])
            ins = s.activation(row_t[0:1, W:2 * W], row_t[0:1, W:2 * W], AF.Exp, scale=-0.5)
            ins.then_inc(P.sem("s_phs"), 1)
        P.emit("scalar", s_rstd)
        phs_rs = P.bump("s_phs", 1)

        def v_mmr(v, phs_rs=phs_rs):
            v.wait_ge(P.sem("s_phs"), phs_rs)
            v.tensor_mul(row_t[0:1, 0:W], row_t[0:1, 2 * W:3 * W], row_t[0:1, W:2 * W])
            ins = v.tensor_scalar_mul(row_t[0:1, 0:W], row_t[0:1, 0:W], -1.0)
            ins.then_inc(P.sem("s_ph"), 1)
        P.emit("vector", v_mmr)
        ph_mm = P.bump("s_ph", 1)

        def t_bc(t, ph_mm=ph_mm):
            t.wait_ge(P.sem("s_ph"), ph_mm)
            ins = t.matmul(ps_bc[:, 0:2 * W], onr_t[0:1, 0:128],
                           row_t[0:1, 0:2 * W], start=True, stop=True)
            ins.then_inc(P.sem("s_mm"), 1)
        P.emit("tensor", t_bc)
        bc_need = P.bump("s_mm", 1)
        emit_drain_fn(P, ps_dummy, dr_w, dr_x, wait_ph=3)

        def v_h(v, c=c, bc_need=bc_need, xc=xc):
            v.wait_ge(P.sem("s_mm"), bc_need + 1)
            v.wait_ge(P.sem("s_phs"), 1)
            for k in range(8):
                sl = slice(k * W, (k + 1) * W)
                v.tensor_mul(r_t[:, sl], xc[:, sl], ps_bc[:, W:2 * W])
                v.tensor_add(r_t[:, sl], r_t[:, sl], ps_bc[:, 0:W])
                ins = v.tensor_scalar(hb[:, sl], r_t[:, sl], cvec[:, 40 + k:41 + k],
                                      cvec[:, 48 + k:49 + k], ALU.mult, ALU.add)
            ins.then_inc(P.sem("s_ph"), 1)
        P.emit("vector", v_h)
        ph_h = P.bump("s_ph", 1)

        for m in range(32):
            bk, pt, freed = pool.take()

            def t_f1(t, m=m, pt=pt, freed=freed, ph_h=ph_h, c=c):
                if m == 0:
                    t.wait_ge(P.sem("s_ph"), ph_h)
                    if c == 0:
                        t.wait_ge(P.sem("s_dq"), DQ_ALL)
                        t.wait_ge(P.sem("s_phs"), WBAR)
                if freed is not None:
                    t.wait_ge(P.sem(freed[0]), freed[1])
                for k in range(8):
                    ins = t.matmul(pt[:, :W],
                                   w1[:, k * 4096 + m * 128:k * 4096 + m * 128 + 128],
                                   hb[:, k * W:(k + 1) * W],
                                   start=(k == 0), stop=(k == 7))
                ins.then_inc(P.sem("s_mm"), 1)
            P.emit("tensor", t_f1)
            need = P.bump("s_mm", 1)

            def s_y(s, m=m, pt=pt, need=need):
                s.wait_ge(P.sem("s_mm"), need + 1)
                ins = s.activation(yb[:, m * W:(m + 1) * W], pt[:, :W], AF.Silu,
                                   bias=cvec[:, m % 32:m % 32 + 1] if False else cvec[:, m:m + 1])
                ins.then_inc(P.sem("s_evs"), 1)
            P.emit("scalar", s_y)
            pool.release(bk, "s_evs", P.bump("s_evs", 1))
        emit_drain_fn(P, ps_dummy, dr_w, dr_x, wait_ph=3)

        for m in range(8):
            bk, pt, freed = pool.take()
            evs_need = P.val("s_evs")

            def t_f2(t, m=m, pt=pt, freed=freed, evs_need=evs_need):
                if m == 0:
                    t.wait_ge(P.sem("s_evs"), evs_need)
                if freed is not None:
                    t.wait_ge(P.sem(freed[0]), freed[1])
                for k in range(32):
                    ins = t.matmul(pt[:, :W],
                                   w2[:, k * 1024 + m * 128:k * 1024 + m * 128 + 128],
                                   yb[:, k * W:(k + 1) * W],
                                   start=(k == 0), stop=(k == 31))
                ins.then_inc(P.sem("s_mm"), 1)
            P.emit("tensor", t_f2)
            need = P.bump("s_mm", 1)

            def v_z(v, m=m, pt=pt, need=need, c=c, xc=xc):
                v.wait_ge(P.sem("s_mm"), need + 1)
                sl = slice(m * W, (m + 1) * W)
                v.scalar_tensor_tensor(r_t[:, sl], pt[:, :W], 0.5, xc[:, sl],
                                       ALU.mult, ALU.add)
                ins = v.tensor_scalar_add(r_t[:, sl], r_t[:, sl], cvec[:, 56 + m:57 + m])
                ins.then_inc(P.sem("s_ev"), 1)
            P.emit("vector", v_z)
            pool.release(bk, "s_ev", P.bump("s_ev", 1))
        emit_drain_fn(P, ps_dummy, dr_w, dr_x, wait_ph=3)

        ev_need = P.val("s_ev")

        def s_sq2(s, ev_need=ev_need):
            s.wait_ge(P.sem("s_ev"), ev_need)
            ins = s.activation(sq_t[:], r_t[:], AF.Square)
            ins.then_inc(P.sem("s_phs"), 1)
        P.emit("scalar", s_sq2)
        phs_sq2 = P.bump("s_phs", 1)

        def t_stat2(t, phs_sq2=phs_sq2):
            t.wait_ge(P.sem("s_phs"), phs_sq2)
            for k in range(8):
                t.matmul(ps_s1[0:1, :W], onc_t[:, 0:1], r_t[:, k * W:(k + 1) * W],
                         start=(k == 0), stop=(k == 7))
            for k in range(8):
                ins = t.matmul(ps_s2[0:1, :W], onc_t[:, 1:2], sq_t[:, k * W:(k + 1) * W],
                               start=(k == 0), stop=(k == 7))
            ins.then_inc(P.sem("s_mm"), 1)
        P.emit("tensor", t_stat2)
        stat2 = P.bump("s_mm", 1)
        emit_drain_fn(P, ps_dummy, dr_w, dr_x, wait_ph=3)

        def v_stat2(v, stat2=stat2):
            v.wait_ge(P.sem("s_mm"), stat2 + 1)
            v.tensor_scalar_mul(row_t[0:1, 2 * W:3 * W], ps_s1[0:1, :W], 1.0 / 1024.0)
            ins = v.tensor_scalar_mul(row_t[0:1, 3 * W:4 * W], ps_s2[0:1, :W], 1.0 / 1024.0)
            ins.then_inc(P.sem("s_ph"), 1)
        P.emit("vector", v_stat2)
        ph_cp2 = P.bump("s_ph", 1)

        def s_stat2(s, ph_cp2=ph_cp2):
            s.wait_ge(P.sem("s_ph"), ph_cp2)
            ins = s.activation(row_t[0:1, 0:W], row_t[0:1, 2 * W:3 * W], AF.Square)
            ins.then_inc(P.sem("s_phs"), 1)
        P.emit("scalar", s_stat2)
        phs_mu22 = P.bump("s_phs", 1)

        def v_var2(v, phs_mu22=phs_mu22):
            v.wait_ge(P.sem("s_phs"), phs_mu22)
            ins = v.tensor_sub(row_t[0:1, W:2 * W], row_t[0:1, 3 * W:4 * W], row_t[0:1, 0:W])
            ins.then_inc(P.sem("s_ph"), 1)
        P.emit("vector", v_var2)
        ph_var2 = P.bump("s_ph", 1)

        def s_rstd2(s, ph_var2=ph_var2):
            s.wait_ge(P.sem("s_ph"), ph_var2)
            s.activation(row_t[0:1, W:2 * W], row_t[0:1, W:2 * W], AF.Ln, bias=eps_t[0:1, 0:1])
            ins = s.activation(row_t[0:1, W:2 * W], row_t[0:1, W:2 * W], AF.Exp, scale=-0.5)
            ins.then_inc(P.sem("s_phs"), 1)
        P.emit("scalar", s_rstd2)
        phs_rs2 = P.bump("s_phs", 1)

        def v_mmr2(v, phs_rs2=phs_rs2):
            v.wait_ge(P.sem("s_phs"), phs_rs2)
            v.tensor_mul(row_t[0:1, 0:W], row_t[0:1, 2 * W:3 * W], row_t[0:1, W:2 * W])
            ins = v.tensor_scalar_mul(row_t[0:1, 0:W], row_t[0:1, 0:W], -1.0)
            ins.then_inc(P.sem("s_ph"), 1)
        P.emit("vector", v_mmr2)
        ph_mm2 = P.bump("s_ph", 1)

        def t_bc2(t, ph_mm2=ph_mm2):
            t.wait_ge(P.sem("s_ph"), ph_mm2)
            ins = t.matmul(ps_bc[:, 0:2 * W], onr_t[0:1, 0:128],
                           row_t[0:1, 0:2 * W], start=True, stop=True)
            ins.then_inc(P.sem("s_mm"), 1)
        P.emit("tensor", t_bc2)
        bc2 = P.bump("s_mm", 1)
        emit_drain_fn(P, ps_dummy, dr_w, dr_x, wait_ph=3)

        st_guard = 16 * c if x2b_single else 16 * max(0, c - 1)

        def v_app2(v, c=c, bc2=bc2, st_guard=st_guard):
            v.wait_ge(P.sem("s_mm"), bc2 + 1)
            v.wait_ge(P.sem("s_st"), st_guard)
            for k in range(8):
                sl = slice(k * W, (k + 1) * W)
                v.tensor_mul(r_t[:, sl], r_t[:, sl], ps_bc[:, W:2 * W])
                ins = v.tensor_add(x2b[c % 2][:, sl], r_t[:, sl], ps_bc[:, 0:W])
            ins.then_inc(P.sem("s_ph"), 1)
        P.emit("vector", v_app2)
        ph_x2 = P.bump("s_ph", 1)

        def sy_st(sy, c=c, ph_x2=ph_x2):
            sy.wait_ge(P.sem("s_ph"), ph_x2)
            sy.dma_start(x2_d.ap()[:, :, c * W:(c + 1) * W].transpose([1, 0, 2]),
                         x2b[c % 2][:].rearrange("p (k m) -> p k m", k=8)).then_inc(P.sem("s_st"), 16)
        P.emit("sync", sy_st)
        P.bump("s_st", 16)

        if c + 2 < nch:
            ph_used = ph_h
            def sy_ld(sy, c2=c + 2, ph_x2=ph_x2):
                sy.wait_ge(P.sem("s_ph"), ph_x2)
                load(sy, c2)
            P.emit("sync", sy_ld)
            P.bump("s_in", 16)

    if cut == 'dbgw':
        w1dbg = nc.dram_tensor("w1dbg", [128, 32768], BF16, kind="ExternalOutput")
        w2dbg = nc.dram_tensor("w2dbg", [128, 32768], BF16, kind="ExternalOutput")
        stw = P.val("s_st")

        def sy_wdbg(sy, stw=stw):
            sy.wait_ge(P.sem("s_st"), stw)
            sy.dma_start(w1dbg.ap(), w1[:]).then_inc(P.sem("s_st"), 16)
            sy.dma_start(w2dbg.ap(), w2[:]).then_inc(P.sem("s_st"), 16)
        P.emit("sync", sy_wdbg)
        P.bump("s_st", 32)

    st_final = P.val("s_st")

    def sy_fin(sy, st_final=st_final):
        sy.wait_ge(P.sem("s_st"), st_final)
    P.emit("sync", sy_fin)
    return P.build()


F32R = mybir.dt.float32r


def build_conv(nbatch=4, cut=None):
    dump = None
    if cut and ':' in cut:
        cut, dump = cut.split(':')
    NP = 64
    R = 128
    nc = bass.Bass("TRN2", target_bir_lowering=False)
    x2f = nc.dram_tensor("x2f", [8, 128, L], BF16, kind="ExternalInput")
    winT1 = nc.dram_tensor("winT1", [8, 128, 1536], F32, kind="ExternalInput")
    t_rep = nc.dram_tensor("t_rep", [65, L], F32, kind="ExternalInput")
    fw1 = nc.dram_tensor("fw1", [64, 2], F32, kind="ExternalInput")
    fb1 = nc.dram_tensor("fb1", [64, 2], F32, kind="ExternalInput")
    w2aug = nc.dram_tensor("w2aug", [2, 65, 512], F32, kind="ExternalInput")
    f1s = nc.dram_tensor("f1s", [2, 128, 128], F32, kind="ExternalInput")
    f1r = nc.dram_tensor("f1r", [2, 64, 128], F32, kind="ExternalInput")
    wsh = nc.dram_tensor("wsh", [3, 128, 192], F32, kind="ExternalInput")
    twc = nc.dram_tensor("twc", [128, 256], F32, kind="ExternalInput")
    tw2c = nc.dram_tensor("tw2c", [128, 256], F32, kind="ExternalInput")
    wi1c = nc.dram_tensor("wi1c", [9, 128, 128], F32, kind="ExternalInput")
    prevc = nc.dram_tensor("prevc", [2, 128, 128], F32, kind="ExternalInput")
    idn = nc.dram_tensor("idn", [128, 128], F32, kind="ExternalInput")
    y_out = nc.dram_tensor("y_out", [4, 128, L], BF16, kind="ExternalOutput")

    kindi = "ExternalOutput" if cut else "Internal"
    stg = nc.dram_tensor("stg", [64, 1536, 128], F32, kind=kindi)
    stgh = nc.dram_tensor("stgh", [64, 1024, 128], F32, kind=kindi)
    stgu = nc.dram_tensor("stgu", [64, 512, 128], F32, kind=kindi)
    hspec = nc.dram_tensor("hspec", [2, 8, 2, 128, 8192], F32, kind=kindi)
    dbg = nc.dram_tensor("dbg", [2, 128, 8192], F32, kind=kindi)

    arena = nc.alloc_sbuf_tensor("arena", [128, 24576], F32)
    gt_t = nc.alloc_sbuf_tensor("gt_t", [128, 16384], F32)
    wsh_t = nc.alloc_sbuf_tensor("wsh_t", [128, 3 * 192], F32)
    twc_t = nc.alloc_sbuf_tensor("twc_t", [128, 256], F32)
    tw2c_t = nc.alloc_sbuf_tensor("tw2c_t", [128, 256], F32)
    dscr = nc.alloc_sbuf_tensor("dscr", [1, 16], F32)
    wi1_t = nc.alloc_sbuf_tensor("wi1_t", [128, 9 * 128], F32)
    f1s_t = nc.alloc_sbuf_tensor("f1s_t", [128, 2 * 128], F32)
    f1r_t = nc.alloc_sbuf_tensor("f1r_t", [64, 2 * 128], F32)
    prev_t = nc.alloc_sbuf_tensor("prev_t", [128, 2 * 128], F32)
    idn_t = nc.alloc_sbuf_tensor("idn_t", [128, 128], F32)
    zch = nc.alloc_sbuf_tensor("zch", [128, 2 * 512], F32)
    uvch = nc.alloc_sbuf_tensor("uvch", [128, 4 * 512], F32)
    sdch = nc.alloc_sbuf_tensor("sdch", [128, 4 * 512], F32)
    ybf = None
    fcv = nc.alloc_sbuf_tensor("fcv", [64, 4], F32)
    eps2 = nc.alloc_sbuf_tensor("eps2", [1, 2], F32)
    a_off = nc.lookup_mls("arena").memory_location.addr
    g_off = nc.lookup_mls("gt_t").memory_location.addr
    wp_t = nc.alloc_sbuf_tensor_at("wp_t", [128, 8 * 1536], BF16, offset=g_off)
    ybf = nc.alloc_sbuf_tensor_at("ybf", [64, 2048], BF16, offset=g_off + 49152)
    xpb = [nc.alloc_sbuf_tensor_at(f"xpb{i}", [128, 8 * 256], BF16, offset=g_off + 24576 + i * 4096)
           for i in range(2)]
    sfe_t = nc.alloc_sbuf_tensor_at("sfe_t", [65, L], F32, offset=a_off)
    w2a_t = nc.alloc_sbuf_tensor_at("w2a_t", [65, 2 * 512], F32, offset=a_off + 32768)
    pev = nc.alloc_sbuf_tensor_at("pev", [128, 512], F32, offset=a_off + 32768 + 4096)
    ytm = nc.alloc_sbuf_tensor_at("ytm", [64, 8192], F32, offset=a_off)
    gtm = nc.alloc_sbuf_tensor_at("gtm", [64, 8192], F32, offset=a_off + 32768)

    P = Prog(nc)
    NPS = 7
    pool = PsumPool(nc, NPS, 512)
    ps_dummy = nc.alloc_psum_tensor("ps_dumc", [128, 320], F32)
    for sname in ["s_in", "s_wt", "s_mm", "s_ev", "s_evs", "s_ph", "s_phs", "s_st",
                  "s_stage", "s_m2", "s_sd", "s_tr", "s_wtg"]:
        P.sem(sname)

    def drain(wait_ph=None):
        def t_drain(t, wait_ph=wait_ph):
            if wait_ph is not None:
                t.wait_ge(P.sem("s_ph"), wait_ph)
            t.matmul(ps_dummy[:, 0:320], wi1_t[:, 0:128], wi1_t[:, 0:320],
                     start=True, stop=True).then_inc(P.sem("s_mm"), 1)
        P.emit("tensor", t_drain)
        return P.bump("s_mm", 1)

    def g_pre(g):
        g.dma_start(wp_t[:].rearrange("p (k m) -> p k m", k=8),
                    winT1.ap().transpose([1, 0, 2])).then_inc(P.sem("s_wtg"), 16)
    P.emit("gpsimd", g_pre)
    P.bump("s_wtg", 16)

    wbar_t = nc.alloc_sbuf_tensor("wbar_ct", [128, 64], BF16)

    def s_wbarc(s):
        s.wait_ge(P.sem("s_wtg"), 16)
        s.wait_ge(P.sem("s_ph"), 1)
        ins = None
        for k in range(8):
            ins = s.activation(wbar_t[:], wp_t[:, k * 1536 + 1472:k * 1536 + 1536], AF.Copy)
        ins.then_inc(P.sem("s_phs"), 1)
    P.emit("scalar", s_wbarc)
    WBARC = P.bump("s_phs", 1)

    def sy_pre(sy):
        sy.dma_start(f1s_t[:].rearrange("p (a m) -> p a m", a=2),
                     f1s.ap().transpose([1, 0, 2])).then_inc(P.sem("s_wt"), 16)
        sy.dma_start(f1r_t[:].rearrange("p (a m) -> p a m", a=2),
                     f1r.ap().transpose([1, 0, 2])).then_inc(P.sem("s_wt"), 16)
        sy.dma_start(wi1_t[:].rearrange("p (a m) -> p a m", a=9),
                     wi1c.ap().transpose([1, 0, 2])).then_inc(P.sem("s_wt"), 16)
        sy.dma_start(prev_t[:].rearrange("p (a m) -> p a m", a=2),
                     prevc.ap().transpose([1, 0, 2])).then_inc(P.sem("s_wt"), 16)
        sy.dma_start(idn_t[:], idn.ap()).then_inc(P.sem("s_wt"), 16)
        sy.dma_start(sfe_t[0:65, :], t_rep.ap()).then_inc(P.sem("s_wt"), 16)
        sy.dma_start(w2a_t[0:65, :].rearrange("p (a m) -> p a m", a=2),
                     w2aug.ap().transpose([1, 0, 2])).then_inc(P.sem("s_wt"), 16)
        sy.dma_start(fcv[0:64, 0:2], fw1.ap()).then_inc(P.sem("s_wt"), 16)
        sy.dma_start(fcv[0:64, 2:4], fb1.ap()).then_inc(P.sem("s_wt"), 16)
        sy.dma_start(wsh_t[:].rearrange("p (a m) -> p a m", a=3),
                     wsh.ap().transpose([1, 0, 2])).then_inc(P.sem("s_wt"), 16)
        sy.dma_start(twc_t[:], twc.ap()).then_inc(P.sem("s_wt"), 16)
        sy.dma_start(tw2c_t[:], tw2c.ap()).then_inc(P.sem("s_wt"), 16)
    P.emit("sync", sy_pre)
    WT0 = P.bump("s_wt", 16 * 12)

    def v_pre(v):
        for _i in range(100):
            v.memset(arena[:, 0:512], 1.0).then_inc(P.sem("s_spin"), 1)
            v.wait_ge(P.sem("s_spin"), _i + 1)
        v.memset(eps2[:], 0.0)
        ins = v.memset(arena[:, 0:320], 1.0)
        ins.then_inc(P.sem("s_ph"), 1)
    P.emit("vector", v_pre)
    P.bump("s_ph", 1)

    nchp = L // 256
    for c0 in range(2):
        def sy_ldx(sy, c=c0):
            sy.dma_start(xpb[c % 2][:].rearrange("p (k m) -> p k m", k=8),
                         x2f.ap()[:, :, c * 256:(c + 1) * 256].transpose([1, 0, 2])
                         ).then_inc(P.sem("s_in"), 16)
        P.emit("sync", sy_ldx)
        P.bump("s_in", 16)
    for c in range(nchp):
        for m in range(12):
            bk, pt, freed = pool.take()

            def t_pj(t, c=c, m=m, pt=pt, freed=freed):
                if c == 0 and m == 0:
                    t.wait_ge(P.sem("s_wt"), WT0)
                    t.wait_ge(P.sem("s_wtg"), 16)
                    t.wait_ge(P.sem("s_phs"), WBARC)
                    t.wait_ge(P.sem("s_ph"), 1)
                if m == 0:
                    t.wait_ge(P.sem("s_in"), 16 * (c + 1))
                if freed is not None:
                    t.wait_ge(P.sem(freed[0]), freed[1])
                for k in range(8):
                    ins = t.matmul(pt[:, 0:256],
                                   wp_t[:, k * 1536 + m * 128:k * 1536 + m * 128 + 128],
                                   xpb[c % 2][:, k * 256:(k + 1) * 256],
                                   start=(k == 0), stop=(k == 7))
                ins.then_inc(P.sem("s_mm"), 1)
            P.emit("tensor", t_pj)
            need = P.bump("s_mm", 1)

            stg_prev = P.val("s_stage")

            def v_pe(v, m=m, pt=pt, need=need, stg_prev=stg_prev):
                v.wait_ge(P.sem("s_mm"), need + 1)
                v.wait_ge(P.sem("s_stage"), stg_prev)
                ins = v.tensor_copy(pev[:, 0:256], pt[:, 0:256])
                ins.then_inc(P.sem("s_ev"), 1)
            P.emit("vector", v_pe)
            evn = P.bump("s_ev", 1)
            pool.release(bk, "s_ev", evn)

            def sy_stg(sy, c=c, m=m, evn=evn):
                sy.wait_ge(P.sem("s_ev"), evn)
                sy.dma_start(
                    stg.ap()[c * 2:(c + 1) * 2, m * 128:(m + 1) * 128, :].transpose([1, 0, 2]),
                    pev[:, 0:256].rearrange("p (a f) -> p a f", a=2),
                ).then_inc(P.sem("s_stage"), 16)
            P.emit("sync", sy_stg)
            P.bump("s_stage", 16)
        if c + 2 < nchp:
            ev_need = P.val("s_ev")

            def sy_ldx2(sy, c2=c + 2, ev_need=ev_need):
                sy.wait_ge(P.sem("s_ev"), ev_need)
                sy.dma_start(xpb[c2 % 2][:].rearrange("p (k m) -> p k m", k=8),
                             x2f.ap()[:, :, c2 * 256:(c2 + 1) * 256].transpose([1, 0, 2])
                             ).then_inc(P.sem("s_in"), 16)
            P.emit("sync", sy_ldx2)
            P.bump("s_in", 16)
    drain(wait_ph=1)
    if cut == 'proj':
        fin = P.val("s_stage")

        def sy_f(sy, fin=fin):
            sy.wait_ge(P.sem("s_stage"), fin)
        P.emit("sync", sy_f)
        return P.build()

    nchf = L // 512
    for o in range(2):
        ev_prev = P.val("s_ev")

        def sy_t(sy, ev_prev=ev_prev):
            sy.wait_ge(P.sem("s_ev"), ev_prev)
            sy.dma_start(sfe_t[0:65, :], t_rep.ap()).then_inc(P.sem("s_in"), 16)
        P.emit("sync", sy_t)
        tin = P.bump("s_in", 16)

        def s_feat_o(s, o=o, tin=tin):
            s.wait_ge(P.sem("s_in"), tin)
            s.wait_ge(P.sem("s_ph"), 1)
            if o == 0:
                s.wait_ge(P.sem("s_wt"), WT0)
            s.activation(sfe_t[0:64, :], sfe_t[0:64, :], AF.Silu,
                         scale=fcv[0:64, o:o + 1], bias=fcv[0:64, 2 + o:3 + o])
            ins = s.mul(sfe_t[0:64, :], sfe_t[0:64, :], 0.5)
            ins.then_inc(P.sem("s_phs"), 1)
        P.emit("scalar", s_feat_o)
        phs_f = P.bump("s_phs", 1)

        def v_ones(v, o=o, phs_f=phs_f):
            v.wait_ge(P.sem("s_phs"), phs_f)
            ins = v.memset(sfe_t[64:65, :], 0.5)
            ins.then_inc(P.sem("s_ph"), 1)
        P.emit("vector", v_ones)
        ph_f = P.bump("s_ph", 1)

        for c in range(nchf):
            for m in range(4):
                bk, pt, freed = pool.take()

                def t_h(t, o=o, c=c, m=m, pt=pt, freed=freed, ph_f=ph_f):
                    if c == 0 and m == 0:
                        t.wait_ge(P.sem("s_ph"), ph_f)
                    if freed is not None:
                        t.wait_ge(P.sem(freed[0]), freed[1])
                    ins = t.matmul(pt[:, 0:512],
                                   w2a_t[0:65, o * 512 + m * 128:o * 512 + (m + 1) * 128],
                                   sfe_t[0:65, c * 512:(c + 1) * 512],
                                   start=True, stop=True)
                    ins.then_inc(P.sem("s_mm"), 1)
                P.emit("tensor", t_h)
                need = P.bump("s_mm", 1)

                stg_prev = P.val("s_stage")

                def v_he(v, pt=pt, need=need, stg_prev=stg_prev):
                    v.wait_ge(P.sem("s_mm"), need + 1)
                    v.wait_ge(P.sem("s_stage"), stg_prev)
                    ins = v.tensor_copy(pev[:, 0:512], pt[:, 0:512])
                    ins.then_inc(P.sem("s_ev"), 1)
                P.emit("vector", v_he)
                evn = P.bump("s_ev", 1)
                pool.release(bk, "s_ev", evn)

                def sy_sth(sy, o=o, c=c, m=m, evn=evn):
                    sy.wait_ge(P.sem("s_ev"), evn)
                    sy.dma_start(
                        stgh.ap()[c * 4:(c + 1) * 4, o * 512 + m * 128:o * 512 + (m + 1) * 128, :
                                  ].transpose([1, 0, 2]),
                        pev[:, 0:512].rearrange("p (a f) -> p a f", a=4),
                    ).then_inc(P.sem("s_stage"), 16)
                P.emit("sync", sy_sth)
                P.bump("s_stage", 16)
        drain()
    if cut == 'filt':
        fin = P.val("s_stage")

        def sy_f(sy, fin=fin):
            sy.wait_ge(P.sem("s_stage"), fin)
        P.emit("sync", sy_f)
        return P.build()

    import dataclasses

    def sb_ap(t, offset, dims, pbase=0, nparts=128):
        ap = t[:]
        W = ap.ap[0][0]
        return dataclasses.replace(
            ap, offset=pbase * W + offset,
            ap=type(ap.ap)([[W, nparts]] + [list(d) for d in dims]))

    GR_OFF = 8192
    GI_OFF = 16384
    QR_OFF = 0
    QI_OFF = 8192
    ZR_OFF = 0
    ZI_OFF = 8192

    def conv_tile(kind, o, idx, dump=None):
        real_in = (kind == 'h')
        pb = idx
        st_pre = P.val("s_stage")
        mm_pre = P.val("s_mm")

        def sy_z(sy, kind=kind, o=o, pb=pb, st_pre=st_pre, mm_pre=mm_pre):
            sy.wait_ge(P.sem("s_stage"), st_pre)
            sy.wait_ge(P.sem("s_mm"), mm_pre)
            if kind == 'h':
                sy.dma_start(arena[0:64, 0:8192].rearrange("p (a f) -> p a f", a=64),
                             stgh.ap()[0:64, o * 512 + pb * 64:o * 512 + (pb + 1) * 64, :]
                             ).then_inc(P.sem("s_in"), 16)
            else:
                src = stg if kind == 'v' else stgu
                sy.dma_start(arena[0:64, 0:8192].rearrange("p (a f) -> p a f", a=64),
                             src.ap()[0:64, pb * 64:(pb + 1) * 64, :]
                             ).then_inc(P.sem("s_in"), 16)
                sy.dma_start(arena[64:128, 0:8192].rearrange("p (a f) -> p a f", a=64),
                             src.ap()[0:64, 256 + pb * 64:256 + (pb + 1) * 64, :]
                             ).then_inc(P.sem("s_in"), 16)
            sy.dma_start(dscr[0:1, 0:16], idn.ap()[0:1, 0:16]).then_inc(P.sem("s_in"), 16)
        P.emit("sync", sy_z)
        zin = P.bump("s_in", 32 if real_in else 48)

        for n in range(16):
            for pl in range(2):
                bk, pt, freed = pool.take()

                def t_f1(t, n=n, pl=pl, pt=pt, freed=freed, zin=zin, real_in=real_in):
                    if n == 0 and pl == 0:
                        t.wait_ge(P.sem("s_in"), zin)
                    if freed is not None:
                        t.wait_ge(P.sem(freed[0]), freed[1])
                    if real_in:
                        ins = t.matmul(pt[:, 0:512],
                                       f1r_t[0:64, pl * 128:(pl + 1) * 128],
                                       arena[0:64, n * 512:(n + 1) * 512],
                                       start=True, stop=True)
                    else:
                        ins = t.matmul(pt[:, 0:512],
                                       f1s_t[:, pl * 128:(pl + 1) * 128],
                                       arena[:, n * 512:(n + 1) * 512],
                                       start=True, stop=True)
                    ins.then_inc(P.sem("s_mm"), 1)
                P.emit("tensor", t_f1)
                need = P.bump("s_mm", 1)

                def s_ge(s, n=n, pl=pl, pt=pt, need=need, first=(n == 0 and pl == 0),
                         st_pre=st_pre):
                    s.wait_ge(P.sem("s_mm"), need + 1)
                    if first:
                        s.wait_ge(P.sem("s_stage"), st_pre)
                    goff = (GR_OFF if pl == 0 else GI_OFF) + n * 512
                    s.activation(arena[:, goff:goff + 512], pt[:, 0:512],
                                 AF.Copy).then_inc(P.sem("s_evs"), 1)
                P.emit("scalar", s_ge)
                evn = P.bump("s_evs", 1)
                pool.release(bk, "s_evs", evn)
        drain()
        f1_evs = P.val("s_evs")

        for pl in range(2):
            for q in range(16):
                bk, pt, freed = pool.take()
                need = None
                for j in range(4):
                    def t_tr(t, pl=pl, q=q, j=j, pt=pt, freed=freed, f1_evs=f1_evs):
                        if pl == 0 and q == 0 and j == 0:
                            t.wait_ge(P.sem("s_evs"), f1_evs)
                        if j == 0 and freed is not None:
                            t.wait_ge(P.sem(freed[0]), freed[1])
                        goff = (GR_OFF if pl == 0 else GI_OFF) + (q * 4 + j) * 128
                        t.transpose(pt[:, j * 128:(j + 1) * 128],
                                    arena[:, goff:goff + 128],
                                    idn_t[:]).then_inc(P.sem("s_mm"), 1)
                    P.emit("tensor", t_tr)
                    need = P.bump("s_mm", 1)

                def v_tre(v, pl=pl, q=q, pt=pt, need=need,
                          first=(pl == 0 and q == 0), st_pre=st_pre):
                    v.wait_ge(P.sem("s_mm"), need + 1)
                    if first:
                        v.wait_ge(P.sem("s_stage"), st_pre)
                    dst = sb_ap(gt_t, (ZR_OFF if pl == 0 else ZI_OFF) + q * 4,
                                [[1, 4], [64, 128]])
                    v.tensor_copy(dst, pt[:, 0:512].rearrange("p (a f) -> p a f", a=4)
                                  ).then_inc(P.sem("s_ev"), 1)
                P.emit("vector", v_tre)
                evn = P.bump("s_ev", 1)
                pool.release(bk, "s_ev", evn)
        drain()
        tr_evs = P.val("s_ev")

        tw0 = P.val("s_ev")
        for k1 in range(128):
            def v_tw(v, k1=k1):
                gr = gt_t[:, ZR_OFF + k1 * 64:ZR_OFF + k1 * 64 + 64]
                gi = gt_t[:, ZI_OFF + k1 * 64:ZI_OFF + k1 * 64 + 64]
                v.tensor_scalar_mul(zch[:, 0:64], gi, twc_t[:, 128 + k1:129 + k1])
                v.tensor_scalar_mul(zch[:, 64:128], gr, twc_t[:, 128 + k1:129 + k1])
                v.scalar_tensor_tensor(gr, gr, twc_t[:, k1:k1 + 1], zch[:, 0:64],
                                       ALU.mult, ALU.subtract)
                ins = v.scalar_tensor_tensor(gi, gi, twc_t[:, k1:k1 + 1], zch[:, 64:128],
                                             ALU.mult, ALU.add)
                ins.then_inc(P.sem("s_ev"), 1)
            P.emit("vector", v_tw)
            P.bump("s_ev", 1)

        for n in range(16):
            bkr, ptr, fr = pool.take()
            bki, pti, fi = pool.take()

            def t_f2(t, n=n, ptr=ptr, pti=pti, fr=fr, fi=fi, tw_need=tw0 + 8 * (n + 1)):
                t.wait_ge(P.sem("s_ev"), tw_need)
                if fr is not None:
                    t.wait_ge(P.sem(fr[0]), fr[1])
                if fi is not None:
                    t.wait_ge(P.sem(fi[0]), fi[1])
                rr = gt_t[:, ZR_OFF + n * 512:ZR_OFF + (n + 1) * 512]
                ri = gt_t[:, ZI_OFF + n * 512:ZI_OFF + (n + 1) * 512]
                t.matmul(ptr[:, 0:512], wsh_t[:, 0:128], rr,
                         start=True, stop=False).then_inc(P.sem("s_mm"), 1)
                t.matmul(ptr[:, 0:512], wsh_t[:, 384:512], ri,
                         start=False, stop=True).then_inc(P.sem("s_mm"), 1)
                t.matmul(pti[:, 0:512], wsh_t[:, 192:320], rr,
                         start=True, stop=False).then_inc(P.sem("s_mm"), 1)
                t.matmul(pti[:, 0:512], wsh_t[:, 0:128], ri,
                         start=False, stop=True).then_inc(P.sem("s_mm"), 1)
            P.emit("tensor", t_f2)
            P.bump("s_mm", 4)
            zneed = P.val("s_mm")

            for pl in range(2):
                def s_ze(s, n=n, pl=pl, pt=(ptr if pl == 0 else pti), zneed=zneed):
                    s.wait_ge(P.sem("s_mm"), zneed + 1)
                    zoff = (ZR_OFF if pl == 0 else ZI_OFF) + n * 512
                    s.activation(gt_t[:, zoff:zoff + 512], pt[:, 0:512],
                                 AF.Copy).then_inc(P.sem("s_evs"), 1)
                P.emit("scalar", s_ze)
                evn = P.bump("s_evs", 1)
                pool.release(bkr if pl == 0 else bki, "s_evs", evn)
        drain()
        f2_evs = P.val("s_evs")

        if dump == 'z':
            def sy_dz(sy, f2_evs=f2_evs):
                sy.wait_ge(P.sem("s_evs"), f2_evs)
                for pl in range(2):
                    sy.dma_start(dbg.ap()[pl, :, :], gt_t[:, pl * 8192:(pl + 1) * 8192]
                                 ).then_inc(P.sem("s_stage"), 16)
            P.emit("sync", sy_dz)
            P.bump("s_stage", 32)
            return

        if kind == 'h':
            def sy_hx(sy, o=o, pb=pb, f2_evs=f2_evs):
                sy.wait_ge(P.sem("s_evs"), f2_evs)
                for pl in range(2):
                    sy.dma_start(hspec.ap()[o, pb, pl, :, :],
                                 gt_t[:, pl * 8192:(pl + 1) * 8192]
                                 ).then_inc(P.sem("s_stage"), 16)
            P.emit("sync", sy_hx)
            P.bump("s_stage", 32)
            return

        hs_pre = P.val("s_stage")
        prev_qneed = 0
        for g in range(16):
            ev_free = P.val("s_ev")

            def sy_sd(sy, o=o, pb=pb, g=g, ev_free=ev_free, hs_pre=hs_pre):
                sy.wait_ge(P.sem("s_ev"), ev_free)
                if g == 0:
                    sy.wait_ge(P.sem("s_stage"), hs_pre)
                for sl, (cb, pl) in enumerate([(pb, 0), (pb, 1), (4 + pb, 0), (4 + pb, 1)]):
                    sy.dma_start(sdch[:, sl * 512:(sl + 1) * 512],
                                 hspec.ap()[o, cb, pl, :, g * 512:(g + 1) * 512]
                                 ).then_inc(P.sem("s_sd"), 16)
                sy.dma_start(dscr[0:1, 0:16], idn.ap()[0:1, 0:16]).then_inc(P.sem("s_sd"), 16)
            P.emit("sync", sy_sd)
            sdn = P.bump("s_sd", 80)

            def v_ds(v, g=g, sdn=sdn, f2_evs=f2_evs, prev_qneed=prev_qneed):
                v.wait_ge(P.sem("s_sd"), sdn)
                if g == 0:
                    v.wait_ge(P.sem("s_evs"), f2_evs)
                else:
                    v.wait_ge(P.sem("s_mm"), prev_qneed)
                v.tensor_sub(uvch[:, 1024:1536], sdch[:, 0:512], sdch[:, 1024:1536])
                v.tensor_sub(uvch[:, 1536:2048], sdch[:, 512:1024], sdch[:, 1536:2048])
                v.tensor_add(sdch[:, 0:512], sdch[:, 0:512], sdch[:, 1024:1536])
                ins = v.tensor_add(sdch[:, 512:1024], sdch[:, 512:1024],
                                   sdch[:, 1536:2048])
                ins.then_inc(P.sem("s_ev"), 1)
            P.emit("vector", v_ds)
            ev_d = P.bump("s_ev", 1)

            bkA, ptA, fA = pool.take()
            bkB, ptB, fB = pool.take()
            for pl in range(2):
                def t_pm(t, g=g, pl=pl, pt=(ptA if pl == 0 else ptB),
                         fx=(fA if pl == 0 else fB), src=(1024 if pl == 0 else 1536),
                         ev_d=ev_d):
                    if fx is not None:
                        t.wait_ge(P.sem(fx[0]), fx[1])
                    if pl == 0:
                        t.wait_ge(P.sem("s_ev"), ev_d)
                    if g == 0:
                        t.matmul(pt[:, 0:64], prev_t[:, 128:256], uvch[:, src:src + 64],
                                 start=True, stop=True).then_inc(P.sem("s_mm"), 1)
                        t.matmul(pt[:, 64:512], prev_t[:, 0:128], uvch[:, src + 64:src + 512],
                                 start=True, stop=True).then_inc(P.sem("s_mm"), 1)
                    else:
                        t.matmul(pt[:, 0:512], prev_t[:, 0:128], uvch[:, src:src + 512],
                                 start=True, stop=True).then_inc(P.sem("s_mm"), 1)
                P.emit("tensor", t_pm)
                P.bump("s_mm", 2 if g == 0 else 1)
            perm_need = P.val("s_mm")

            def v_u(v, g=g):
                zr = gt_t[:, ZR_OFF + g * 512:ZR_OFF + (g + 1) * 512]
                zi = gt_t[:, ZI_OFF + g * 512:ZI_OFF + (g + 1) * 512]
                v.tensor_mul(zch[:, 0:512], zr, sdch[:, 0:512])
                v.tensor_mul(zch[:, 512:1024], zi, sdch[:, 512:1024])
                v.tensor_sub(uvch[:, 0:512], zch[:, 0:512], zch[:, 512:1024])
                v.tensor_mul(zch[:, 0:512], zr, sdch[:, 512:1024])
                v.tensor_mul(zch[:, 512:1024], zi, sdch[:, 0:512])
                ins = v.tensor_add(uvch[:, 512:1024], zch[:, 0:512], zch[:, 512:1024])
                ins.then_inc(P.sem("s_ev"), 1)
            P.emit("vector", v_u)
            ev_u = P.bump("s_ev", 1)

            def v_dv(v, ptA=ptA, ptB=ptB, perm_need=perm_need):
                v.wait_ge(P.sem("s_mm"), perm_need + 1)
                v.tensor_copy(uvch[:, 1024:1536], ptA[:, 0:512])
                ins = v.tensor_copy(uvch[:, 1536:2048], ptB[:, 0:512])
                ins.then_inc(P.sem("s_ev"), 1)
            P.emit("vector", v_dv)
            ev_dp = P.bump("s_ev", 1)
            pool.release(bkA, "s_ev", ev_dp)
            pool.release(bkB, "s_ev", ev_dp)

            def v_v(v, g=g):
                if g == 0:
                    segs = [(0, 0, 1), (8128, 64, 7)]
                else:
                    segs = [((128 - 8 * g) * 64, 0, 8)]
                ins = None
                for base, c0, nb in segs:
                    w = nb * 64
                    if nb > 1:
                        zr = sb_ap(gt_t, ZR_OFF + base, [[-64, nb], [1, 64]])
                        zi = sb_ap(gt_t, ZI_OFF + base, [[-64, nb], [1, 64]])
                        rs = lambda apx: apx.rearrange("p (a f) -> p a f", a=nb)
                    else:
                        zr = gt_t[:, ZR_OFF + base:ZR_OFF + base + 64]
                        zi = gt_t[:, ZI_OFF + base:ZI_OFF + base + 64]
                        rs = lambda apx: apx
                    dpr = rs(uvch[:, 1024 + c0:1024 + c0 + w])
                    dpi = rs(uvch[:, 1536 + c0:1536 + c0 + w])
                    t1 = rs(zch[:, 0:w])
                    t2 = rs(zch[:, 512:512 + w])
                    t3 = rs(sdch[:, 0:w])
                    t4 = rs(sdch[:, 512:512 + w])
                    v.tensor_mul(t1, zr, dpi)
                    v.tensor_mul(t2, zi, dpr)
                    v.tensor_mul(t3, zr, dpr)
                    v.tensor_mul(t4, zi, dpi)
                    v.tensor_add(rs(uvch[:, 1024 + c0:1024 + c0 + w]), t3, t4)
                    ins = v.tensor_sub(rs(uvch[:, 1536 + c0:1536 + c0 + w]), t1, t2)
                ins.then_inc(P.sem("s_ev"), 1)
            P.emit("vector", v_v)
            ev_v = P.bump("s_ev", 1)

            bkQr, ptQr, fQr = pool.take()
            bkQi, ptQi, fQi = pool.take()

            def t_i1(t, g=g, ptQr=ptQr, ptQi=ptQi, fQr=fQr, fQi=fQi,
                     ev_u=ev_u, ev_v=ev_v):
                if fQr is not None:
                    t.wait_ge(P.sem(fQr[0]), fQr[1])
                if fQi is not None:
                    t.wait_ge(P.sem(fQi[0]), fQi[1])
                t.wait_ge(P.sem("s_ev"), ev_u)
                W = wi1_t
                if g == 0:
                    t.matmul(ptQr[:, 0:64], W[:, 0 * 128:1 * 128], uvch[:, 0:64],
                             start=True, stop=False).then_inc(P.sem("s_mm"), 1)
                    t.matmul(ptQr[:, 0:64], W[:, 2 * 128:3 * 128], uvch[:, 512:576],
                             start=False, stop=False).then_inc(P.sem("s_mm"), 1)
                    t.matmul(ptQi[:, 0:64], W[:, 1 * 128:2 * 128], uvch[:, 0:64],
                             start=True, stop=False).then_inc(P.sem("s_mm"), 1)
                    t.matmul(ptQi[:, 0:64], W[:, 0 * 128:1 * 128], uvch[:, 512:576],
                             start=False, stop=False).then_inc(P.sem("s_mm"), 1)
                    t.wait_ge(P.sem("s_ev"), ev_v)
                    t.matmul(ptQr[:, 0:64], W[:, 6 * 128:7 * 128], uvch[:, 1024:1088],
                             start=False, stop=False).then_inc(P.sem("s_mm"), 1)
                    t.matmul(ptQr[:, 0:64], W[:, 8 * 128:9 * 128], uvch[:, 1536:1600],
                             start=False, stop=True).then_inc(P.sem("s_mm"), 1)
                    t.matmul(ptQi[:, 0:64], W[:, 7 * 128:8 * 128], uvch[:, 1024:1088],
                             start=False, stop=False).then_inc(P.sem("s_mm"), 1)
                    t.matmul(ptQi[:, 0:64], W[:, 6 * 128:7 * 128], uvch[:, 1536:1600],
                             start=False, stop=True).then_inc(P.sem("s_mm"), 1)
                    t.matmul(ptQr[:, 64:512], W[:, 0 * 128:1 * 128], uvch[:, 64:512],
                             start=True, stop=False).then_inc(P.sem("s_mm"), 1)
                    t.matmul(ptQr[:, 64:512], W[:, 2 * 128:3 * 128], uvch[:, 576:1024],
                             start=False, stop=False).then_inc(P.sem("s_mm"), 1)
                    t.matmul(ptQi[:, 64:512], W[:, 1 * 128:2 * 128], uvch[:, 64:512],
                             start=True, stop=False).then_inc(P.sem("s_mm"), 1)
                    t.matmul(ptQi[:, 64:512], W[:, 0 * 128:1 * 128], uvch[:, 576:1024],
                             start=False, stop=False).then_inc(P.sem("s_mm"), 1)
                    t.matmul(ptQr[:, 64:512], W[:, 3 * 128:4 * 128], uvch[:, 1088:1536],
                             start=False, stop=False).then_inc(P.sem("s_mm"), 1)
                    t.matmul(ptQr[:, 64:512], W[:, 5 * 128:6 * 128], uvch[:, 1600:2048],
                             start=False, stop=True).then_inc(P.sem("s_mm"), 1)
                    t.matmul(ptQi[:, 64:512], W[:, 4 * 128:5 * 128], uvch[:, 1088:1536],
                             start=False, stop=False).then_inc(P.sem("s_mm"), 1)
                    t.matmul(ptQi[:, 64:512], W[:, 3 * 128:4 * 128], uvch[:, 1600:2048],
                             start=False, stop=True).then_inc(P.sem("s_mm"), 1)
                else:
                    t.matmul(ptQr[:, 0:512], W[:, 0 * 128:1 * 128], uvch[:, 0:512],
                             start=True, stop=False).then_inc(P.sem("s_mm"), 1)
                    t.matmul(ptQr[:, 0:512], W[:, 2 * 128:3 * 128], uvch[:, 512:1024],
                             start=False, stop=False).then_inc(P.sem("s_mm"), 1)
                    t.matmul(ptQi[:, 0:512], W[:, 1 * 128:2 * 128], uvch[:, 0:512],
                             start=True, stop=False).then_inc(P.sem("s_mm"), 1)
                    t.matmul(ptQi[:, 0:512], W[:, 0 * 128:1 * 128], uvch[:, 512:1024],
                             start=False, stop=False).then_inc(P.sem("s_mm"), 1)
                    t.wait_ge(P.sem("s_ev"), ev_v)
                    t.matmul(ptQr[:, 0:512], W[:, 3 * 128:4 * 128], uvch[:, 1024:1536],
                             start=False, stop=False).then_inc(P.sem("s_mm"), 1)
                    t.matmul(ptQr[:, 0:512], W[:, 5 * 128:6 * 128], uvch[:, 1536:2048],
                             start=False, stop=True).then_inc(P.sem("s_mm"), 1)
                    t.matmul(ptQi[:, 0:512], W[:, 4 * 128:5 * 128], uvch[:, 1024:1536],
                             start=False, stop=False).then_inc(P.sem("s_mm"), 1)
                    t.matmul(ptQi[:, 0:512], W[:, 3 * 128:4 * 128], uvch[:, 1536:2048],
                             start=False, stop=True).then_inc(P.sem("s_mm"), 1)
            P.emit("tensor", t_i1)
            P.bump("s_mm", 16 if g == 0 else 8)
            qneed = P.val("s_mm")
            prev_qneed = qneed

            for pl in range(2):
                def s_qe(s, g=g, pl=pl, pt=(ptQr if pl == 0 else ptQi), qneed=qneed):
                    s.wait_ge(P.sem("s_mm"), qneed + 1)
                    qoff = (QR_OFF if pl == 0 else QI_OFF) + g * 512
                    s.activation(arena[:, qoff:qoff + 512], pt[:, 0:512],
                                 AF.Copy).then_inc(P.sem("s_evs"), 1)
                P.emit("scalar", s_qe)
                evn = P.bump("s_evs", 1)
                pool.release(bkQr if pl == 0 else bkQi, "s_evs", evn)
            if dump == 'uv' and g == 0:
                drain()
                ev_now = P.val("s_ev")

                def sy_duv(sy, ev_now=ev_now):
                    sy.wait_ge(P.sem("s_ev"), ev_now)
                    sy.dma_start(dbg.ap()[0, :, 0:2048], uvch[:]
                                 ).then_inc(P.sem("s_stage"), 16)
                    sy.dma_start(dbg.ap()[1, :, 0:2048], sdch[:]
                                 ).then_inc(P.sem("s_stage"), 16)
                P.emit("sync", sy_duv)
                P.bump("s_stage", 32)
                return
        drain()
        i1_evs = P.val("s_evs")
        if dump == 'q':
            def sy_dq(sy, i1_evs=i1_evs):
                sy.wait_ge(P.sem("s_evs"), i1_evs)
                for pl in range(2):
                    sy.dma_start(dbg.ap()[pl, :, :],
                                 arena[:, pl * 8192:(pl + 1) * 8192]
                                 ).then_inc(P.sem("s_stage"), 16)
            P.emit("sync", sy_dq)
            P.bump("s_stage", 32)
            return

        for pl in range(2):
            for q in range(16):
                bk, pt, freed = pool.take()
                need = None
                for j in range(4):
                    def t_qt(t, pl=pl, q=q, j=j, pt=pt, freed=freed, i1_evs=i1_evs):
                        if pl == 0 and q == 0 and j == 0:
                            t.wait_ge(P.sem("s_evs"), i1_evs)
                        if j == 0 and freed is not None:
                            t.wait_ge(P.sem(freed[0]), freed[1])
                        p = q * 4 + j
                        src = sb_ap(arena, (QR_OFF if pl == 0 else QI_OFF) + p,
                                    [[64, 128]])
                        t.transpose(pt[:, j * 128:(j + 1) * 128], src,
                                    idn_t[:]).then_inc(P.sem("s_mm"), 1)
                    P.emit("tensor", t_qt)
                    need = P.bump("s_mm", 1)

                def v_qte(v, pl=pl, q=q, pt=pt, need=need):
                    v.wait_ge(P.sem("s_mm"), need + 1)
                    dst = sb_ap(gt_t, (ZR_OFF if pl == 0 else ZI_OFF) + q * 4,
                                [[1, 4], [64, 128]])
                    v.tensor_copy(dst, pt[:, 0:512].rearrange("p (a f) -> p a f", a=4)
                                  ).then_inc(P.sem("s_ev"), 1)
                P.emit("vector", v_qte)
                evn = P.bump("s_ev", 1)
                pool.release(bk, "s_ev", evn)
        drain()
        qt_evs = P.val("s_ev")
        if dump == 'qt':
            mmqt = P.val("s_mm")

            def sy_dqt(sy, qt_evs=qt_evs, mmqt=mmqt):
                sy.wait_ge(P.sem("s_ev"), qt_evs)
                for pl in range(2):
                    sy.dma_start(dbg.ap()[pl, :, :], gt_t[:, pl * 8192:(pl + 1) * 8192]
                                 ).then_inc(P.sem("s_stage"), 16)
            P.emit("sync", sy_dqt)
            P.bump("s_stage", 32)
            return

        tw20 = P.val("s_ev")
        for l2 in range(128):
            def v_tw2(v, l2=l2):
                qr = gt_t[:, ZR_OFF + l2 * 64:ZR_OFF + l2 * 64 + 64]
                qi = gt_t[:, ZI_OFF + l2 * 64:ZI_OFF + l2 * 64 + 64]
                v.tensor_scalar_mul(zch[:, 0:64], qi, tw2c_t[:, 128 + l2:129 + l2])
                v.tensor_scalar_mul(zch[:, 64:128], qr, tw2c_t[:, 128 + l2:129 + l2])
                v.scalar_tensor_tensor(qr, qr, tw2c_t[:, l2:l2 + 1], zch[:, 0:64],
                                       ALU.mult, ALU.subtract)
                ins = v.scalar_tensor_tensor(qi, qi, tw2c_t[:, l2:l2 + 1], zch[:, 64:128],
                                             ALU.mult, ALU.add)
                ins.then_inc(P.sem("s_ev"), 1)
            P.emit("vector", v_tw2)
            P.bump("s_ev", 1)

        for n in range(16):
            bkr, ptr, fr = pool.take()
            bki, pti, fi = pool.take()

            def t_i2(t, n=n, ptr=ptr, pti=pti, fr=fr, fi=fi, tw_need=tw20 + 8 * (n + 1)):
                t.wait_ge(P.sem("s_ev"), tw_need)
                if fr is not None:
                    t.wait_ge(P.sem(fr[0]), fr[1])
                if fi is not None:
                    t.wait_ge(P.sem(fi[0]), fi[1])
                rr = gt_t[:, ZR_OFF + n * 512:ZR_OFF + (n + 1) * 512]
                ri = gt_t[:, ZI_OFF + n * 512:ZI_OFF + (n + 1) * 512]
                t.matmul(ptr[0:64, 0:512], wsh_t[:, 128:192], rr,
                         start=True, stop=False).then_inc(P.sem("s_mm"), 1)
                t.matmul(ptr[0:64, 0:512], wsh_t[:, 512:576], ri,
                         start=False, stop=True).then_inc(P.sem("s_mm"), 1)
                t.matmul(pti[0:64, 0:512], wsh_t[:, 320:384], rr,
                         start=True, stop=False).then_inc(P.sem("s_mm"), 1)
                t.matmul(pti[0:64, 0:512], wsh_t[:, 128:192], ri,
                         start=False, stop=True).then_inc(P.sem("s_mm"), 1)
            P.emit("tensor", t_i2)
            P.bump("s_mm", 4)
            ineed = P.val("s_mm")

            for pl in range(2):
                def v_ie(v, n=n, pl=pl, pt=(ptr if pl == 0 else pti), ineed=ineed):
                    v.wait_ge(P.sem("s_mm"), ineed + 1)
                    ybase = 0 if pl == 0 else GI_OFF
                    srcp = sb_ap(pt, 0, [[64, 8], [1, 64]], nparts=64)
                    dst = sb_ap(arena, ybase + 8 * n, [[1, 8], [128, 64]], nparts=64)
                    ins = v.tensor_copy(dst, srcp)
                    ins.then_inc(P.sem("s_ev"), 1)
                P.emit("vector", v_ie)
                evn = P.bump("s_ev", 1)
                pool.release(bkr if pl == 0 else bki, "s_ev", evn)
        drain()
        if dump == 'p':
            ev_i2 = P.val("s_ev")

            def sy_dp(sy, ev_i2=ev_i2):
                sy.wait_ge(P.sem("s_ev"), ev_i2)
                sy.dma_start(dbg.ap()[0, 0:64, :], arena[0:64, 0:8192]
                             ).then_inc(P.sem("s_stage"), 16)
                sy.dma_start(dbg.ap()[1, 0:64, :], arena[0:64, GI_OFF:GI_OFF + 8192]
                             ).then_inc(P.sem("s_stage"), 16)
            P.emit("sync", sy_dp)
            P.bump("s_stage", 32)
            return

        qt_mm = P.val("s_mm")
        gbase = 512 + 512 * o
        for half in range(2):
            ch0 = gbase + (256 * half) + pb * 64
            ev_g_pre = P.val("s_ev")

            def sy_g(sy, ch0=ch0, qt_mm=qt_mm, half=half, ev_g_pre=ev_g_pre):
                if half == 0:
                    sy.wait_ge(P.sem("s_mm"), qt_mm)
                else:
                    sy.wait_ge(P.sem("s_ev"), ev_g_pre)
                sy.dma_start(arena[0:64, 8192:16384].rearrange(
                    "p (a f) -> p a f", a=64),
                    stg.ap()[0:64, ch0:ch0 + 64, :]).then_inc(P.sem("s_in"), 16)
                sy.dma_start(dscr[0:1, 0:16], idn.ap()[0:1, 0:16]).then_inc(P.sem("s_in"), 16)
            P.emit("sync", sy_g)
            gin = P.bump("s_in", 32)

            ysrc = 0 if half == 0 else GI_OFF
            if kind == 'v':
                def v_gm(v, gin=gin, ysrc=ysrc):
                    v.wait_ge(P.sem("s_in"), gin)
                    ins = v.tensor_mul(arena[0:64, ysrc:ysrc + 8192],
                                       arena[0:64, ysrc:ysrc + 8192],
                                       arena[0:64, 8192:16384])
                    ins.then_inc(P.sem("s_ev"), 1)
                P.emit("vector", v_gm)
                evg = P.bump("s_ev", 1)

                def sy_u1(sy, pb=pb, half=half, evg=evg, ysrc=ysrc):
                    sy.wait_ge(P.sem("s_ev"), evg)
                    sy.dma_start(
                        stgu.ap()[0:64, 256 * half + pb * 64:256 * half + (pb + 1) * 64, :],
                        arena[0:64, ysrc:ysrc + 8192].rearrange(
                            "p (a f) -> p a f", a=64)).then_inc(P.sem("s_stage"), 16)
                P.emit("sync", sy_u1)
                P.bump("s_stage", 16)
            else:
                for hx in range(4):
                    st_y = P.val("s_stage")

                    def v_gy(v, gin=gin, hx=hx, ysrc=ysrc, st_y=st_y):
                        v.wait_ge(P.sem("s_in"), gin)
                        v.wait_ge(P.sem("s_stage"), st_y)
                        ins = v.tensor_mul(
                            ybf[0:64, 0:2048],
                            arena[0:64, ysrc + hx * 2048:ysrc + (hx + 1) * 2048],
                            arena[0:64, 8192 + hx * 2048:8192 + (hx + 1) * 2048])
                        ins.then_inc(P.sem("s_ev"), 1)
                    P.emit("vector", v_gy)
                    evy = P.bump("s_ev", 1)

                    gch = 256 * half + pb * 64 + hx * 16
                    kt, r0 = gch // 128, gch % 128

                    def sy_y(sy, kt=kt, r0=r0, evy=evy):
                        sy.wait_ge(P.sem("s_ev"), evy)
                        dst = y_out.ap()[kt, r0:r0 + 16, :].rearrange(
                            "c (a f) -> c a f", a=64).transpose([1, 0, 2])
                        sy.dma_start(dst, ybf[0:64, 0:2048].rearrange(
                            "p (a f) -> p a f", a=16)).then_inc(P.sem("s_stage"), 16)
                    P.emit("sync", sy_y)
                    P.bump("s_stage", 16)

    if cut in ('p3a', 'c0a'):
        if cut == 'p3a':
            conv_tile('h', 0, 0)
        else:
            conv_tile('h', 0, 0)
            conv_tile('h', 0, 4)
            conv_tile('v', 0, 0, dump=dump)
        fin = P.val("s_stage")

        def sy_fc(sy, fin=fin):
            sy.wait_ge(P.sem("s_stage"), fin)
        P.emit("sync", sy_fc)
        return P.build()

    for o in range(2):
        for cb in range(8):
            conv_tile('h', o, cb)
    if cut == 'p3':
        fin = P.val("s_stage")

        def sy_f3(sy, fin=fin):
            sy.wait_ge(P.sem("s_stage"), fin)
        P.emit("sync", sy_f3)
        return P.build()

    for pb in range(4):
        conv_tile('v', 0, pb)
    if cut == 'c0':
        fin = P.val("s_stage")

        def sy_f4(sy, fin=fin):
            sy.wait_ge(P.sem("s_stage"), fin)
        P.emit("sync", sy_f4)
        return P.build()

    for pb in range(4):
        conv_tile('u', 1, pb)
    fin = P.val("s_stage")

    def sy_fin(sy, fin=fin):
        sy.wait_ge(P.sem("s_stage"), fin)
    P.emit("sync", sy_fin)
    return P.build()


# version 12 (final)
# speedup vs baseline: 1.1730x; 1.1730x over previous
import numpy as np
import ml_dtypes
import concourse.bass as bass
import concourse.mybir as mybir
from concourse.bass_utils import run_bass_kernel_spmd

F32 = mybir.dt.float32
BF16 = mybir.dt.bfloat16
I32 = mybir.dt.int32
AF = mybir.ActivationFunctionType
ALU = mybir.AluOpType
bf16 = ml_dtypes.bfloat16

D = 1024
DI = 1536
L = 8192
B = 4
NCORE = 8
TPC = 4096


class Prog:

    def __init__(self, nc):
        self.nc = nc
        self.seq = {"tensor": [], "vector": [], "scalar": [], "sync": [], "gpsimd": []}
        self.sems = {}

    def sem(self, name):
        if name not in self.sems:
            self.sems[name] = [self.nc.alloc_semaphore(name), 0]
        return self.sems[name][0]

    def bump(self, name, amount):
        self.sem(name)
        self.sems[name][1] += amount
        return self.sems[name][1]

    def val(self, name):
        self.sem(name)
        return self.sems[name][1]

    def emit(self, engine, fn):
        self.seq[engine].append(fn)

    def build(self):
        nc = self.nc
        with nc.Block() as blk:
            @blk.sync
            def _(e):
                for fn in self.seq["sync"]:
                    fn(e)

            @blk.gpsimd
            def _(e):
                for fn in self.seq["gpsimd"]:
                    fn(e)

            @blk.tensor
            def _(e):
                for fn in self.seq["tensor"]:
                    fn(e)

            @blk.vector
            def _(e):
                for fn in self.seq["vector"]:
                    fn(e)

            @blk.scalar
            def _(e):
                for fn in self.seq["scalar"]:
                    fn(e)
        return nc


class PsumPool:

    def __init__(self, nc, n, width, name="ps"):
        self.tiles = [nc.alloc_psum_tensor(f"{name}{i}", [128, width], F32) for i in range(n)]
        self.free_at = [None] * n
        self.idx = 0
        self.n = n

    def take(self):
        i = self.idx % self.n
        self.idx += 1
        return i, self.tiles[i], self.free_at[i]

    def release(self, i, sem_name, value):
        self.free_at[i] = (sem_name, value)


def build_mamba_A(chunk=128, tokens=TPC, debug=False, snap_chunk=None):
    import os
    GUARD = int(os.environ.get("GUARD", "0"))
    nch = tokens // chunk
    nc = bass.Bass("TRN2", target_bir_lowering=False)
    xh = nc.dram_tensor("xh", [8, 128, TPC + 2], F32, kind="ExternalInput")
    winT = nc.dram_tensor("winT", [8, 128, 3072], F32, kind="ExternalInput")
    dtwT = nc.dram_tensor("dtwT", [12, 128, 1536], F32, kind="ExternalInput")
    woutT = nc.dram_tensor("woutT", [12, 128, 1024], F32, kind="ExternalInput")
    cvec = nc.dram_tensor("cvec", [128, 12, 8], F32, kind="ExternalInput")
    out_main = nc.dram_tensor("out_main", [8, 128, TPC], BF16, kind="ExternalOutput")
    gmat = nc.dram_tensor("gmat", [12, 128, TPC], BF16, kind="ExternalOutput")
    totals = nc.dram_tensor("totals", [128, 12], F32, kind="ExternalOutput")
    if snap_chunk is not None:
        snap_g = nc.dram_tensor("snap_g", [128, 12 * chunk], F32, kind="ExternalOutput")
        snap_ae = nc.dram_tensor("snap_ae", [128, 12 * chunk], F32, kind="ExternalOutput")
        snap_sz = nc.dram_tensor("snap_sz", [128, 12 * chunk], F32, kind="ExternalOutput")
        sg_t = nc.alloc_sbuf_tensor("sg_t", [128, 12 * chunk], F32)
        sa_t = nc.alloc_sbuf_tensor("sa_t", [128, 12 * chunk], F32)
        ss_t = nc.alloc_sbuf_tensor("ss_t", [128, 12 * chunk], F32)
    if debug:
        dbg = {}
        for nm, wdt in [("d_xi", 12 * (chunk + 2)), ("d_xc", 12 * chunk), ("d_sz", 12 * chunk),
                        ("d_dt", 12 * chunk), ("d_ae", 12 * chunk), ("d_cm", 12 * chunk),
                        ("d_yb", 12 * chunk)]:
            dbg[nm] = nc.dram_tensor(nm, [128, wdt], F32, kind="ExternalOutput")

    w_in = nc.alloc_sbuf_tensor("w_in", [128, 8 * 3072], BF16)
    w_dt = nc.alloc_sbuf_tensor("w_dt", [128, 12 * 1536], BF16)
    w_out = nc.alloc_sbuf_tensor("w_out", [128, 12 * 1024], BF16)
    cv = nc.alloc_sbuf_tensor("cv", [128, 12 * 8], F32)

    W = chunk
    WH = chunk + 2
    xb = [nc.alloc_sbuf_tensor(f"xb{i}", [128, 8 * WH], BF16) for i in range(2)]
    xf = [nc.alloc_sbuf_tensor(f"xf{i}", [128, 8 * WH], F32) for i in range(2)]
    xi_t = nc.alloc_sbuf_tensor("xi", [128, 12 * WH], F32)
    dt_t = nc.alloc_sbuf_tensor("dt", [128, 12 * W], F32)
    sp_t = nc.alloc_sbuf_tensor("sp", [128, 12 * W], F32)
    ae_t = nc.alloc_sbuf_tensor("ae", [128, 12 * W], F32)
    xc_t = nc.alloc_sbuf_tensor("xc", [128, 12 * W], F32)
    w1_t = nc.alloc_sbuf_tensor("w1s", [128, 12 * W], F32)
    cm_t = nc.alloc_sbuf_tensor("cm", [128, 12 * W], F32)
    szb = nc.alloc_sbuf_tensor("szb", [128, 12 * W], F32)
    xcb = nc.alloc_sbuf_tensor("xcb", [128, 12 * W], BF16)
    yb = nc.alloc_sbuf_tensor("yb", [128, 12 * W], BF16)
    gbs = [nc.alloc_sbuf_tensor(f"gb{i}", [128, 12 * W], BF16) for i in range(2)]
    obs = [nc.alloc_sbuf_tensor(f"ob{i}", [128, 8 * W], BF16) for i in range(2)]
    ones = nc.alloc_sbuf_tensor("ones", [128, W], F32)
    carry = nc.alloc_sbuf_tensor("carry", [128, 16], F32)
    rs_t = nc.alloc_sbuf_tensor("rs", [128, 16], F32)

    P = Prog(nc)
    pool = PsumPool(nc, 7, WH)
    ps_dummy = nc.alloc_psum_tensor("ps_dummy", [128, 320], F32)
    for s in ["s_x", "s_xb", "s_wt", "s_cv", "s_mm", "s_evs", "s_evv", "s_ph", "s_phs", "s_st", "s_xc", "s_cy"]:
        P.sem(s)

    def g_pre(g):
        g.dma_start(w_in[:].rearrange("p (k m) -> p k m", k=8),
                    winT.ap().transpose([1, 0, 2])).then_inc(P.sem("s_wt"), 16)
        g.wait_ge(P.sem("s_wt"), 16)
        g.dma_start(w_dt[:].rearrange("p (k m) -> p k m", k=12),
                    dtwT.ap().transpose([1, 0, 2])).then_inc(P.sem("s_wt"), 16)
        g.wait_ge(P.sem("s_wt"), 32)
        g.dma_start(w_out[:].rearrange("p (k m) -> p k m", k=12),
                    woutT.ap().transpose([1, 0, 2])).then_inc(P.sem("s_wt"), 16)
    P.emit("gpsimd", g_pre)
    P.bump("s_wt", 48)

    def sy_pre(sy):
        sy.dma_start(cv[:], cvec.ap().rearrange("p a b -> p (a b)")).then_inc(P.sem("s_cv"), 16)
    P.emit("sync", sy_pre)
    P.bump("s_cv", 16)
    WT_FULL = P.val("s_wt")

    wbar_t = nc.alloc_sbuf_tensor("wbar_t", [128, 64], BF16)

    def s_wbar(s):
        s.wait_ge(P.sem("s_wt"), WT_FULL)
        ins = None
        for k in range(8):
            ins = s.activation(wbar_t[:], w_in[:, k * 3072 + 3008:k * 3072 + 3072], AF.Copy)
        for k in range(12):
            ins = s.activation(wbar_t[:], w_dt[:, k * 1536 + 1472:k * 1536 + 1536], AF.Copy)
        for k in range(12):
            ins = s.activation(wbar_t[:], w_out[:, k * 1024 + 960:k * 1024 + 1024], AF.Copy)
        ins.then_inc(P.sem("s_phs"), 1)
    P.emit("scalar", s_wbar)
    WBARA = P.bump("s_phs", 1)

    def v_pre(v):
        for _i in range(100):
            v.memset(ones[:], 1.0).then_inc(P.sem("s_spin"), 1)
            v.wait_ge(P.sem("s_spin"), _i + 1)
        v.memset(ones[:], 1.0)
        v.memset(carry[:], 0.0).then_inc(P.sem("s_ph"), 1)
    P.emit("vector", v_pre)
    P.bump("s_ph", 1)

    def xload(g, c, wait_xc, idx):
        if wait_xc is not None:
            g.wait_ge(P.sem("s_xc"), wait_xc)
        g.dma_start(
            xf[c % 2][:].rearrange("p (k m) -> p k m", k=8),
            xh.ap()[:, :, c * W:c * W + WH].transpose([1, 0, 2]),
        ).then_inc(P.sem("s_x"), 16)

    for c in range(min(2, nch)):
        P.emit("sync", lambda g, c=c: xload(g, c, None, c))
        P.bump("s_x", 16)

        def v_cast0(v, c=c):
            v.wait_ge(P.sem("s_x"), 16 * (c + 1))
            v.tensor_copy(xb[c % 2][:], xf[c % 2][:]).then_inc(P.sem("s_xb"), 1)
        P.emit("vector", v_cast0)
        P.bump("s_xb", 1)

    ph_y_hist = []

    def emit_drain():
        def t_drain(t):
            t.matmul(ps_dummy[:, 0:320], w_in[:, 0:128], xb[0][:, 0:320],
                     start=True, stop=True).then_inc(P.sem("s_mm"), 1)
        P.emit("tensor", t_drain)
        P.bump("s_mm", 1)

    for c in range(nch):
        xcur = xb[c % 2]
        gb = gbs[c % 2]
        ob = obs[c % 2]
        for m in range(24):
            bk, pt, freed = pool.take()
            x_need = (c + 1) * 16

            gphy = (ph_y_hist[-1] if ph_y_hist else 0) if (GUARD & 2 and m == 0) else None

            def t_mm(t, c=c, m=m, pt=pt, freed=freed, x_need=x_need, xcur=xcur, gphy=gphy):
                if c == 0 and m == 0:
                    t.wait_ge(P.sem("s_wt"), WT_FULL)
                    t.wait_ge(P.sem("s_cv"), 16)
                    t.wait_ge(P.sem("s_phs"), WBARA)
                if gphy is not None:
                    t.wait_ge(P.sem("s_ph"), gphy)
                if m == 0:
                    t.wait_ge(P.sem("s_xb"), c + 1)
                if freed is not None:
                    t.wait_ge(P.sem(freed[0]), freed[1])
                for k in range(8):
                    ins = t.matmul(
                        pt[:, :WH],
                        w_in[:, k * 3072 + m * 128: k * 3072 + m * 128 + 128],
                        xcur[:, k * WH:(k + 1) * WH],
                        start=(k == 0), stop=(k == 7),
                    )
                ins.then_inc(P.sem("s_mm"), 1)
            P.emit("tensor", t_mm)
            need = P.bump("s_mm", 1)
            if m == 23:
                evs_all = P.val("s_evs") + 1

                def v_xcons(v, evs_all=evs_all):
                    v.wait_ge(P.sem("s_evs"), evs_all)
                    v.engine_nop().then_inc(P.sem("s_xc"), 1)
                P.emit("vector", v_xcons)
                P.bump("s_xc", 1)

            if m < 12:
                gph = P.val("s_ph") if GUARD & 1 else None

                def s_ev(s, m=m, pt=pt, need=need, gph=gph):
                    if gph is not None:
                        s.wait_ge(P.sem("s_ph"), gph)
                    s.wait_ge(P.sem("s_mm"), need + 1)
                    s.activation(xi_t[:, m * WH:(m + 1) * WH], pt[:, :WH], AF.Copy).then_inc(
                        P.sem("s_evs"), 1)
                P.emit("scalar", s_ev)
                pool.release(bk, "s_evs", P.bump("s_evs", 1))
            else:
                prev_ph_y = ph_y_hist[-1] if (m == 12 and ph_y_hist) else None

                def s_ev(s, m=m, pt=pt, need=need, prev_ph_y=prev_ph_y):
                    if prev_ph_y is not None:
                        s.wait_ge(P.sem("s_ph"), prev_ph_y)
                    s.wait_ge(P.sem("s_mm"), need + 1)
                    s.activation(szb[:, (m - 12) * W:(m - 11) * W], pt[:, 1:1 + W],
                                 AF.Silu).then_inc(P.sem("s_evs"), 1)
                P.emit("scalar", s_ev)
                pool.release(bk, "s_evs", P.bump("s_evs", 1))

        emit_drain()

        if c + 2 < nch:
            P.emit("sync", lambda g, c2=c + 2, w=c + 1: xload(g, c2, w, c2))
            P.bump("s_x", 16)
            x_exp = P.val("s_x")

            def v_cast(v, c2=c + 2, x_exp=x_exp):
                v.wait_ge(P.sem("s_x"), x_exp)
                v.tensor_copy(xb[c2 % 2][:], xf[c2 % 2][:]).then_inc(P.sem("s_xb"), 1)
            P.emit("vector", v_cast)
            P.bump("s_xb", 1)

        evs_need = P.val("s_evs")

        gmm = P.val("s_mm") if GUARD & 4 else None

        def v_conv(v, evs_need=evs_need, gmm=gmm):
            if gmm is not None:
                v.wait_ge(P.sem("s_mm"), gmm)
            v.wait_ge(P.sem("s_evs"), evs_need)
            for j in range(12):
                o = j * WH
                a = w1_t[:, j * W:(j + 1) * W]
                bsl = cm_t[:, j * W:(j + 1) * W]
                v.tensor_scalar_mul(a, xi_t[:, o:o + W], cv[:, j * 8:j * 8 + 1])
                v.scalar_tensor_tensor(bsl, xi_t[:, o + 1:o + 1 + W],
                                       cv[:, j * 8 + 1:j * 8 + 2], a, ALU.mult, ALU.add)
                ins = v.scalar_tensor_tensor(a, xi_t[:, o + 2:o + 2 + W],
                                             cv[:, j * 8 + 2:j * 8 + 3], bsl, ALU.mult, ALU.add)
            ins.then_inc(P.sem("s_ph"), 1)
        P.emit("vector", v_conv)
        ph_conv = P.bump("s_ph", 1)

        gmm2 = P.val("s_mm") if GUARD & 8 else None

        def s_conv(s, ph_conv=ph_conv, gmm2=gmm2):
            if gmm2 is not None:
                s.wait_ge(P.sem("s_mm"), gmm2)
            s.wait_ge(P.sem("s_ph"), ph_conv)
            for j in range(12):
                s.activation(xc_t[:, j * W:(j + 1) * W], w1_t[:, j * W:(j + 1) * W],
                             AF.Silu, bias=cv[:, j * 8 + 3:j * 8 + 4])
            s.activation(xcb[:], xc_t[:], AF.Copy).then_inc(P.sem("s_phs"), 1)
        P.emit("scalar", s_conv)
        phs_xc = P.bump("s_phs", 1)

        for m in range(12):
            bk, pt, freed = pool.take()

            def t_dt(t, m=m, pt=pt, freed=freed, phs_xc=phs_xc):
                if m == 0:
                    t.wait_ge(P.sem("s_phs"), phs_xc)
                if freed is not None:
                    t.wait_ge(P.sem(freed[0]), freed[1])
                for k in range(12):
                    ins = t.matmul(
                        pt[:, :W],
                        w_dt[:, k * 1536 + m * 128: k * 1536 + m * 128 + 128],
                        xcb[:, k * W:(k + 1) * W],
                        start=(k == 0), stop=(k == 11),
                    )
                ins.then_inc(P.sem("s_mm"), 1)
            P.emit("tensor", t_dt)
            need = P.bump("s_mm", 1)

            def s_dtev(s, m=m, pt=pt, need=need):
                s.wait_ge(P.sem("s_mm"), need + 1)
                s.tensor_scalar_add(dt_t[:, m * W:(m + 1) * W], pt[:, :W],
                                    cv[:, m * 8 + 4:m * 8 + 5]).then_inc(P.sem("s_evv"), 1)
            P.emit("vector", s_dtev)
            pool.release(bk, "s_evv", P.bump("s_evv", 1))

        emit_drain()
        evs_dt = P.val("s_evv")

        def s_softplus(s, evs_dt=evs_dt):
            s.wait_ge(P.sem("s_evv"), evs_dt)
            s.activation(sp_t[:], dt_t[:], AF.Relu)
            s.activation(dt_t[:], dt_t[:], AF.Abs)
            s.activation(dt_t[:], dt_t[:], AF.Exp, scale=-1.0)
            s.activation(dt_t[:], dt_t[:], AF.Ln, bias=1.0).then_inc(P.sem("s_phs"), 1)
        P.emit("scalar", s_softplus)
        phs_sp = P.bump("s_phs", 1)

        def v_dt(v, phs_sp=phs_sp):
            v.wait_ge(P.sem("s_phs"), phs_sp)
            v.tensor_add(dt_t[:], dt_t[:], sp_t[:]).then_inc(P.sem("s_ph"), 1)
        P.emit("vector", v_dt)
        ph_dt = P.bump("s_ph", 1)

        def s_aexp(s, ph_dt=ph_dt):
            s.wait_ge(P.sem("s_ph"), ph_dt)
            for j in range(12):
                ins = s.activation(ae_t[:, j * W:(j + 1) * W], dt_t[:, j * W:(j + 1) * W],
                                   AF.Exp, scale=cv[:, j * 8 + 5:j * 8 + 6])
            ins.then_inc(P.sem("s_phs"), 1)
        P.emit("scalar", s_aexp)
        phs_ae = P.bump("s_phs", 1)

        if snap_chunk == c:
            def v_snap(v, phs_ae=phs_ae):
                v.wait_ge(P.sem("s_phs"), phs_ae)
                v.tensor_copy(sg_t[:], xc_t[:])
                v.tensor_copy(sa_t[:], ae_t[:])
                ins = v.tensor_copy(ss_t[:], dt_t[:])
                ins.then_inc(P.sem("s_ph"), 1)
            P.emit("vector", v_snap)
            snap_ph = P.bump("s_ph", 1)

            def sy_snap(sy, snap_ph=snap_ph):
                sy.wait_ge(P.sem("s_ph"), snap_ph)
                sy.dma_start(snap_g.ap(), sg_t[:]).then_inc(P.sem("s_st"), 16)
                sy.dma_start(snap_ae.ap(), sa_t[:]).then_inc(P.sem("s_st"), 16)
                sy.dma_start(snap_sz.ap(), ss_t[:]).then_inc(P.sem("s_st"), 16)
            P.emit("sync", sy_snap)
            P.bump("s_st", 48)

        st_prev = P.val("s_st")

        cyw = P.val("s_cy")

        def v_wy(v, phs_ae=phs_ae, st_prev=st_prev, cyw=cyw, gb=gb):
            v.wait_ge(P.sem("s_st"), st_prev)
            v.wait_ge(P.sem("s_phs"), phs_ae)
            v.wait_ge(P.sem("s_cy"), cyw)
            v.tensor_mul(w1_t[:], xc_t[:], dt_t[:])
            v.tensor_mul(dt_t[:], w1_t[:], ae_t[:])
            for j in range(12):
                v.reduce_sum(rs_t[:, j:j + 1], dt_t[:, j * W:(j + 1) * W],
                             axis=mybir.AxisListType.X)
            for j in range(12):
                v.tensor_tensor_scan(
                    cm_t[:, j * W:(j + 1) * W], ones[:, :W], dt_t[:, j * W:(j + 1) * W],
                    carry[:, j:j + 1], ALU.mult, ALU.add)
            v.tensor_mul(w1_t[:], cm_t[:], ae_t[:])
            for j in range(12):
                v.scalar_tensor_tensor(
                    dt_t[:, j * W:(j + 1) * W], xc_t[:, j * W:(j + 1) * W],
                    cv[:, j * 8 + 6:j * 8 + 7], w1_t[:, j * W:(j + 1) * W],
                    ALU.mult, ALU.add)
            v.tensor_mul(yb[:], dt_t[:], szb[:])
            ins = v.tensor_mul(gb[:], ae_t[:], szb[:])
            ins.then_inc(P.sem("s_ph"), 1)
        P.emit("vector", v_wy)
        ph_y = P.bump("s_ph", 1)
        ph_y_hist.append(ph_y)



        def g_carry(g, ph_y=ph_y):
            g.wait_ge(P.sem("s_ph"), ph_y)
            g.tensor_add(carry[:, 0:12], carry[:, 0:12], rs_t[:, 0:12]).then_inc(
                P.sem("s_cy"), 1)
        P.emit("gpsimd", g_carry)
        P.bump("s_cy", 1)

        for m in range(8):
            bk, pt, freed = pool.take()

            def t_wo(t, m=m, pt=pt, freed=freed, ph_y=ph_y):
                if m == 0:
                    t.wait_ge(P.sem("s_ph"), ph_y)
                if freed is not None:
                    t.wait_ge(P.sem(freed[0]), freed[1])
                for k in range(12):
                    ins = t.matmul(
                        pt[:, :W],
                        w_out[:, k * 1024 + m * 128: k * 1024 + m * 128 + 128],
                        yb[:, k * W:(k + 1) * W],
                        start=(k == 0), stop=(k == 11),
                    )
                ins.then_inc(P.sem("s_mm"), 1)
            P.emit("tensor", t_wo)
            need = P.bump("s_mm", 1)

            def v_oev(v, m=m, pt=pt, need=need, ob=ob):
                v.wait_ge(P.sem("s_mm"), need + 1)
                v.tensor_copy(ob[:, m * W:(m + 1) * W], pt[:, :W]).then_inc(P.sem("s_evv"), 1)
            P.emit("vector", v_oev)
            pool.release(bk, "s_evv", P.bump("s_evv", 1))

        emit_drain()
        evv_o = P.val("s_evv")

        def sy_store(sy, c=c, evv_o=evv_o, ph_y=ph_y, gb=gb, ob=ob):
            sy.wait_ge(P.sem("s_evv"), evv_o)
            sy.dma_start(
                out_main.ap()[:, :, c * W:(c + 1) * W].transpose([1, 0, 2]),
                ob[:].rearrange("p (k m) -> p k m", k=8),
            ).then_inc(P.sem("s_st"), 16)
            sy.wait_ge(P.sem("s_ph"), ph_y)
            sy.dma_start(
                gmat.ap()[:, :, c * W:(c + 1) * W].transpose([1, 0, 2]),
                gb[:].rearrange("p (k m) -> p k m", k=12),
            ).then_inc(P.sem("s_st"), 16)
        P.emit("sync", sy_store)
        P.bump("s_st", 32)

    if debug:
        ph_last_d = P.val("s_ph")

        def sy_dbg(sy, ph_last_d=ph_last_d):
            sy.wait_ge(P.sem("s_ph"), ph_last_d)
            sy.dma_start(dbg["d_xi"].ap(), xi_t[:]).then_inc(P.sem("s_st"), 16)
            sy.dma_start(dbg["d_xc"].ap(), xc_t[:]).then_inc(P.sem("s_st"), 16)
            sy.dma_start(dbg["d_sz"].ap(), szb[:]).then_inc(P.sem("s_st"), 16)
            sy.dma_start(dbg["d_dt"].ap(), dt_t[:]).then_inc(P.sem("s_st"), 16)
            sy.dma_start(dbg["d_ae"].ap(), ae_t[:]).then_inc(P.sem("s_st"), 16)
            sy.dma_start(dbg["d_cm"].ap(), cm_t[:]).then_inc(P.sem("s_st"), 16)
        P.emit("sync", sy_dbg)

        def g_dbg(g, ph_last_d=ph_last_d):
            g.wait_ge(P.sem("s_ph"), ph_last_d)
            g.dma_start(dbg["d_yb"].ap(), yb[:]).then_inc(P.sem("s_st"), 16)
        P.emit("gpsimd", g_dbg)
        P.bump("s_st", 7 * 16)

    def t_fin(t):
        t.matmul(ps_dummy[:, 0:2], w_in[:, 0:128], xb[0][:, 0:2],
                 start=True, stop=True).then_inc(P.sem("s_mm"), 1)
    P.emit("tensor", t_fin)
    P.bump("s_mm", 1)

    def s_fin(s):
        s.activation(ones[:, 0:2], ones[:, 0:2], AF.Copy).then_inc(P.sem("s_evs"), 1)
    P.emit("scalar", s_fin)
    P.bump("s_evs", 1)

    def v_fin(v):
        v.tensor_copy(carry[:, 14:15], carry[:, 14:15]).then_inc(P.sem("s_evv"), 1)
    P.emit("vector", v_fin)
    P.bump("s_evv", 1)

    ph_last = P.val("s_ph")
    st_final = P.val("s_st") + 16

    cy_last = P.val("s_cy")

    def sy_tot(sy, ph_last=ph_last, st_final=st_final, cy_last=cy_last):
        sy.wait_ge(P.sem("s_cy"), cy_last)
        sy.wait_ge(P.sem("s_ph"), ph_last)
        sy.dma_start(totals.ap()[:, :], carry[:, 0:12]).then_inc(P.sem("s_st"), 16)
        sy.wait_ge(P.sem("s_st"), st_final)
    P.emit("sync", sy_tot)
    P.bump("s_st", 16)

    return P.build()


def host_prep_mamba_A(inputs):
    x = np.asarray(inputs["L0_attn_x"])
    W_in = np.asarray(inputs["L0_attn_W_in"])
    dt_w = np.asarray(inputs["L0_attn_dt_w"])
    W_out = np.asarray(inputs["L0_attn_W_out"])
    conv_w = np.asarray(inputs["L0_attn_conv_w"])
    conv_b = np.asarray(inputs["L0_attn_conv_b"])
    dt_b = np.asarray(inputs["L0_attn_dt_b"])
    A = np.asarray(inputs["L0_attn_A"])
    Dp = np.asarray(inputs["L0_attn_Dp"])

    winT = np.ascontiguousarray(W_in.T).reshape(8, 128, 3072)
    dtwT = np.ascontiguousarray(dt_w.T).reshape(12, 128, 1536)
    woutT = np.ascontiguousarray(W_out.T).reshape(12, 128, 1024)
    cvec = np.zeros((128, 12, 8), np.float32)
    for idx, arr in enumerate([conv_w[:, 0, 0], conv_w[:, 0, 1], conv_w[:, 0, 2],
                               conv_b, dt_b, A, Dp]):
        cvec[:, :, idx] = arr.reshape(12, 128).T
    in_maps = []
    for core in range(NCORE):
        b, h = core // 2, core % 2
        x_cm = np.ascontiguousarray(x[b].T)
        xhm = np.zeros((1024, TPC + 2), np.float32)
        t0 = h * TPC
        xhm[:, 1:TPC + 1] = x_cm[:, t0:t0 + TPC]
        if t0 > 0:
            xhm[:, 0] = x_cm[:, t0 - 1]
        if t0 + TPC < L:
            xhm[:, TPC + 1] = x_cm[:, t0 + TPC]
        in_maps.append({
            "xh": np.ascontiguousarray(xhm.reshape(8, 128, TPC + 2)),
            "winT": winT, "dtwT": dtwT, "woutT": woutT,
            "cvec": cvec,
        })
    return in_maps


def _col128(a):
    return np.ascontiguousarray(np.asarray(a, np.float32).reshape(-1, 128).T)


def _conv_consts():
    R = 128
    W128 = np.exp(-2j * np.pi * np.outer(np.arange(R), np.arange(R)) / R)
    twn = np.exp(-2j * np.pi / 16384)
    F1c = W128[:64, :]
    f1s = np.zeros((2, 128, 128), np.float32)
    f1s[0, 0:64] = F1c.real
    f1s[0, 64:128] = -F1c.imag
    f1s[1, 0:64] = F1c.imag
    f1s[1, 64:128] = F1c.real
    f1r = np.zeros((2, 64, 128), np.float32)
    f1r[0] = F1c.real
    f1r[1] = F1c.imag
    wsh = np.zeros((3, 128, 192), np.float32)
    WI2 = np.conj(W128[:, :64])
    for a, (w2p, wi2p) in enumerate(((W128.real, WI2.real), (W128.imag, WI2.imag),
                                     (-W128.imag, -WI2.imag))):
        wsh[a, :, 0:128] = w2p
        wsh[a, :, 128:192] = wi2p
    ang = 2.0 * np.pi * np.outer(np.arange(128), np.arange(128)) / 16384.0
    twcm = np.zeros((128, 256), np.float32)
    twcm[:, 0:128] = np.cos(ang)
    twcm[:, 128:256] = -np.sin(ang)
    tw2cm = np.zeros((128, 256), np.float32)
    tw2cm[:, 0:128] = (np.cos(ang) / 16384.0).T.astype(np.float32).T
    tw2cm[:, 0:128] = np.cos(ang).T / 16384.0
    tw2cm[:, 128:256] = np.sin(ang).T / 16384.0
    WI1 = np.conj(W128)
    WI1p = WI1[::-1, :]
    WI1p0 = WI1[(128 - np.arange(128)) % 128, :]
    wi1c = np.zeros((9, 128, 128), np.float32)
    wi1c[0] = WI1.real; wi1c[1] = WI1.imag; wi1c[2] = -WI1.imag
    wi1c[3] = WI1p.real; wi1c[4] = WI1p.imag; wi1c[5] = -WI1p.imag
    wi1c[6] = WI1p0.real; wi1c[7] = WI1p0.imag; wi1c[8] = -WI1p0.imag
    prevc = np.zeros((2, 128, 128), np.float32)
    for j in range(128):
        prevc[0, 127 - j, j] = 1.0
        prevc[1, (128 - j) % 128, j] = 1.0
    t_rep = np.ascontiguousarray(
        np.broadcast_to(np.linspace(0, 1, L, dtype=np.float32)[None, :], (65, L)))
    idn = np.eye(128, dtype=np.float32)
    return dict(f1s=f1s, f1r=f1r, wsh=wsh, twc=twcm, tw2c=tw2cm, wi1c=wi1c,
                prevc=prevc, t_rep=t_rep, idn=idn)


def _ffn_maps(inp, pfx, xins):
    base = {
        'q1': np.ascontiguousarray(inp[pfx + 'fc1_q'].reshape(8, 128, 4096)),
        'q2': np.ascontiguousarray(inp[pfx + 'fc2_q'].reshape(32, 128, 1024)),
        'sc1': np.ascontiguousarray(np.broadcast_to(
            inp[pfx + 'fc1_scales'][:, None, :], (8, 128, 4096))),
        'sc2': np.ascontiguousarray(np.broadcast_to(
            inp[pfx + 'fc2_scales'][:, None, :], (32, 128, 1024))),
        'b1': _col128(inp[pfx + 'fc1_bias']), 'b2': _col128(inp[pfx + 'fc2_bias']),
        'gam': _col128(inp[pfx + 'ln_gamma']), 'bet': _col128(inp[pfx + 'ln_beta']),
    }
    return [dict(base, xin=np.ascontiguousarray(xi)) for xi in xins]


def kernel(**inputs):
    inp = {k: np.asarray(v) for k, v in inputs.items()}
    cores = list(range(NCORE))

    resA = run_bass_kernel_spmd(build_mamba_A(), host_prep_mamba_A(inp), core_ids=cores)

    x = inp['L0_attn_x']
    woutT = np.ascontiguousarray(inp['L0_attn_W_out'].T).reshape(12, 128, 1024)
    mapsB1 = []
    for core in cores:
        b, h = core // 2, core % 2
        x_cm = np.ascontiguousarray(x[b].T[:, h * TPC:(h + 1) * TPC]).reshape(8, 128, TPC)
        rA = resA.results[core]
        tprev = np.zeros((128, 12), np.float32) if h == 0 \
            else np.ascontiguousarray(resA.results[core - 1]['totals'])
        mapsB1.append({'xres': x_cm, 'rhs': rA['gmat'], 'base': rA['out_main'],
                       'tprev': tprev, 'wT': woutT})
    resB1 = run_bass_kernel_spmd(
        build_mm_res_ln(nk=12, scale_rhs=True, has_base=True, xres_bf16=False, chunk=512),
        mapsB1, core_ids=cores)

    resB2 = run_bass_kernel_spmd(
        build_ffn(chunk=256),
        _ffn_maps(inp, 'L0_ffn_', [resB1.results[c]['x1'] for c in cores]),
        core_ids=cores)

    cc = _conv_consts()
    W_in1 = inp['L1_attn_W_in']
    f_w1 = inp['L1_attn_f_w1']; f_b1 = inp['L1_attn_f_b1']
    f_w2 = inp['L1_attn_f_w2']; f_b2 = inp['L1_attn_f_b2']
    x2f_b = []
    for b in range(B):
        x2f_b.append(np.ascontiguousarray(np.concatenate(
            [resB2.results[2 * b]['x2'], resB2.results[2 * b + 1]['x2']], axis=2)))
    mapsC = []
    for core in cores:
        b, cg = core // 2, core % 2
        sel = np.concatenate([np.arange(cg * 512, cg * 512 + 512) + g * 1024
                              for g in range(3)])
        winT1 = np.ascontiguousarray(W_in1.T[:, sel]).reshape(8, 128, 1536)
        selv = np.arange(cg * 512, cg * 512 + 512)
        w2aug = np.zeros((2, 65, 512), np.float32)
        for o in range(2):
            w2aug[o, 0:64] = f_w2[o][selv, :].T
            w2aug[o, 64] = f_b2[o][selv]
        mapsC.append(dict(cc, x2f=x2f_b[b], winT1=winT1,
                          fw1=np.ascontiguousarray(f_w1[:, :, 0].T),
                          fb1=np.ascontiguousarray(f_b1.T), w2aug=w2aug))
    resC = run_bass_kernel_spmd(build_conv(), mapsC, core_ids=cores)

    wT1 = np.ascontiguousarray(inp['L1_attn_W_out'].T).reshape(8, 128, 1024)
    mapsD = []
    for core in cores:
        b, h = core // 2, core % 2
        yk = np.concatenate([resC.results[2 * b]['y_out'],
                             resC.results[2 * b + 1]['y_out']], axis=0)
        mapsD.append({'xres': resB2.results[core]['x2'],
                      'rhs': np.ascontiguousarray(yk[:, :, h * TPC:(h + 1) * TPC]),
                      'wT': wT1})
    resD = run_bass_kernel_spmd(
        build_mm_res_ln(nk=8, scale_rhs=False, has_base=False, xres_bf16=True, chunk=512),
        mapsD, core_ids=cores)

    resE = run_bass_kernel_spmd(
        build_ffn(chunk=256, out_f32=True),
        _ffn_maps(inp, 'L1_ffn_', [resD.results[c]['x1'] for c in cores]),
        core_ids=cores)

    out = np.empty((B, L, D), np.float32)
    for core in cores:
        b, h = core // 2, core % 2
        out[b, h * TPC:(h + 1) * TPC, :] = \
            resE.results[core]['x2'].reshape(D, TPC).T
    return out


def emit_drain_fn(P, ps_dummy, wsb, xsb, wait_ph=None):
    def t_drain(t, wait_ph=wait_ph):
        if wait_ph is not None:
            t.wait_ge(P.sem("s_ph"), wait_ph)
        t.matmul(ps_dummy[:, 0:320], wsb[:, 0:128], xsb[:, 0:320],
                 start=True, stop=True).then_inc(P.sem("s_mm"), 1)
    P.emit("tensor", t_drain)
    P.bump("s_mm", 1)


def build_mm_res_ln(nk, scale_rhs, has_base, xres_bf16, chunk=256):
    nch = TPC // chunk
    W = chunk
    XDT = BF16 if xres_bf16 else F32
    nc = bass.Bass("TRN2", target_bir_lowering=False)
    xres = nc.dram_tensor("xres", [8, 128, TPC], XDT, kind="ExternalInput")
    rhs_d = nc.dram_tensor("rhs", [nk, 128, TPC], BF16, kind="ExternalInput")
    wT = nc.dram_tensor("wT", [nk, 128, 1024], F32, kind="ExternalInput")
    if has_base:
        base_d = nc.dram_tensor("base", [8, 128, TPC], BF16, kind="ExternalInput")
    if scale_rhs:
        tprev = nc.dram_tensor("tprev", [128, nk], F32, kind="ExternalInput")
    x1_d = nc.dram_tensor("x1", [8, 128, TPC], BF16, kind="ExternalOutput")

    w_sb = nc.alloc_sbuf_tensor("w_sb", [128, nk * 1024], BF16)
    xr = [nc.alloc_sbuf_tensor(f"xr{i}", [128, 8 * W], XDT) for i in range(2)]
    rh = [nc.alloc_sbuf_tensor(f"rh{i}", [128, nk * W], BF16) for i in range(2)]
    bs = [nc.alloc_sbuf_tensor(f"bs{i}", [128, 8 * W], BF16) for i in range(2)] if has_base else None
    gp_t = nc.alloc_sbuf_tensor("gp", [128, nk * W], BF16) if scale_rhs else None
    r_t = nc.alloc_sbuf_tensor("r_t", [128, 8 * W], F32)
    sq_t = nc.alloc_sbuf_tensor("sq", [128, 8 * W], F32)
    row_t = nc.alloc_sbuf_tensor("row_t", [1, 4 * W], F32)
    x1b = [nc.alloc_sbuf_tensor(f"x1b{i}", [128, 8 * W], BF16) for i in range(2)]
    onc_t = nc.alloc_sbuf_tensor("onc_t", [128, 2], F32)
    onr_t = nc.alloc_sbuf_tensor("onr_t", [1, 128], F32)
    eps_t = nc.alloc_sbuf_tensor("eps_t", [1, 2], F32)
    if scale_rhs:
        tp_t = nc.alloc_sbuf_tensor("tp_t", [128, nk], F32)

    P = Prog(nc)
    pool = PsumPool(nc, 4 if W <= 256 else 3, W)
    ps_s12 = nc.alloc_psum_tensor("ps_s12", [1, 2 * W], F32)
    ps_s1 = ps_s12[0:1, 0:W]
    ps_s2 = ps_s12[0:1, W:2 * W]
    ps_bc = nc.alloc_psum_tensor("ps_bc", [128, 2 * W], F32)
    ps_dummy = nc.alloc_psum_tensor("ps_dum", [128, 320], F32)
    for sname in ["s_in", "s_wt", "s_mm", "s_ev", "s_ph", "s_phs", "s_st"]:
        P.sem(sname)

    def g_pre(g):
        g.dma_start(w_sb[:].rearrange("p (k m) -> p k m", k=nk),
                    wT.ap().transpose([1, 0, 2])).then_inc(P.sem("s_wt"), 16)
    P.emit("gpsimd", g_pre)
    P.bump("s_wt", 16)

    wbar_t = nc.alloc_sbuf_tensor("wbar_t", [128, 64], BF16)

    def s_wbar(s, nk=nk):
        s.wait_ge(P.sem("s_wt"), 16)
        ins = None
        for k in range(nk):
            ins = s.activation(wbar_t[:], w_sb[:, k * 1024 + 960:k * 1024 + 1024], AF.Copy)
        ins.then_inc(P.sem("s_phs"), 1)
    P.emit("scalar", s_wbar)
    WBARB = P.bump("s_phs", 1)

    def v_pre(v):
        for _i in range(100):
            v.memset(r_t[:, 0:512], 0.0).then_inc(P.sem("s_spin"), 1)
            v.wait_ge(P.sem("s_spin"), _i + 1)
        v.memset(onc_t[:], 1.0)
        v.memset(eps_t[:], 1e-5)
        ins = v.memset(onr_t[:], 1.0)
        ins.then_inc(P.sem("s_ph"), 1)
    P.emit("vector", v_pre)
    P.bump("s_ph", 1)

    def load(sy, c):
        sl = slice(c * W, (c + 1) * W)
        sy.dma_start(xr[c % 2][:].rearrange("p (k m) -> p k m", k=8),
                     xres.ap()[:, :, sl].transpose([1, 0, 2])).then_inc(P.sem("s_in"), 16)
        sy.dma_start(rh[c % 2][:].rearrange("p (k m) -> p k m", k=nk),
                     rhs_d.ap()[:, :, sl].transpose([1, 0, 2])).then_inc(P.sem("s_in"), 16)
        if has_base:
            sy.dma_start(bs[c % 2][:].rearrange("p (k m) -> p k m", k=8),
                         base_d.ap()[:, :, sl].transpose([1, 0, 2])).then_inc(P.sem("s_in"), 16)
    NLD = 3 if has_base else 2

    if scale_rhs:
        def sy_tp(sy):
            sy.dma_start(tp_t[:], tprev.ap()).then_inc(P.sem("s_in"), 16)
        P.emit("sync", sy_tp)
        P.bump("s_in", 16)
    IN0 = P.val("s_in")

    for c in range(min(2, nch)):
        P.emit("sync", lambda sy, c=c: load(sy, c))
        P.bump("s_in", 16 * NLD)

    for c in range(nch):
        in_need = IN0 + 16 * NLD * (c + 1)
        ph_gp = None
        if scale_rhs:
            def v_gp(v, c=c, in_need=in_need):
                v.wait_ge(P.sem("s_in"), in_need)
                for j in range(nk):
                    ins = v.tensor_scalar_mul(gp_t[:, j * W:(j + 1) * W],
                                              rh[c % 2][:, j * W:(j + 1) * W], tp_t[:, j:j + 1])
                ins.then_inc(P.sem("s_ph"), 1)
            P.emit("vector", v_gp)
            ph_gp = P.bump("s_ph", 1)

        for m in range(8):
            bk, pt, freed = pool.take()

            def t_mm(t, c=c, m=m, pt=pt, freed=freed, ph_gp=ph_gp, in_need=in_need):
                if m == 0:
                    t.wait_ge(P.sem("s_wt"), 16)
                    if c == 0:
                        t.wait_ge(P.sem("s_phs"), WBARB)
                        t.wait_ge(P.sem("s_ph"), 1)
                    if ph_gp is not None:
                        t.wait_ge(P.sem("s_ph"), ph_gp)
                    t.wait_ge(P.sem("s_in"), in_need)
                if freed is not None:
                    t.wait_ge(P.sem(freed[0]), freed[1])
                src = gp_t if scale_rhs else rh[c % 2]
                for k in range(nk):
                    ins = t.matmul(
                        pt[:, :W],
                        w_sb[:, k * 1024 + m * 128:k * 1024 + m * 128 + 128],
                        src[:, k * W:(k + 1) * W],
                        start=(k == 0), stop=(k == nk - 1),
                    )
                ins.then_inc(P.sem("s_mm"), 1)
            P.emit("tensor", t_mm)
            need = P.bump("s_mm", 1)

            def v_ev(v, c=c, m=m, pt=pt, need=need):
                v.wait_ge(P.sem("s_mm"), need + 1)
                sl = slice(m * W, (m + 1) * W)
                if has_base:
                    v.tensor_add(r_t[:, sl], pt[:, :W], bs[c % 2][:, sl])
                    ins = v.tensor_add(r_t[:, sl], r_t[:, sl], xr[c % 2][:, sl])
                else:
                    ins = v.tensor_add(r_t[:, sl], pt[:, :W], xr[c % 2][:, sl])
                ins.then_inc(P.sem("s_ev"), 1)
            P.emit("vector", v_ev)
            pool.release(bk, "s_ev", P.bump("s_ev", 1))
        emit_drain_fn(P, ps_dummy, w_sb, rh[0])

        ev_need = P.val("s_ev")

        def s_sq(s, ev_need=ev_need):
            s.wait_ge(P.sem("s_ev"), ev_need)
            ins = s.activation(sq_t[:], r_t[:], AF.Square)
            ins.then_inc(P.sem("s_phs"), 1)
        P.emit("scalar", s_sq)
        phs_sq = P.bump("s_phs", 1)

        def t_stat(t, phs_sq=phs_sq):
            t.wait_ge(P.sem("s_phs"), phs_sq)
            for k in range(8):
                t.matmul(ps_s1, onc_t[:, 0:1], r_t[:, k * W:(k + 1) * W],
                         start=(k == 0), stop=(k == 7))
            for k in range(8):
                ins = t.matmul(ps_s2, onc_t[:, 1:2], sq_t[:, k * W:(k + 1) * W],
                               start=(k == 0), stop=(k == 7))
            ins.then_inc(P.sem("s_mm"), 1)
        P.emit("tensor", t_stat)
        stat_need = P.bump("s_mm", 1)
        emit_drain_fn(P, ps_dummy, w_sb, rh[0])

        def v_stat(v, stat_need=stat_need):
            v.wait_ge(P.sem("s_mm"), stat_need + 1)
            v.tensor_scalar_mul(row_t[0:1, 2 * W:3 * W], ps_s1, 1.0 / 1024.0)
            ins = v.tensor_scalar_mul(row_t[0:1, 3 * W:4 * W], ps_s2, 1.0 / 1024.0)
            ins.then_inc(P.sem("s_ph"), 1)
        P.emit("vector", v_stat)
        ph_cp = P.bump("s_ph", 1)

        def s_stat(s, ph_cp=ph_cp):
            s.wait_ge(P.sem("s_ph"), ph_cp)
            ins = s.activation(row_t[0:1, 0:W], row_t[0:1, 2 * W:3 * W], AF.Square)
            ins.then_inc(P.sem("s_phs"), 1)
        P.emit("scalar", s_stat)
        phs_mu2 = P.bump("s_phs", 1)

        def v_var(v, phs_mu2=phs_mu2):
            v.wait_ge(P.sem("s_phs"), phs_mu2)
            ins = v.tensor_sub(row_t[0:1, W:2 * W], row_t[0:1, 3 * W:4 * W], row_t[0:1, 0:W])
            ins.then_inc(P.sem("s_ph"), 1)
        P.emit("vector", v_var)
        ph_var = P.bump("s_ph", 1)

        def s_rstd(s, ph_var=ph_var):
            s.wait_ge(P.sem("s_ph"), ph_var)
            s.activation(row_t[0:1, W:2 * W], row_t[0:1, W:2 * W], AF.Ln, bias=eps_t[0:1, 0:1])
            ins = s.activation(row_t[0:1, W:2 * W], row_t[0:1, W:2 * W], AF.Exp, scale=-0.5)
            ins.then_inc(P.sem("s_phs"), 1)
        P.emit("scalar", s_rstd)
        phs_rs = P.bump("s_phs", 1)

        def v_mmr(v, phs_rs=phs_rs):
            v.wait_ge(P.sem("s_phs"), phs_rs)
            v.tensor_mul(row_t[0:1, 0:W], row_t[0:1, 2 * W:3 * W], row_t[0:1, W:2 * W])
            ins = v.tensor_scalar_mul(row_t[0:1, 0:W], row_t[0:1, 0:W], -1.0)
            ins.then_inc(P.sem("s_ph"), 1)
        P.emit("vector", v_mmr)
        ph_mm = P.bump("s_ph", 1)

        def t_bc(t, ph_mm=ph_mm):
            t.wait_ge(P.sem("s_ph"), ph_mm)
            if W > 256:
                t.matmul(ps_bc[:, 0:W], onr_t[0:1, 0:128],
                         row_t[0:1, 0:W], start=True, stop=True)
                ins = t.matmul(ps_bc[:, W:2 * W], onr_t[0:1, 0:128],
                               row_t[0:1, W:2 * W], start=True, stop=True)
            else:
                ins = t.matmul(ps_bc[:, 0:2 * W], onr_t[0:1, 0:128],
                               row_t[0:1, 0:2 * W], start=True, stop=True)
            ins.then_inc(P.sem("s_mm"), 1)
        P.emit("tensor", t_bc)
        bc_need = P.bump("s_mm", 1)
        emit_drain_fn(P, ps_dummy, w_sb, rh[0])

        st_guard = 16 * max(0, c - 1)

        def v_app(v, c=c, bc_need=bc_need, st_guard=st_guard):
            v.wait_ge(P.sem("s_mm"), bc_need + 1)
            v.wait_ge(P.sem("s_st"), st_guard)
            for k in range(8):
                sl = slice(k * W, (k + 1) * W)
                v.tensor_mul(r_t[:, sl], r_t[:, sl], ps_bc[:, W:2 * W])
                ins = v.tensor_add(x1b[c % 2][:, sl], r_t[:, sl], ps_bc[:, 0:W])
            ins.then_inc(P.sem("s_ph"), 1)
        P.emit("vector", v_app)
        ph_x1 = P.bump("s_ph", 1)

        def sy_st(sy, c=c, ph_x1=ph_x1):
            sy.wait_ge(P.sem("s_ph"), ph_x1)
            sy.dma_start(x1_d.ap()[:, :, c * W:(c + 1) * W].transpose([1, 0, 2]),
                         x1b[c % 2][:].rearrange("p (k m) -> p k m", k=8)).then_inc(P.sem("s_st"), 16)
        P.emit("sync", sy_st)
        P.bump("s_st", 16)

        if c + 2 < nch:
            ev_done = P.val("s_ev")

            def sy_ld(sy, c2=c + 2, ev_done=ev_done):
                sy.wait_ge(P.sem("s_ev"), ev_done)
                load(sy, c2)
            P.emit("sync", sy_ld)
            P.bump("s_in", 16 * NLD)

    st_final = P.val("s_st")

    def sy_fin(sy, st_final=st_final):
        sy.wait_ge(P.sem("s_st"), st_final)
    P.emit("sync", sy_fin)
    return P.build()


def build_ffn(chunk=128, tokens=TPC, cut=None, dq_g=8, dq_g2=32, out_f32=False):
    nch = tokens // chunk
    W = chunk
    nc = bass.Bass("TRN2", target_bir_lowering=False)
    xin = nc.dram_tensor("xin", [8, 128, TPC], BF16, kind="ExternalInput")
    q1 = nc.dram_tensor("q1", [8, 128, 4096], I32, kind="ExternalInput")
    q2 = nc.dram_tensor("q2", [32, 128, 1024], I32, kind="ExternalInput")
    sc1 = nc.dram_tensor("sc1", [8, 128, 4096], F32, kind="ExternalInput")
    sc2 = nc.dram_tensor("sc2", [32, 128, 1024], F32, kind="ExternalInput")
    b1 = nc.dram_tensor("b1", [128, 32], F32, kind="ExternalInput")
    b2 = nc.dram_tensor("b2", [128, 8], F32, kind="ExternalInput")
    gam = nc.dram_tensor("gam", [128, 8], F32, kind="ExternalInput")
    bet = nc.dram_tensor("bet", [128, 8], F32, kind="ExternalInput")
    x2_d = nc.dram_tensor("x2", [8, 128, TPC], F32 if out_f32 else BF16, kind="ExternalOutput")

    w1 = nc.alloc_sbuf_tensor("w1", [128, 8 * 4096], BF16)
    w2 = nc.alloc_sbuf_tensor("w2", [128, 32 * 1024], BF16)
    qt = [nc.alloc_sbuf_tensor(f"qt{i}", [128, 1024], I32) for i in range(3)]
    scb = [nc.alloc_sbuf_tensor(f"scb{i}", [128, 512], F32) for i in range(4)]
    xb = [nc.alloc_sbuf_tensor(f"xb{i}", [128, 8 * W], BF16) for i in range(2)]
    hb = nc.alloc_sbuf_tensor("hb", [128, 8 * W], BF16)
    yb = nc.alloc_sbuf_tensor("yb", [128, 32 * W], BF16)
    r_t = nc.alloc_sbuf_tensor("r_t", [128, 8 * W], F32)
    sq_t = nc.alloc_sbuf_tensor("sq", [128, 8 * W], F32)
    dq_t = nc.alloc_sbuf_tensor("dq", [128, 512], F32)
    row_t = nc.alloc_sbuf_tensor("row_t", [1, 4 * W], F32)
    if out_f32 and W >= 256:
        _x2b1 = nc.alloc_sbuf_tensor("x2b0", [128, 8 * W], F32)
        x2b = [_x2b1, _x2b1]
        x2b_single = True
    else:
        x2b = [nc.alloc_sbuf_tensor(f"x2b{i}", [128, 8 * W], F32 if out_f32 else BF16)
               for i in range(2)]
        x2b_single = False
    onc_t = nc.alloc_sbuf_tensor("onc_t", [128, 2], F32)
    oncb_t = nc.alloc_sbuf_tensor("oncb_t", [128, 2], BF16)
    onr_t = nc.alloc_sbuf_tensor("onr_t", [1, 128], F32)
    eps_t = nc.alloc_sbuf_tensor("eps_t", [1, 2], F32)
    dr_w = nc.alloc_sbuf_tensor("dr_w", [128, 128], BF16)
    dr_x = nc.alloc_sbuf_tensor("dr_x", [128, 320], BF16)
    cvec = nc.alloc_sbuf_tensor("cvec", [128, 64], F32)

    P = Prog(nc)
    pool = PsumPool(nc, 4, max(W, 256))
    ps_s1 = nc.alloc_psum_tensor("ps_s1", [1, W], F32)
    ps_s2 = nc.alloc_psum_tensor("ps_s2", [1, W], F32)
    ps_bc = nc.alloc_psum_tensor("ps_bc", [128, 512], F32)
    ps_dummy = nc.alloc_psum_tensor("ps_dum", [128, 320], F32)
    for sname in ["s_in", "s_q", "s_wt", "s_mm", "s_ev", "s_evs", "s_ph", "s_phs", "s_st", "s_dq"]:
        P.sem(sname)

    def sy_pre(sy):
        sy.dma_start(cvec[:, 0:32], b1.ap()).then_inc(P.sem("s_wt"), 16)
        sy.dma_start(cvec[:, 32:40], b2.ap()).then_inc(P.sem("s_wt"), 16)
        sy.dma_start(cvec[:, 40:48], gam.ap()).then_inc(P.sem("s_wt"), 16)
        sy.dma_start(cvec[:, 48:56], bet.ap()).then_inc(P.sem("s_wt"), 16)
    P.emit("sync", sy_pre)
    WT0 = P.bump("s_wt", 64)

    def v_pre(v):
        for _i in range(100):
            v.memset(r_t[:, 0:512], 0.0).then_inc(P.sem("s_spin"), 1)
            v.wait_ge(P.sem("s_spin"), _i + 1)
        v.memset(onc_t[:], 1.0)
        v.memset(oncb_t[:], 1.0)
        v.memset(eps_t[:], 1e-5)
        v.memset(onr_t[:], 1.0).then_inc(P.sem("s_ph"), 1)
        v.memset(dr_w[:], 1.0).then_inc(P.sem("s_ph"), 1)
        v.memset(dr_x[:], 1.0).then_inc(P.sem("s_ph"), 1)
    P.emit("vector", v_pre)
    P.bump("s_ph", 3)

    def g_nop(g):
        g.engine_nop()
    P.emit("gpsimd", g_nop)

    def s_half2(s):
        s.wait_ge(P.sem("s_wt"), WT0)
        ins = s.mul(cvec[:, 56:64], cvec[:, 32:40], 0.5)
        ins.then_inc(P.sem("s_phs"), 1)
    P.emit("scalar", s_half2)
    P.bump("s_phs", 1)

    if cut == 'consts':
        cdbg = nc.dram_tensor("cdbg", [128, 64], F32, kind="ExternalOutput")

        def sy_cutc(sy):
            sy.wait_ge(P.sem("s_phs"), 1)
            sy.dma_start(cdbg.ap(), cvec[:]).then_inc(P.sem("s_st"), 16)
            sy.wait_ge(P.sem("s_st"), 16)
        P.emit("sync", sy_cutc)
        P.bump("s_st", 16)
        return P.build()

    slabs = []
    for qsrc, scsrc, wdst, ntile, ncol in ((q1, sc1, w1, dq_g, 4096), (q2, sc2, w2, dq_g2, 1024)):
        for g in range(ntile):
            for slab in range(ncol // 1024):
                slabs.append((qsrc, scsrc, wdst, g, slab, ncol))
    NSLAB = len(slabs)
    SQ_TOTAL = NSLAB * 48 + 32
    sidx = 0
    for j, (qsrc, scsrc, wdst, g, slab, ncol) in enumerate(slabs):
        qpar = j % 3
        dqq = max(0, 2 * j - 4)

        def sy_q(sy, qsrc=qsrc, g=g, slab=slab, dqq=dqq, qpar=qpar):
            sy.wait_ge(P.sem("s_dq"), dqq)
            sy.dma_start(qt[qpar][:], qsrc.ap()[g, :, slab * 1024:(slab + 1) * 1024]
                         ).then_inc(P.sem("s_q"), 16)
        P.emit("sync", sy_q)
        P.bump("s_q", 16)

        for ss in range(2):
            col0 = slab * 1024 + ss * 512
            spar = (2 * j + ss) % 4
            dqs = max(0, 2 * j + ss - 3)

            def sy_sc(sy, scsrc=scsrc, g=g, col0=col0, dqs=dqs, spar=spar):
                sy.wait_ge(P.sem("s_dq"), dqs)
                sy.dma_start(scb[spar][:], scsrc.ap()[g, :, col0:col0 + 512]
                             ).then_inc(P.sem("s_q"), 16)
            P.emit("sync", sy_sc)
            P.bump("s_q", 16)

            need = 48 * j + 32 + 16 * ss
            need_ahead = min(SQ_TOTAL, need + 32)

            def v_dq(v, g=g, ss=ss, col0=col0, need_ahead=need_ahead,
                     wdst=wdst, ncol=ncol, qpar=qpar, spar=spar):
                v.wait_ge(P.sem("s_q"), need_ahead)
                v.tensor_scalar_add(dq_t[:], qt[qpar][:, (col0 % 1024):(col0 % 1024) + 512], -128.0)
                ins = v.tensor_mul(wdst[:, g * ncol + col0:g * ncol + col0 + 512],
                                   dq_t[:], scb[spar][:])
                ins.then_inc(P.sem("s_dq"), 1)
            P.emit("vector", v_dq)
            P.bump("s_dq", 1)

    for d in range(2):
        qpar = (NSLAB + d) % 3
        dqq = 2 * (NSLAB + d) - 4

        def sy_dummy(sy, qpar=qpar, dqq=dqq):
            sy.wait_ge(P.sem("s_dq"), min(dqq, 2 * NSLAB))
            sy.dma_start(qt[qpar][:], q1.ap()[0, :, 0:1024]).then_inc(P.sem("s_q"), 16)
        P.emit("sync", sy_dummy)
        P.bump("s_q", 16)

    DQ_ALL = P.val("s_dq")

    def s_wbar(s, DQ_ALL=DQ_ALL):
        s.wait_ge(P.sem("s_dq"), DQ_ALL)
        for g in range(dq_g):
            s.activation(hb[:, 0:64], w1[:, g * 4096 + 4032:g * 4096 + 4096], AF.Copy)
        for g in range(dq_g2):
            s.activation(hb[:, 0:64], w2[:, g * 1024 + 960:g * 1024 + 1024], AF.Copy)
        ins = s.activation(hb[:, 0:64], hb[:, 0:64], AF.Copy)
        ins.then_inc(P.sem("s_phs"), 1)
    P.emit("scalar", s_wbar)
    WBAR = P.bump("s_phs", 1)

    if cut == 'dq':
        w1dbg = nc.dram_tensor("w1dbg", [128, 4096], BF16, kind="ExternalOutput")

        def sy_cut(sy):
            sy.wait_ge(P.sem("s_dq"), DQ_ALL)
            sy.dma_start(w1dbg.ap(), w1[:, 0:4096]).then_inc(P.sem("s_st"), 16)
            sy.wait_ge(P.sem("s_st"), 16)
        P.emit("sync", sy_cut)
        P.bump("s_st", 16)
        return P.build()

    def load(sy, c):
        sy.dma_start(xb[c % 2][:].rearrange("p (k m) -> p k m", k=8),
                     xin.ap()[:, :, c * W:(c + 1) * W].transpose([1, 0, 2])).then_inc(P.sem("s_in"), 16)
    for c in range(min(2, nch)):
        P.emit("sync", lambda sy, c=c: load(sy, c))
        P.bump("s_in", 16)

    for c in range(nch):
        in_need = 16 * (c + 1)
        xc = xb[c % 2]
        def s_sq(s, c=c, in_need=in_need, xc=xc):
            s.wait_ge(P.sem("s_in"), in_need)
            if c == 0:
                s.wait_ge(P.sem("s_ph"), 3)
            ins = s.activation(sq_t[:], xc[:], AF.Square)
            ins.then_inc(P.sem("s_phs"), 1)
        P.emit("scalar", s_sq)
        phs_sq = P.bump("s_phs", 1)

        def t_stat(t, c=c, phs_sq=phs_sq, xc=xc):
            t.wait_ge(P.sem("s_phs"), phs_sq)
            for k in range(8):
                t.matmul(ps_s1[0:1, :W], oncb_t[:, 0:1], xc[:, k * W:(k + 1) * W],
                         start=(k == 0), stop=(k == 7))
            for k in range(8):
                ins = t.matmul(ps_s2[0:1, :W], onc_t[:, 1:2], sq_t[:, k * W:(k + 1) * W],
                               start=(k == 0), stop=(k == 7))
            ins.then_inc(P.sem("s_mm"), 1)
        P.emit("tensor", t_stat)
        stat_need = P.bump("s_mm", 1)
        emit_drain_fn(P, ps_dummy, dr_w, dr_x, wait_ph=3)

        def v_stat(v, stat_need=stat_need):
            v.wait_ge(P.sem("s_mm"), stat_need + 1)
            v.tensor_scalar_mul(row_t[0:1, 2 * W:3 * W], ps_s1[0:1, :W], 1.0 / 1024.0)
            ins = v.tensor_scalar_mul(row_t[0:1, 3 * W:4 * W], ps_s2[0:1, :W], 1.0 / 1024.0)
            ins.then_inc(P.sem("s_ph"), 1)
        P.emit("vector", v_stat)
        ph_cp = P.bump("s_ph", 1)

        def s_stat(s, ph_cp=ph_cp):
            s.wait_ge(P.sem("s_ph"), ph_cp)
            ins = s.activation(row_t[0:1, 0:W], row_t[0:1, 2 * W:3 * W], AF.Square)
            ins.then_inc(P.sem("s_phs"), 1)
        P.emit("scalar", s_stat)
        phs_mu2 = P.bump("s_phs", 1)

        def v_var(v, phs_mu2=phs_mu2):
            v.wait_ge(P.sem("s_phs"), phs_mu2)
            ins = v.tensor_sub(row_t[0:1, W:2 * W], row_t[0:1, 3 * W:4 * W], row_t[0:1, 0:W])
            ins.then_inc(P.sem("s_ph"), 1)
        P.emit("vector", v_var)
        ph_var = P.bump("s_ph", 1)

        def s_rstd(s, ph_var=ph_var):
            s.wait_ge(P.sem("s_ph"), ph_var)
            s.activation(row_t[0:1, W:2 * W], row_t[0:1, W:2 * W], AF.Ln, bias=eps_t[0:1, 0:1])
            ins = s.activation(row_t[0:1, W:2 * W], row_t[0:1, W:2 * W], AF.Exp, scale=-0.5)
            ins.then_inc(P.sem("s_phs"), 1)
        P.emit("scalar", s_rstd)
        phs_rs = P.bump("s_phs", 1)

        def v_mmr(v, phs_rs=phs_rs):
            v.wait_ge(P.sem("s_phs"), phs_rs)
            v.tensor_mul(row_t[0:1, 0:W], row_t[0:1, 2 * W:3 * W], row_t[0:1, W:2 * W])
            ins = v.tensor_scalar_mul(row_t[0:1, 0:W], row_t[0:1, 0:W], -1.0)
            ins.then_inc(P.sem("s_ph"), 1)
        P.emit("vector", v_mmr)
        ph_mm = P.bump("s_ph", 1)

        def t_bc(t, ph_mm=ph_mm):
            t.wait_ge(P.sem("s_ph"), ph_mm)
            ins = t.matmul(ps_bc[:, 0:2 * W], onr_t[0:1, 0:128],
                           row_t[0:1, 0:2 * W], start=True, stop=True)
            ins.then_inc(P.sem("s_mm"), 1)
        P.emit("tensor", t_bc)
        bc_need = P.bump("s_mm", 1)
        emit_drain_fn(P, ps_dummy, dr_w, dr_x, wait_ph=3)

        def v_h(v, c=c, bc_need=bc_need, xc=xc):
            v.wait_ge(P.sem("s_mm"), bc_need + 1)
            v.wait_ge(P.sem("s_phs"), 1)
            for k in range(8):
                sl = slice(k * W, (k + 1) * W)
                v.tensor_mul(r_t[:, sl], xc[:, sl], ps_bc[:, W:2 * W])
                v.tensor_add(r_t[:, sl], r_t[:, sl], ps_bc[:, 0:W])
                ins = v.tensor_scalar(hb[:, sl], r_t[:, sl], cvec[:, 40 + k:41 + k],
                                      cvec[:, 48 + k:49 + k], ALU.mult, ALU.add)
            ins.then_inc(P.sem("s_ph"), 1)
        P.emit("vector", v_h)
        ph_h = P.bump("s_ph", 1)

        for m in range(32):
            bk, pt, freed = pool.take()

            def t_f1(t, m=m, pt=pt, freed=freed, ph_h=ph_h, c=c):
                if m == 0:
                    t.wait_ge(P.sem("s_ph"), ph_h)
                    if c == 0:
                        t.wait_ge(P.sem("s_dq"), DQ_ALL)
                        t.wait_ge(P.sem("s_phs"), WBAR)
                if freed is not None:
                    t.wait_ge(P.sem(freed[0]), freed[1])
                for k in range(8):
                    ins = t.matmul(pt[:, :W],
                                   w1[:, k * 4096 + m * 128:k * 4096 + m * 128 + 128],
                                   hb[:, k * W:(k + 1) * W],
                                   start=(k == 0), stop=(k == 7))
                ins.then_inc(P.sem("s_mm"), 1)
            P.emit("tensor", t_f1)
            need = P.bump("s_mm", 1)

            def s_y(s, m=m, pt=pt, need=need):
                s.wait_ge(P.sem("s_mm"), need + 1)
                ins = s.activation(yb[:, m * W:(m + 1) * W], pt[:, :W], AF.Silu,
                                   bias=cvec[:, m % 32:m % 32 + 1] if False else cvec[:, m:m + 1])
                ins.then_inc(P.sem("s_evs"), 1)
            P.emit("scalar", s_y)
            pool.release(bk, "s_evs", P.bump("s_evs", 1))
        emit_drain_fn(P, ps_dummy, dr_w, dr_x, wait_ph=3)

        for m in range(8):
            bk, pt, freed = pool.take()
            evs_need = P.val("s_evs")

            def t_f2(t, m=m, pt=pt, freed=freed, evs_need=evs_need):
                if m == 0:
                    t.wait_ge(P.sem("s_evs"), evs_need)
                if freed is not None:
                    t.wait_ge(P.sem(freed[0]), freed[1])
                for k in range(32):
                    ins = t.matmul(pt[:, :W],
                                   w2[:, k * 1024 + m * 128:k * 1024 + m * 128 + 128],
                                   yb[:, k * W:(k + 1) * W],
                                   start=(k == 0), stop=(k == 31))
                ins.then_inc(P.sem("s_mm"), 1)
            P.emit("tensor", t_f2)
            need = P.bump("s_mm", 1)

            def v_z(v, m=m, pt=pt, need=need, c=c, xc=xc):
                v.wait_ge(P.sem("s_mm"), need + 1)
                sl = slice(m * W, (m + 1) * W)
                v.scalar_tensor_tensor(r_t[:, sl], pt[:, :W], 0.5, xc[:, sl],
                                       ALU.mult, ALU.add)
                ins = v.tensor_scalar_add(r_t[:, sl], r_t[:, sl], cvec[:, 56 + m:57 + m])
                ins.then_inc(P.sem("s_ev"), 1)
            P.emit("vector", v_z)
            pool.release(bk, "s_ev", P.bump("s_ev", 1))
        emit_drain_fn(P, ps_dummy, dr_w, dr_x, wait_ph=3)

        ev_need = P.val("s_ev")

        def s_sq2(s, ev_need=ev_need):
            s.wait_ge(P.sem("s_ev"), ev_need)
            ins = s.activation(sq_t[:], r_t[:], AF.Square)
            ins.then_inc(P.sem("s_phs"), 1)
        P.emit("scalar", s_sq2)
        phs_sq2 = P.bump("s_phs", 1)

        def t_stat2(t, phs_sq2=phs_sq2):
            t.wait_ge(P.sem("s_phs"), phs_sq2)
            for k in range(8):
                t.matmul(ps_s1[0:1, :W], onc_t[:, 0:1], r_t[:, k * W:(k + 1) * W],
                         start=(k == 0), stop=(k == 7))
            for k in range(8):
                ins = t.matmul(ps_s2[0:1, :W], onc_t[:, 1:2], sq_t[:, k * W:(k + 1) * W],
                               start=(k == 0), stop=(k == 7))
            ins.then_inc(P.sem("s_mm"), 1)
        P.emit("tensor", t_stat2)
        stat2 = P.bump("s_mm", 1)
        emit_drain_fn(P, ps_dummy, dr_w, dr_x, wait_ph=3)

        def v_stat2(v, stat2=stat2):
            v.wait_ge(P.sem("s_mm"), stat2 + 1)
            v.tensor_scalar_mul(row_t[0:1, 2 * W:3 * W], ps_s1[0:1, :W], 1.0 / 1024.0)
            ins = v.tensor_scalar_mul(row_t[0:1, 3 * W:4 * W], ps_s2[0:1, :W], 1.0 / 1024.0)
            ins.then_inc(P.sem("s_ph"), 1)
        P.emit("vector", v_stat2)
        ph_cp2 = P.bump("s_ph", 1)

        def s_stat2(s, ph_cp2=ph_cp2):
            s.wait_ge(P.sem("s_ph"), ph_cp2)
            ins = s.activation(row_t[0:1, 0:W], row_t[0:1, 2 * W:3 * W], AF.Square)
            ins.then_inc(P.sem("s_phs"), 1)
        P.emit("scalar", s_stat2)
        phs_mu22 = P.bump("s_phs", 1)

        def v_var2(v, phs_mu22=phs_mu22):
            v.wait_ge(P.sem("s_phs"), phs_mu22)
            ins = v.tensor_sub(row_t[0:1, W:2 * W], row_t[0:1, 3 * W:4 * W], row_t[0:1, 0:W])
            ins.then_inc(P.sem("s_ph"), 1)
        P.emit("vector", v_var2)
        ph_var2 = P.bump("s_ph", 1)

        def s_rstd2(s, ph_var2=ph_var2):
            s.wait_ge(P.sem("s_ph"), ph_var2)
            s.activation(row_t[0:1, W:2 * W], row_t[0:1, W:2 * W], AF.Ln, bias=eps_t[0:1, 0:1])
            ins = s.activation(row_t[0:1, W:2 * W], row_t[0:1, W:2 * W], AF.Exp, scale=-0.5)
            ins.then_inc(P.sem("s_phs"), 1)
        P.emit("scalar", s_rstd2)
        phs_rs2 = P.bump("s_phs", 1)

        def v_mmr2(v, phs_rs2=phs_rs2):
            v.wait_ge(P.sem("s_phs"), phs_rs2)
            v.tensor_mul(row_t[0:1, 0:W], row_t[0:1, 2 * W:3 * W], row_t[0:1, W:2 * W])
            ins = v.tensor_scalar_mul(row_t[0:1, 0:W], row_t[0:1, 0:W], -1.0)
            ins.then_inc(P.sem("s_ph"), 1)
        P.emit("vector", v_mmr2)
        ph_mm2 = P.bump("s_ph", 1)

        def t_bc2(t, ph_mm2=ph_mm2):
            t.wait_ge(P.sem("s_ph"), ph_mm2)
            ins = t.matmul(ps_bc[:, 0:2 * W], onr_t[0:1, 0:128],
                           row_t[0:1, 0:2 * W], start=True, stop=True)
            ins.then_inc(P.sem("s_mm"), 1)
        P.emit("tensor", t_bc2)
        bc2 = P.bump("s_mm", 1)
        emit_drain_fn(P, ps_dummy, dr_w, dr_x, wait_ph=3)

        st_guard = 16 * c if x2b_single else 16 * max(0, c - 1)

        def v_app2(v, c=c, bc2=bc2, st_guard=st_guard):
            v.wait_ge(P.sem("s_mm"), bc2 + 1)
            v.wait_ge(P.sem("s_st"), st_guard)
            for k in range(8):
                sl = slice(k * W, (k + 1) * W)
                v.tensor_mul(r_t[:, sl], r_t[:, sl], ps_bc[:, W:2 * W])
                ins = v.tensor_add(x2b[c % 2][:, sl], r_t[:, sl], ps_bc[:, 0:W])
            ins.then_inc(P.sem("s_ph"), 1)
        P.emit("vector", v_app2)
        ph_x2 = P.bump("s_ph", 1)

        def sy_st(sy, c=c, ph_x2=ph_x2):
            sy.wait_ge(P.sem("s_ph"), ph_x2)
            sy.dma_start(x2_d.ap()[:, :, c * W:(c + 1) * W].transpose([1, 0, 2]),
                         x2b[c % 2][:].rearrange("p (k m) -> p k m", k=8)).then_inc(P.sem("s_st"), 16)
        P.emit("sync", sy_st)
        P.bump("s_st", 16)

        if c + 2 < nch:
            ph_used = ph_h
            def sy_ld(sy, c2=c + 2, ph_x2=ph_x2):
                sy.wait_ge(P.sem("s_ph"), ph_x2)
                load(sy, c2)
            P.emit("sync", sy_ld)
            P.bump("s_in", 16)

    if cut == 'dbgw':
        w1dbg = nc.dram_tensor("w1dbg", [128, 32768], BF16, kind="ExternalOutput")
        w2dbg = nc.dram_tensor("w2dbg", [128, 32768], BF16, kind="ExternalOutput")
        stw = P.val("s_st")

        def sy_wdbg(sy, stw=stw):
            sy.wait_ge(P.sem("s_st"), stw)
            sy.dma_start(w1dbg.ap(), w1[:]).then_inc(P.sem("s_st"), 16)
            sy.dma_start(w2dbg.ap(), w2[:]).then_inc(P.sem("s_st"), 16)
        P.emit("sync", sy_wdbg)
        P.bump("s_st", 32)

    st_final = P.val("s_st")

    def sy_fin(sy, st_final=st_final):
        sy.wait_ge(P.sem("s_st"), st_final)
    P.emit("sync", sy_fin)
    return P.build()


F32R = mybir.dt.float32r


def build_conv(nbatch=4, cut=None):
    dump = None
    if cut and ':' in cut:
        cut, dump = cut.split(':')
    NP = 64
    R = 128
    nc = bass.Bass("TRN2", target_bir_lowering=False)
    x2f = nc.dram_tensor("x2f", [8, 128, L], BF16, kind="ExternalInput")
    winT1 = nc.dram_tensor("winT1", [8, 128, 1536], F32, kind="ExternalInput")
    t_rep = nc.dram_tensor("t_rep", [65, L], F32, kind="ExternalInput")
    fw1 = nc.dram_tensor("fw1", [64, 2], F32, kind="ExternalInput")
    fb1 = nc.dram_tensor("fb1", [64, 2], F32, kind="ExternalInput")
    w2aug = nc.dram_tensor("w2aug", [2, 65, 512], F32, kind="ExternalInput")
    f1s = nc.dram_tensor("f1s", [2, 128, 128], F32, kind="ExternalInput")
    f1r = nc.dram_tensor("f1r", [2, 64, 128], F32, kind="ExternalInput")
    wsh = nc.dram_tensor("wsh", [3, 128, 192], F32, kind="ExternalInput")
    twc = nc.dram_tensor("twc", [128, 256], F32, kind="ExternalInput")
    tw2c = nc.dram_tensor("tw2c", [128, 256], F32, kind="ExternalInput")
    wi1c = nc.dram_tensor("wi1c", [9, 128, 128], F32, kind="ExternalInput")
    prevc = nc.dram_tensor("prevc", [2, 128, 128], F32, kind="ExternalInput")
    idn = nc.dram_tensor("idn", [128, 128], F32, kind="ExternalInput")
    y_out = nc.dram_tensor("y_out", [4, 128, L], BF16, kind="ExternalOutput")

    kindi = "ExternalOutput" if cut else "Internal"
    stg = nc.dram_tensor("stg", [64, 1536, 128], F32, kind=kindi)
    stgh = nc.dram_tensor("stgh", [64, 1024, 128], F32, kind=kindi)
    stgu = nc.dram_tensor("stgu", [64, 512, 128], F32, kind=kindi)
    hspec = nc.dram_tensor("hspec", [2, 8, 2, 128, 8192], F32, kind=kindi)
    dbg = nc.dram_tensor("dbg", [2, 128, 8192], F32, kind=kindi)

    arena = nc.alloc_sbuf_tensor("arena", [128, 24576], F32)
    gt_t = nc.alloc_sbuf_tensor("gt_t", [128, 16384], F32)
    wsh_t = nc.alloc_sbuf_tensor("wsh_t", [128, 3 * 192], F32)
    twc_t = nc.alloc_sbuf_tensor("twc_t", [128, 256], F32)
    tw2c_t = nc.alloc_sbuf_tensor("tw2c_t", [128, 256], F32)
    dscr = nc.alloc_sbuf_tensor("dscr", [1, 16], F32)
    wi1_t = nc.alloc_sbuf_tensor("wi1_t", [128, 9 * 128], F32)
    f1s_t = nc.alloc_sbuf_tensor("f1s_t", [128, 2 * 128], F32)
    f1r_t = nc.alloc_sbuf_tensor("f1r_t", [64, 2 * 128], F32)
    prev_t = nc.alloc_sbuf_tensor("prev_t", [128, 2 * 128], F32)
    idn_t = nc.alloc_sbuf_tensor("idn_t", [128, 128], F32)
    zch = nc.alloc_sbuf_tensor("zch", [128, 2 * 512], F32)
    uvch = nc.alloc_sbuf_tensor("uvch", [128, 4 * 512], F32)
    sdch = nc.alloc_sbuf_tensor("sdch", [128, 4 * 512], F32)
    ybf = None
    fcv = nc.alloc_sbuf_tensor("fcv", [64, 4], F32)
    eps2 = nc.alloc_sbuf_tensor("eps2", [1, 2], F32)
    a_off = nc.lookup_mls("arena").memory_location.addr
    g_off = nc.lookup_mls("gt_t").memory_location.addr
    wp_t = nc.alloc_sbuf_tensor_at("wp_t", [128, 8 * 1536], BF16, offset=g_off)
    ybf = nc.alloc_sbuf_tensor_at("ybf", [64, 2048], BF16, offset=g_off + 49152)
    xpb = [nc.alloc_sbuf_tensor_at(f"xpb{i}", [128, 8 * 256], BF16, offset=g_off + 24576 + i * 4096)
           for i in range(2)]
    sfe_t = nc.alloc_sbuf_tensor_at("sfe_t", [65, L], F32, offset=a_off)
    w2a_t = nc.alloc_sbuf_tensor_at("w2a_t", [65, 2 * 512], F32, offset=a_off + 32768)
    pev = nc.alloc_sbuf_tensor_at("pev", [128, 512], F32, offset=a_off + 32768 + 4096)
    ytm = nc.alloc_sbuf_tensor_at("ytm", [64, 8192], F32, offset=a_off)
    gtm = nc.alloc_sbuf_tensor_at("gtm", [64, 8192], F32, offset=a_off + 32768)

    P = Prog(nc)
    NPS = 7
    pool = PsumPool(nc, NPS, 512)
    ps_dummy = nc.alloc_psum_tensor("ps_dumc", [128, 320], F32)
    for sname in ["s_in", "s_wt", "s_mm", "s_ev", "s_evs", "s_ph", "s_phs", "s_st",
                  "s_stage", "s_m2", "s_sd", "s_tr", "s_wtg"]:
        P.sem(sname)

    def drain(wait_ph=None):
        def t_drain(t, wait_ph=wait_ph):
            if wait_ph is not None:
                t.wait_ge(P.sem("s_ph"), wait_ph)
            t.matmul(ps_dummy[:, 0:320], wi1_t[:, 0:128], wi1_t[:, 0:320],
                     start=True, stop=True).then_inc(P.sem("s_mm"), 1)
        P.emit("tensor", t_drain)
        return P.bump("s_mm", 1)

    def g_pre(g):
        g.dma_start(wp_t[:].rearrange("p (k m) -> p k m", k=8),
                    winT1.ap().transpose([1, 0, 2])).then_inc(P.sem("s_wtg"), 16)
    P.emit("gpsimd", g_pre)
    P.bump("s_wtg", 16)

    wbar_t = nc.alloc_sbuf_tensor("wbar_ct", [128, 64], BF16)

    def s_wbarc(s):
        s.wait_ge(P.sem("s_wtg"), 16)
        s.wait_ge(P.sem("s_ph"), 1)
        ins = None
        for k in range(8):
            ins = s.activation(wbar_t[:], wp_t[:, k * 1536 + 1472:k * 1536 + 1536], AF.Copy)
        ins.then_inc(P.sem("s_phs"), 1)
    P.emit("scalar", s_wbarc)
    WBARC = P.bump("s_phs", 1)

    def sy_pre(sy):
        sy.dma_start(f1s_t[:].rearrange("p (a m) -> p a m", a=2),
                     f1s.ap().transpose([1, 0, 2])).then_inc(P.sem("s_wt"), 16)
        sy.dma_start(f1r_t[:].rearrange("p (a m) -> p a m", a=2),
                     f1r.ap().transpose([1, 0, 2])).then_inc(P.sem("s_wt"), 16)
        sy.dma_start(wi1_t[:].rearrange("p (a m) -> p a m", a=9),
                     wi1c.ap().transpose([1, 0, 2])).then_inc(P.sem("s_wt"), 16)
        sy.dma_start(prev_t[:].rearrange("p (a m) -> p a m", a=2),
                     prevc.ap().transpose([1, 0, 2])).then_inc(P.sem("s_wt"), 16)
        sy.dma_start(idn_t[:], idn.ap()).then_inc(P.sem("s_wt"), 16)
        sy.dma_start(sfe_t[0:65, :], t_rep.ap()).then_inc(P.sem("s_wt"), 16)
        sy.dma_start(w2a_t[0:65, :].rearrange("p (a m) -> p a m", a=2),
                     w2aug.ap().transpose([1, 0, 2])).then_inc(P.sem("s_wt"), 16)
        sy.dma_start(fcv[0:64, 0:2], fw1.ap()).then_inc(P.sem("s_wt"), 16)
        sy.dma_start(fcv[0:64, 2:4], fb1.ap()).then_inc(P.sem("s_wt"), 16)
        sy.dma_start(wsh_t[:].rearrange("p (a m) -> p a m", a=3),
                     wsh.ap().transpose([1, 0, 2])).then_inc(P.sem("s_wt"), 16)
        sy.dma_start(twc_t[:], twc.ap()).then_inc(P.sem("s_wt"), 16)
        sy.dma_start(tw2c_t[:], tw2c.ap()).then_inc(P.sem("s_wt"), 16)
    P.emit("sync", sy_pre)
    WT0 = P.bump("s_wt", 16 * 12)

    def v_pre(v):
        for _i in range(100):
            v.memset(arena[:, 0:512], 1.0).then_inc(P.sem("s_spin"), 1)
            v.wait_ge(P.sem("s_spin"), _i + 1)
        v.memset(eps2[:], 0.0)
        ins = v.memset(arena[:, 0:320], 1.0)
        ins.then_inc(P.sem("s_ph"), 1)
    P.emit("vector", v_pre)
    P.bump("s_ph", 1)

    nchp = L // 256
    for c0 in range(2):
        def sy_ldx(sy, c=c0):
            sy.dma_start(xpb[c % 2][:].rearrange("p (k m) -> p k m", k=8),
                         x2f.ap()[:, :, c * 256:(c + 1) * 256].transpose([1, 0, 2])
                         ).then_inc(P.sem("s_in"), 16)
        P.emit("sync", sy_ldx)
        P.bump("s_in", 16)
    for c in range(nchp):
        for m in range(12):
            bk, pt, freed = pool.take()

            def t_pj(t, c=c, m=m, pt=pt, freed=freed):
                if c == 0 and m == 0:
                    t.wait_ge(P.sem("s_wt"), WT0)
                    t.wait_ge(P.sem("s_wtg"), 16)
                    t.wait_ge(P.sem("s_phs"), WBARC)
                    t.wait_ge(P.sem("s_ph"), 1)
                if m == 0:
                    t.wait_ge(P.sem("s_in"), 16 * (c + 1))
                if freed is not None:
                    t.wait_ge(P.sem(freed[0]), freed[1])
                for k in range(8):
                    ins = t.matmul(pt[:, 0:256],
                                   wp_t[:, k * 1536 + m * 128:k * 1536 + m * 128 + 128],
                                   xpb[c % 2][:, k * 256:(k + 1) * 256],
                                   start=(k == 0), stop=(k == 7))
                ins.then_inc(P.sem("s_mm"), 1)
            P.emit("tensor", t_pj)
            need = P.bump("s_mm", 1)

            stg_prev = P.val("s_stage")

            def v_pe(v, m=m, pt=pt, need=need, stg_prev=stg_prev):
                v.wait_ge(P.sem("s_mm"), need + 1)
                v.wait_ge(P.sem("s_stage"), stg_prev)
                ins = v.tensor_copy(pev[:, 0:256], pt[:, 0:256])
                ins.then_inc(P.sem("s_ev"), 1)
            P.emit("vector", v_pe)
            evn = P.bump("s_ev", 1)
            pool.release(bk, "s_ev", evn)

            def sy_stg(sy, c=c, m=m, evn=evn):
                sy.wait_ge(P.sem("s_ev"), evn)
                sy.dma_start(
                    stg.ap()[c * 2:(c + 1) * 2, m * 128:(m + 1) * 128, :].transpose([1, 0, 2]),
                    pev[:, 0:256].rearrange("p (a f) -> p a f", a=2),
                ).then_inc(P.sem("s_stage"), 16)
            P.emit("sync", sy_stg)
            P.bump("s_stage", 16)
        if c + 2 < nchp:
            ev_need = P.val("s_ev")

            def sy_ldx2(sy, c2=c + 2, ev_need=ev_need):
                sy.wait_ge(P.sem("s_ev"), ev_need)
                sy.dma_start(xpb[c2 % 2][:].rearrange("p (k m) -> p k m", k=8),
                             x2f.ap()[:, :, c2 * 256:(c2 + 1) * 256].transpose([1, 0, 2])
                             ).then_inc(P.sem("s_in"), 16)
            P.emit("sync", sy_ldx2)
            P.bump("s_in", 16)
    drain(wait_ph=1)
    if cut == 'proj':
        fin = P.val("s_stage")

        def sy_f(sy, fin=fin):
            sy.wait_ge(P.sem("s_stage"), fin)
        P.emit("sync", sy_f)
        return P.build()

    nchf = L // 512
    for o in range(2):
        ev_prev = P.val("s_ev")

        def sy_t(sy, ev_prev=ev_prev):
            sy.wait_ge(P.sem("s_ev"), ev_prev)
            sy.dma_start(sfe_t[0:65, :], t_rep.ap()).then_inc(P.sem("s_in"), 16)
        P.emit("sync", sy_t)
        tin = P.bump("s_in", 16)

        def s_feat_o(s, o=o, tin=tin):
            s.wait_ge(P.sem("s_in"), tin)
            s.wait_ge(P.sem("s_ph"), 1)
            if o == 0:
                s.wait_ge(P.sem("s_wt"), WT0)
            s.activation(sfe_t[0:64, :], sfe_t[0:64, :], AF.Silu,
                         scale=fcv[0:64, o:o + 1], bias=fcv[0:64, 2 + o:3 + o])
            ins = s.mul(sfe_t[0:64, :], sfe_t[0:64, :], 0.5)
            ins.then_inc(P.sem("s_phs"), 1)
        P.emit("scalar", s_feat_o)
        phs_f = P.bump("s_phs", 1)

        def v_ones(v, o=o, phs_f=phs_f):
            v.wait_ge(P.sem("s_phs"), phs_f)
            ins = v.memset(sfe_t[64:65, :], 0.5)
            ins.then_inc(P.sem("s_ph"), 1)
        P.emit("vector", v_ones)
        ph_f = P.bump("s_ph", 1)

        for c in range(nchf):
            for m in range(4):
                bk, pt, freed = pool.take()

                def t_h(t, o=o, c=c, m=m, pt=pt, freed=freed, ph_f=ph_f):
                    if c == 0 and m == 0:
                        t.wait_ge(P.sem("s_ph"), ph_f)
                    if freed is not None:
                        t.wait_ge(P.sem(freed[0]), freed[1])
                    ins = t.matmul(pt[:, 0:512],
                                   w2a_t[0:65, o * 512 + m * 128:o * 512 + (m + 1) * 128],
                                   sfe_t[0:65, c * 512:(c + 1) * 512],
                                   start=True, stop=True)
                    ins.then_inc(P.sem("s_mm"), 1)
                P.emit("tensor", t_h)
                need = P.bump("s_mm", 1)

                stg_prev = P.val("s_stage")

                def v_he(v, pt=pt, need=need, stg_prev=stg_prev):
                    v.wait_ge(P.sem("s_mm"), need + 1)
                    v.wait_ge(P.sem("s_stage"), stg_prev)
                    ins = v.tensor_copy(pev[:, 0:512], pt[:, 0:512])
                    ins.then_inc(P.sem("s_ev"), 1)
                P.emit("vector", v_he)
                evn = P.bump("s_ev", 1)
                pool.release(bk, "s_ev", evn)

                def sy_sth(sy, o=o, c=c, m=m, evn=evn):
                    sy.wait_ge(P.sem("s_ev"), evn)
                    sy.dma_start(
                        stgh.ap()[c * 4:(c + 1) * 4, o * 512 + m * 128:o * 512 + (m + 1) * 128, :
                                  ].transpose([1, 0, 2]),
                        pev[:, 0:512].rearrange("p (a f) -> p a f", a=4),
                    ).then_inc(P.sem("s_stage"), 16)
                P.emit("sync", sy_sth)
                P.bump("s_stage", 16)
        drain()
    if cut == 'filt':
        fin = P.val("s_stage")

        def sy_f(sy, fin=fin):
            sy.wait_ge(P.sem("s_stage"), fin)
        P.emit("sync", sy_f)
        return P.build()

    import dataclasses

    def sb_ap(t, offset, dims, pbase=0, nparts=128):
        ap = t[:]
        W = ap.ap[0][0]
        return dataclasses.replace(
            ap, offset=pbase * W + offset,
            ap=type(ap.ap)([[W, nparts]] + [list(d) for d in dims]))

    GR_OFF = 8192
    GI_OFF = 16384
    QR_OFF = 0
    QI_OFF = 8192
    ZR_OFF = 0
    ZI_OFF = 8192

    def conv_tile(kind, o, idx, dump=None):
        real_in = (kind == 'h')
        pb = idx
        st_pre = P.val("s_stage")
        mm_pre = P.val("s_mm")

        def sy_z(sy, kind=kind, o=o, pb=pb, st_pre=st_pre, mm_pre=mm_pre):
            sy.wait_ge(P.sem("s_stage"), st_pre)
            sy.wait_ge(P.sem("s_mm"), mm_pre)
            if kind == 'h':
                sy.dma_start(arena[0:64, 0:8192].rearrange("p (a f) -> p a f", a=64),
                             stgh.ap()[0:64, o * 512 + pb * 64:o * 512 + (pb + 1) * 64, :]
                             ).then_inc(P.sem("s_in"), 16)
            else:
                src = stg if kind == 'v' else stgu
                sy.dma_start(arena[0:64, 0:8192].rearrange("p (a f) -> p a f", a=64),
                             src.ap()[0:64, pb * 64:(pb + 1) * 64, :]
                             ).then_inc(P.sem("s_in"), 16)
                sy.dma_start(arena[64:128, 0:8192].rearrange("p (a f) -> p a f", a=64),
                             src.ap()[0:64, 256 + pb * 64:256 + (pb + 1) * 64, :]
                             ).then_inc(P.sem("s_in"), 16)
            sy.dma_start(dscr[0:1, 0:16], idn.ap()[0:1, 0:16]).then_inc(P.sem("s_in"), 16)
        P.emit("sync", sy_z)
        zin = P.bump("s_in", 32 if real_in else 48)

        evns = {}
        for n in range(16):
            for pl in range(2):
                bk, pt, freed = pool.take()

                def t_f1(t, n=n, pl=pl, pt=pt, freed=freed, zin=zin, real_in=real_in):
                    if n == 0 and pl == 0:
                        t.wait_ge(P.sem("s_in"), zin)
                    if freed is not None:
                        t.wait_ge(P.sem(freed[0]), freed[1])
                    if real_in:
                        ins = t.matmul(pt[:, 0:512],
                                       f1r_t[0:64, pl * 128:(pl + 1) * 128],
                                       arena[0:64, n * 512:(n + 1) * 512],
                                       start=True, stop=True)
                    else:
                        ins = t.matmul(pt[:, 0:512],
                                       f1s_t[:, pl * 128:(pl + 1) * 128],
                                       arena[:, n * 512:(n + 1) * 512],
                                       start=True, stop=True)
                    ins.then_inc(P.sem("s_mm"), 1)
                P.emit("tensor", t_f1)
                need = P.bump("s_mm", 1)

                def s_ge(s, n=n, pl=pl, pt=pt, need=need, first=(n == 0 and pl == 0),
                         st_pre=st_pre):
                    s.wait_ge(P.sem("s_mm"), need + 1)
                    if first:
                        s.wait_ge(P.sem("s_stage"), st_pre)
                    goff = (GR_OFF if pl == 0 else GI_OFF) + n * 512
                    s.activation(arena[:, goff:goff + 512], pt[:, 0:512],
                                 AF.Copy).then_inc(P.sem("s_evs"), 1)
                P.emit("scalar", s_ge)
                evn = P.bump("s_evs", 1)
                evns[(n, pl)] = evn
                pool.release(bk, "s_evs", evn)
        drain()
        f1_evs = P.val("s_evs")

        for pl in range(2):
            for q in range(16):
                bk, pt, freed = pool.take()
                need = None
                for j in range(4):
                    def t_tr(t, pl=pl, q=q, j=j, pt=pt, freed=freed, evq=evns[(q, pl)]):
                        if j == 0:
                            t.wait_ge(P.sem("s_evs"), evq)
                        if j == 0 and freed is not None:
                            t.wait_ge(P.sem(freed[0]), freed[1])
                        goff = (GR_OFF if pl == 0 else GI_OFF) + (q * 4 + j) * 128
                        t.transpose(pt[:, j * 128:(j + 1) * 128],
                                    arena[:, goff:goff + 128],
                                    idn_t[:]).then_inc(P.sem("s_mm"), 1)
                    P.emit("tensor", t_tr)
                    need = P.bump("s_mm", 1)

                def v_tre(v, pl=pl, q=q, pt=pt, need=need,
                          first=(pl == 0 and q == 0), st_pre=st_pre):
                    v.wait_ge(P.sem("s_mm"), need + 1)
                    if first:
                        v.wait_ge(P.sem("s_stage"), st_pre)
                    dst = sb_ap(gt_t, (ZR_OFF if pl == 0 else ZI_OFF) + q * 4,
                                [[1, 4], [64, 128]])
                    v.tensor_copy(dst, pt[:, 0:512].rearrange("p (a f) -> p a f", a=4)
                                  ).then_inc(P.sem("s_ev"), 1)
                P.emit("vector", v_tre)
                evn = P.bump("s_ev", 1)
                pool.release(bk, "s_ev", evn)
        drain()
        tr_evs = P.val("s_ev")

        tw0 = P.val("s_ev")
        for k1 in range(128):
            def v_tw(v, k1=k1):
                gr = gt_t[:, ZR_OFF + k1 * 64:ZR_OFF + k1 * 64 + 64]
                gi = gt_t[:, ZI_OFF + k1 * 64:ZI_OFF + k1 * 64 + 64]
                v.tensor_scalar_mul(zch[:, 0:64], gi, twc_t[:, 128 + k1:129 + k1])
                v.tensor_scalar_mul(zch[:, 64:128], gr, twc_t[:, 128 + k1:129 + k1])
                v.scalar_tensor_tensor(gr, gr, twc_t[:, k1:k1 + 1], zch[:, 0:64],
                                       ALU.mult, ALU.subtract)
                ins = v.scalar_tensor_tensor(gi, gi, twc_t[:, k1:k1 + 1], zch[:, 64:128],
                                             ALU.mult, ALU.add)
                ins.then_inc(P.sem("s_ev"), 1)
            P.emit("vector", v_tw)
            P.bump("s_ev", 1)

        for n in range(16):
            bkr, ptr, fr = pool.take()
            bki, pti, fi = pool.take()

            def t_f2(t, n=n, ptr=ptr, pti=pti, fr=fr, fi=fi, tw_need=tw0 + 8 * (n + 1)):
                t.wait_ge(P.sem("s_ev"), tw_need)
                if fr is not None:
                    t.wait_ge(P.sem(fr[0]), fr[1])
                if fi is not None:
                    t.wait_ge(P.sem(fi[0]), fi[1])
                rr = gt_t[:, ZR_OFF + n * 512:ZR_OFF + (n + 1) * 512]
                ri = gt_t[:, ZI_OFF + n * 512:ZI_OFF + (n + 1) * 512]
                t.matmul(ptr[:, 0:512], wsh_t[:, 0:128], rr,
                         start=True, stop=False).then_inc(P.sem("s_mm"), 1)
                t.matmul(ptr[:, 0:512], wsh_t[:, 384:512], ri,
                         start=False, stop=True).then_inc(P.sem("s_mm"), 1)
                t.matmul(pti[:, 0:512], wsh_t[:, 192:320], rr,
                         start=True, stop=False).then_inc(P.sem("s_mm"), 1)
                t.matmul(pti[:, 0:512], wsh_t[:, 0:128], ri,
                         start=False, stop=True).then_inc(P.sem("s_mm"), 1)
            P.emit("tensor", t_f2)
            P.bump("s_mm", 4)
            zneed = P.val("s_mm")

            for pl in range(2):
                def s_ze(s, n=n, pl=pl, pt=(ptr if pl == 0 else pti), zneed=zneed):
                    s.wait_ge(P.sem("s_mm"), zneed + 1)
                    zoff = (ZR_OFF if pl == 0 else ZI_OFF) + n * 512
                    s.activation(gt_t[:, zoff:zoff + 512], pt[:, 0:512],
                                 AF.Copy).then_inc(P.sem("s_evs"), 1)
                P.emit("scalar", s_ze)
                evn = P.bump("s_evs", 1)
                pool.release(bkr if pl == 0 else bki, "s_evs", evn)
        drain()
        f2_evs = P.val("s_evs")

        if dump == 'z':
            def sy_dz(sy, f2_evs=f2_evs):
                sy.wait_ge(P.sem("s_evs"), f2_evs)
                for pl in range(2):
                    sy.dma_start(dbg.ap()[pl, :, :], gt_t[:, pl * 8192:(pl + 1) * 8192]
                                 ).then_inc(P.sem("s_stage"), 16)
            P.emit("sync", sy_dz)
            P.bump("s_stage", 32)
            return

        if kind == 'h':
            def sy_hx(sy, o=o, pb=pb, f2_evs=f2_evs):
                sy.wait_ge(P.sem("s_evs"), f2_evs)
                for pl in range(2):
                    sy.dma_start(hspec.ap()[o, pb, pl, :, :],
                                 gt_t[:, pl * 8192:(pl + 1) * 8192]
                                 ).then_inc(P.sem("s_stage"), 16)
            P.emit("sync", sy_hx)
            P.bump("s_stage", 32)
            return

        hs_pre = P.val("s_stage")
        prev_qneed = 0
        for g in range(16):
            ev_free = P.val("s_ev")

            def sy_sd(sy, o=o, pb=pb, g=g, ev_free=ev_free, hs_pre=hs_pre):
                sy.wait_ge(P.sem("s_ev"), ev_free)
                if g == 0:
                    sy.wait_ge(P.sem("s_stage"), hs_pre)
                for sl, (cb, pl) in enumerate([(pb, 0), (pb, 1), (4 + pb, 0), (4 + pb, 1)]):
                    sy.dma_start(sdch[:, sl * 512:(sl + 1) * 512],
                                 hspec.ap()[o, cb, pl, :, g * 512:(g + 1) * 512]
                                 ).then_inc(P.sem("s_sd"), 16)
                sy.dma_start(dscr[0:1, 0:16], idn.ap()[0:1, 0:16]).then_inc(P.sem("s_sd"), 16)
            P.emit("sync", sy_sd)
            sdn = P.bump("s_sd", 80)

            def v_ds(v, g=g, sdn=sdn, f2_evs=f2_evs, prev_qneed=prev_qneed):
                v.wait_ge(P.sem("s_sd"), sdn)
                if g == 0:
                    v.wait_ge(P.sem("s_evs"), f2_evs)
                else:
                    v.wait_ge(P.sem("s_mm"), prev_qneed)
                v.tensor_sub(uvch[:, 1024:1536], sdch[:, 0:512], sdch[:, 1024:1536])
                v.tensor_sub(uvch[:, 1536:2048], sdch[:, 512:1024], sdch[:, 1536:2048])
                v.tensor_add(sdch[:, 0:512], sdch[:, 0:512], sdch[:, 1024:1536])
                ins = v.tensor_add(sdch[:, 512:1024], sdch[:, 512:1024],
                                   sdch[:, 1536:2048])
                ins.then_inc(P.sem("s_ev"), 1)
            P.emit("vector", v_ds)
            ev_d = P.bump("s_ev", 1)

            bkA, ptA, fA = pool.take()
            bkB, ptB, fB = pool.take()
            for pl in range(2):
                def t_pm(t, g=g, pl=pl, pt=(ptA if pl == 0 else ptB),
                         fx=(fA if pl == 0 else fB), src=(1024 if pl == 0 else 1536),
                         ev_d=ev_d):
                    if fx is not None:
                        t.wait_ge(P.sem(fx[0]), fx[1])
                    if pl == 0:
                        t.wait_ge(P.sem("s_ev"), ev_d)
                    if g == 0:
                        t.matmul(pt[:, 0:64], prev_t[:, 128:256], uvch[:, src:src + 64],
                                 start=True, stop=True).then_inc(P.sem("s_mm"), 1)
                        t.matmul(pt[:, 64:512], prev_t[:, 0:128], uvch[:, src + 64:src + 512],
                                 start=True, stop=True).then_inc(P.sem("s_mm"), 1)
                    else:
                        t.matmul(pt[:, 0:512], prev_t[:, 0:128], uvch[:, src:src + 512],
                                 start=True, stop=True).then_inc(P.sem("s_mm"), 1)
                P.emit("tensor", t_pm)
                P.bump("s_mm", 2 if g == 0 else 1)
            perm_need = P.val("s_mm")

            def v_u(v, g=g):
                zr = gt_t[:, ZR_OFF + g * 512:ZR_OFF + (g + 1) * 512]
                zi = gt_t[:, ZI_OFF + g * 512:ZI_OFF + (g + 1) * 512]
                v.tensor_mul(zch[:, 0:512], zr, sdch[:, 0:512])
                v.tensor_mul(zch[:, 512:1024], zi, sdch[:, 512:1024])
                v.tensor_sub(uvch[:, 0:512], zch[:, 0:512], zch[:, 512:1024])
                v.tensor_mul(zch[:, 0:512], zr, sdch[:, 512:1024])
                v.tensor_mul(zch[:, 512:1024], zi, sdch[:, 0:512])
                ins = v.tensor_add(uvch[:, 512:1024], zch[:, 0:512], zch[:, 512:1024])
                ins.then_inc(P.sem("s_ev"), 1)
            P.emit("vector", v_u)
            ev_u = P.bump("s_ev", 1)

            def v_dv(v, ptA=ptA, ptB=ptB, perm_need=perm_need):
                v.wait_ge(P.sem("s_mm"), perm_need + 1)
                v.tensor_copy(uvch[:, 1024:1536], ptA[:, 0:512])
                ins = v.tensor_copy(uvch[:, 1536:2048], ptB[:, 0:512])
                ins.then_inc(P.sem("s_ev"), 1)
            P.emit("vector", v_dv)
            ev_dp = P.bump("s_ev", 1)
            pool.release(bkA, "s_ev", ev_dp)
            pool.release(bkB, "s_ev", ev_dp)

            def v_v(v, g=g):
                if g == 0:
                    segs = [(0, 0, 1), (8128, 64, 7)]
                else:
                    segs = [((128 - 8 * g) * 64, 0, 8)]
                ins = None
                for base, c0, nb in segs:
                    w = nb * 64
                    if nb > 1:
                        zr = sb_ap(gt_t, ZR_OFF + base, [[-64, nb], [1, 64]])
                        zi = sb_ap(gt_t, ZI_OFF + base, [[-64, nb], [1, 64]])
                        rs = lambda apx: apx.rearrange("p (a f) -> p a f", a=nb)
                    else:
                        zr = gt_t[:, ZR_OFF + base:ZR_OFF + base + 64]
                        zi = gt_t[:, ZI_OFF + base:ZI_OFF + base + 64]
                        rs = lambda apx: apx
                    dpr = rs(uvch[:, 1024 + c0:1024 + c0 + w])
                    dpi = rs(uvch[:, 1536 + c0:1536 + c0 + w])
                    t1 = rs(zch[:, 0:w])
                    t2 = rs(zch[:, 512:512 + w])
                    t3 = rs(sdch[:, 0:w])
                    t4 = rs(sdch[:, 512:512 + w])
                    v.tensor_mul(t1, zr, dpi)
                    v.tensor_mul(t2, zi, dpr)
                    v.tensor_mul(t3, zr, dpr)
                    v.tensor_mul(t4, zi, dpi)
                    v.tensor_add(rs(uvch[:, 1024 + c0:1024 + c0 + w]), t3, t4)
                    ins = v.tensor_sub(rs(uvch[:, 1536 + c0:1536 + c0 + w]), t1, t2)
                ins.then_inc(P.sem("s_ev"), 1)
            P.emit("vector", v_v)
            ev_v = P.bump("s_ev", 1)

            bkQr, ptQr, fQr = pool.take()
            bkQi, ptQi, fQi = pool.take()

            def t_i1(t, g=g, ptQr=ptQr, ptQi=ptQi, fQr=fQr, fQi=fQi,
                     ev_u=ev_u, ev_v=ev_v):
                if fQr is not None:
                    t.wait_ge(P.sem(fQr[0]), fQr[1])
                if fQi is not None:
                    t.wait_ge(P.sem(fQi[0]), fQi[1])
                t.wait_ge(P.sem("s_ev"), ev_u)
                W = wi1_t
                if g == 0:
                    t.matmul(ptQr[:, 0:64], W[:, 0 * 128:1 * 128], uvch[:, 0:64],
                             start=True, stop=False).then_inc(P.sem("s_mm"), 1)
                    t.matmul(ptQr[:, 0:64], W[:, 2 * 128:3 * 128], uvch[:, 512:576],
                             start=False, stop=False).then_inc(P.sem("s_mm"), 1)
                    t.matmul(ptQi[:, 0:64], W[:, 1 * 128:2 * 128], uvch[:, 0:64],
                             start=True, stop=False).then_inc(P.sem("s_mm"), 1)
                    t.matmul(ptQi[:, 0:64], W[:, 0 * 128:1 * 128], uvch[:, 512:576],
                             start=False, stop=False).then_inc(P.sem("s_mm"), 1)
                    t.wait_ge(P.sem("s_ev"), ev_v)
                    t.matmul(ptQr[:, 0:64], W[:, 6 * 128:7 * 128], uvch[:, 1024:1088],
                             start=False, stop=False).then_inc(P.sem("s_mm"), 1)
                    t.matmul(ptQr[:, 0:64], W[:, 8 * 128:9 * 128], uvch[:, 1536:1600],
                             start=False, stop=True).then_inc(P.sem("s_mm"), 1)
                    t.matmul(ptQi[:, 0:64], W[:, 7 * 128:8 * 128], uvch[:, 1024:1088],
                             start=False, stop=False).then_inc(P.sem("s_mm"), 1)
                    t.matmul(ptQi[:, 0:64], W[:, 6 * 128:7 * 128], uvch[:, 1536:1600],
                             start=False, stop=True).then_inc(P.sem("s_mm"), 1)
                    t.matmul(ptQr[:, 64:512], W[:, 0 * 128:1 * 128], uvch[:, 64:512],
                             start=True, stop=False).then_inc(P.sem("s_mm"), 1)
                    t.matmul(ptQr[:, 64:512], W[:, 2 * 128:3 * 128], uvch[:, 576:1024],
                             start=False, stop=False).then_inc(P.sem("s_mm"), 1)
                    t.matmul(ptQi[:, 64:512], W[:, 1 * 128:2 * 128], uvch[:, 64:512],
                             start=True, stop=False).then_inc(P.sem("s_mm"), 1)
                    t.matmul(ptQi[:, 64:512], W[:, 0 * 128:1 * 128], uvch[:, 576:1024],
                             start=False, stop=False).then_inc(P.sem("s_mm"), 1)
                    t.matmul(ptQr[:, 64:512], W[:, 3 * 128:4 * 128], uvch[:, 1088:1536],
                             start=False, stop=False).then_inc(P.sem("s_mm"), 1)
                    t.matmul(ptQr[:, 64:512], W[:, 5 * 128:6 * 128], uvch[:, 1600:2048],
                             start=False, stop=True).then_inc(P.sem("s_mm"), 1)
                    t.matmul(ptQi[:, 64:512], W[:, 4 * 128:5 * 128], uvch[:, 1088:1536],
                             start=False, stop=False).then_inc(P.sem("s_mm"), 1)
                    t.matmul(ptQi[:, 64:512], W[:, 3 * 128:4 * 128], uvch[:, 1600:2048],
                             start=False, stop=True).then_inc(P.sem("s_mm"), 1)
                else:
                    t.matmul(ptQr[:, 0:512], W[:, 0 * 128:1 * 128], uvch[:, 0:512],
                             start=True, stop=False).then_inc(P.sem("s_mm"), 1)
                    t.matmul(ptQr[:, 0:512], W[:, 2 * 128:3 * 128], uvch[:, 512:1024],
                             start=False, stop=False).then_inc(P.sem("s_mm"), 1)
                    t.matmul(ptQi[:, 0:512], W[:, 1 * 128:2 * 128], uvch[:, 0:512],
                             start=True, stop=False).then_inc(P.sem("s_mm"), 1)
                    t.matmul(ptQi[:, 0:512], W[:, 0 * 128:1 * 128], uvch[:, 512:1024],
                             start=False, stop=False).then_inc(P.sem("s_mm"), 1)
                    t.wait_ge(P.sem("s_ev"), ev_v)
                    t.matmul(ptQr[:, 0:512], W[:, 3 * 128:4 * 128], uvch[:, 1024:1536],
                             start=False, stop=False).then_inc(P.sem("s_mm"), 1)
                    t.matmul(ptQr[:, 0:512], W[:, 5 * 128:6 * 128], uvch[:, 1536:2048],
                             start=False, stop=True).then_inc(P.sem("s_mm"), 1)
                    t.matmul(ptQi[:, 0:512], W[:, 4 * 128:5 * 128], uvch[:, 1024:1536],
                             start=False, stop=False).then_inc(P.sem("s_mm"), 1)
                    t.matmul(ptQi[:, 0:512], W[:, 3 * 128:4 * 128], uvch[:, 1536:2048],
                             start=False, stop=True).then_inc(P.sem("s_mm"), 1)
            P.emit("tensor", t_i1)
            P.bump("s_mm", 16 if g == 0 else 8)
            qneed = P.val("s_mm")
            prev_qneed = qneed

            for pl in range(2):
                def s_qe(s, g=g, pl=pl, pt=(ptQr if pl == 0 else ptQi), qneed=qneed):
                    s.wait_ge(P.sem("s_mm"), qneed + 1)
                    qoff = (QR_OFF if pl == 0 else QI_OFF) + g * 512
                    s.activation(arena[:, qoff:qoff + 512], pt[:, 0:512],
                                 AF.Copy).then_inc(P.sem("s_evs"), 1)
                P.emit("scalar", s_qe)
                evn = P.bump("s_evs", 1)
                pool.release(bkQr if pl == 0 else bkQi, "s_evs", evn)
            if dump == 'uv' and g == 0:
                drain()
                ev_now = P.val("s_ev")

                def sy_duv(sy, ev_now=ev_now):
                    sy.wait_ge(P.sem("s_ev"), ev_now)
                    sy.dma_start(dbg.ap()[0, :, 0:2048], uvch[:]
                                 ).then_inc(P.sem("s_stage"), 16)
                    sy.dma_start(dbg.ap()[1, :, 0:2048], sdch[:]
                                 ).then_inc(P.sem("s_stage"), 16)
                P.emit("sync", sy_duv)
                P.bump("s_stage", 32)
                return
        drain()
        i1_evs = P.val("s_evs")
        if dump == 'q':
            def sy_dq(sy, i1_evs=i1_evs):
                sy.wait_ge(P.sem("s_evs"), i1_evs)
                for pl in range(2):
                    sy.dma_start(dbg.ap()[pl, :, :],
                                 arena[:, pl * 8192:(pl + 1) * 8192]
                                 ).then_inc(P.sem("s_stage"), 16)
            P.emit("sync", sy_dq)
            P.bump("s_stage", 32)
            return

        for pl in range(2):
            for q in range(16):
                bk, pt, freed = pool.take()
                need = None
                for j in range(4):
                    def t_qt(t, pl=pl, q=q, j=j, pt=pt, freed=freed, i1_evs=i1_evs):
                        if pl == 0 and q == 0 and j == 0:
                            t.wait_ge(P.sem("s_evs"), i1_evs)
                        if j == 0 and freed is not None:
                            t.wait_ge(P.sem(freed[0]), freed[1])
                        p = q * 4 + j
                        src = sb_ap(arena, (QR_OFF if pl == 0 else QI_OFF) + p,
                                    [[64, 128]])
                        t.transpose(pt[:, j * 128:(j + 1) * 128], src,
                                    idn_t[:]).then_inc(P.sem("s_mm"), 1)
                    P.emit("tensor", t_qt)
                    need = P.bump("s_mm", 1)

                def v_qte(v, pl=pl, q=q, pt=pt, need=need):
                    v.wait_ge(P.sem("s_mm"), need + 1)
                    dst = sb_ap(gt_t, (ZR_OFF if pl == 0 else ZI_OFF) + q * 4,
                                [[1, 4], [64, 128]])
                    v.tensor_copy(dst, pt[:, 0:512].rearrange("p (a f) -> p a f", a=4)
                                  ).then_inc(P.sem("s_ev"), 1)
                P.emit("vector", v_qte)
                evn = P.bump("s_ev", 1)
                pool.release(bk, "s_ev", evn)
        drain()
        qt_evs = P.val("s_ev")
        if dump == 'qt':
            mmqt = P.val("s_mm")

            def sy_dqt(sy, qt_evs=qt_evs, mmqt=mmqt):
                sy.wait_ge(P.sem("s_ev"), qt_evs)
                for pl in range(2):
                    sy.dma_start(dbg.ap()[pl, :, :], gt_t[:, pl * 8192:(pl + 1) * 8192]
                                 ).then_inc(P.sem("s_stage"), 16)
            P.emit("sync", sy_dqt)
            P.bump("s_stage", 32)
            return

        tw20 = P.val("s_ev")
        for l2 in range(128):
            def v_tw2(v, l2=l2):
                qr = gt_t[:, ZR_OFF + l2 * 64:ZR_OFF + l2 * 64 + 64]
                qi = gt_t[:, ZI_OFF + l2 * 64:ZI_OFF + l2 * 64 + 64]
                v.tensor_scalar_mul(zch[:, 0:64], qi, tw2c_t[:, 128 + l2:129 + l2])
                v.tensor_scalar_mul(zch[:, 64:128], qr, tw2c_t[:, 128 + l2:129 + l2])
                v.scalar_tensor_tensor(qr, qr, tw2c_t[:, l2:l2 + 1], zch[:, 0:64],
                                       ALU.mult, ALU.subtract)
                ins = v.scalar_tensor_tensor(qi, qi, tw2c_t[:, l2:l2 + 1], zch[:, 64:128],
                                             ALU.mult, ALU.add)
                ins.then_inc(P.sem("s_ev"), 1)
            P.emit("vector", v_tw2)
            P.bump("s_ev", 1)

        for n in range(16):
            bkr, ptr, fr = pool.take()
            bki, pti, fi = pool.take()

            def t_i2(t, n=n, ptr=ptr, pti=pti, fr=fr, fi=fi, tw_need=tw20 + 8 * (n + 1)):
                t.wait_ge(P.sem("s_ev"), tw_need)
                if fr is not None:
                    t.wait_ge(P.sem(fr[0]), fr[1])
                if fi is not None:
                    t.wait_ge(P.sem(fi[0]), fi[1])
                rr = gt_t[:, ZR_OFF + n * 512:ZR_OFF + (n + 1) * 512]
                ri = gt_t[:, ZI_OFF + n * 512:ZI_OFF + (n + 1) * 512]
                t.matmul(ptr[0:64, 0:512], wsh_t[:, 128:192], rr,
                         start=True, stop=False).then_inc(P.sem("s_mm"), 1)
                t.matmul(ptr[0:64, 0:512], wsh_t[:, 512:576], ri,
                         start=False, stop=True).then_inc(P.sem("s_mm"), 1)
                t.matmul(pti[0:64, 0:512], wsh_t[:, 320:384], rr,
                         start=True, stop=False).then_inc(P.sem("s_mm"), 1)
                t.matmul(pti[0:64, 0:512], wsh_t[:, 128:192], ri,
                         start=False, stop=True).then_inc(P.sem("s_mm"), 1)
            P.emit("tensor", t_i2)
            P.bump("s_mm", 4)
            ineed = P.val("s_mm")

            for pl in range(2):
                def v_ie(v, n=n, pl=pl, pt=(ptr if pl == 0 else pti), ineed=ineed):
                    v.wait_ge(P.sem("s_mm"), ineed + 1)
                    ybase = 0 if pl == 0 else GI_OFF
                    srcp = sb_ap(pt, 0, [[64, 8], [1, 64]], nparts=64)
                    dst = sb_ap(arena, ybase + 8 * n, [[1, 8], [128, 64]], nparts=64)
                    ins = v.tensor_copy(dst, srcp)
                    ins.then_inc(P.sem("s_ev"), 1)
                P.emit("vector", v_ie)
                evn = P.bump("s_ev", 1)
                pool.release(bkr if pl == 0 else bki, "s_ev", evn)
        drain()
        if dump == 'p':
            ev_i2 = P.val("s_ev")

            def sy_dp(sy, ev_i2=ev_i2):
                sy.wait_ge(P.sem("s_ev"), ev_i2)
                sy.dma_start(dbg.ap()[0, 0:64, :], arena[0:64, 0:8192]
                             ).then_inc(P.sem("s_stage"), 16)
                sy.dma_start(dbg.ap()[1, 0:64, :], arena[0:64, GI_OFF:GI_OFF + 8192]
                             ).then_inc(P.sem("s_stage"), 16)
            P.emit("sync", sy_dp)
            P.bump("s_stage", 32)
            return

        qt_mm = P.val("s_mm")
        gbase = 512 + 512 * o
        for half in range(2):
            ch0 = gbase + (256 * half) + pb * 64
            ev_g_pre = P.val("s_ev")

            def sy_g(sy, ch0=ch0, qt_mm=qt_mm, half=half, ev_g_pre=ev_g_pre):
                if half == 0:
                    sy.wait_ge(P.sem("s_mm"), qt_mm)
                else:
                    sy.wait_ge(P.sem("s_ev"), ev_g_pre)
                sy.dma_start(arena[0:64, 8192:16384].rearrange(
                    "p (a f) -> p a f", a=64),
                    stg.ap()[0:64, ch0:ch0 + 64, :]).then_inc(P.sem("s_in"), 16)
                sy.dma_start(dscr[0:1, 0:16], idn.ap()[0:1, 0:16]).then_inc(P.sem("s_in"), 16)
            P.emit("sync", sy_g)
            gin = P.bump("s_in", 32)

            ysrc = 0 if half == 0 else GI_OFF
            if kind == 'v':
                def v_gm(v, gin=gin, ysrc=ysrc):
                    v.wait_ge(P.sem("s_in"), gin)
                    ins = v.tensor_mul(arena[0:64, ysrc:ysrc + 8192],
                                       arena[0:64, ysrc:ysrc + 8192],
                                       arena[0:64, 8192:16384])
                    ins.then_inc(P.sem("s_ev"), 1)
                P.emit("vector", v_gm)
                evg = P.bump("s_ev", 1)

                def sy_u1(sy, pb=pb, half=half, evg=evg, ysrc=ysrc):
                    sy.wait_ge(P.sem("s_ev"), evg)
                    sy.dma_start(
                        stgu.ap()[0:64, 256 * half + pb * 64:256 * half + (pb + 1) * 64, :],
                        arena[0:64, ysrc:ysrc + 8192].rearrange(
                            "p (a f) -> p a f", a=64)).then_inc(P.sem("s_stage"), 16)
                P.emit("sync", sy_u1)
                P.bump("s_stage", 16)
            else:
                for hx in range(4):
                    st_y = P.val("s_stage")

                    def v_gy(v, gin=gin, hx=hx, ysrc=ysrc, st_y=st_y):
                        v.wait_ge(P.sem("s_in"), gin)
                        v.wait_ge(P.sem("s_stage"), st_y)
                        ins = v.tensor_mul(
                            ybf[0:64, 0:2048],
                            arena[0:64, ysrc + hx * 2048:ysrc + (hx + 1) * 2048],
                            arena[0:64, 8192 + hx * 2048:8192 + (hx + 1) * 2048])
                        ins.then_inc(P.sem("s_ev"), 1)
                    P.emit("vector", v_gy)
                    evy = P.bump("s_ev", 1)

                    gch = 256 * half + pb * 64 + hx * 16
                    kt, r0 = gch // 128, gch % 128

                    def sy_y(sy, kt=kt, r0=r0, evy=evy):
                        sy.wait_ge(P.sem("s_ev"), evy)
                        dst = y_out.ap()[kt, r0:r0 + 16, :].rearrange(
                            "c (a f) -> c a f", a=64).transpose([1, 0, 2])
                        sy.dma_start(dst, ybf[0:64, 0:2048].rearrange(
                            "p (a f) -> p a f", a=16)).then_inc(P.sem("s_stage"), 16)
                    P.emit("sync", sy_y)
                    P.bump("s_stage", 16)

    if cut in ('p3a', 'c0a'):
        if cut == 'p3a':
            conv_tile('h', 0, 0)
        else:
            conv_tile('h', 0, 0)
            conv_tile('h', 0, 4)
            conv_tile('v', 0, 0, dump=dump)
        fin = P.val("s_stage")

        def sy_fc(sy, fin=fin):
            sy.wait_ge(P.sem("s_stage"), fin)
        P.emit("sync", sy_fc)
        return P.build()

    for o in range(2):
        for cb in range(8):
            conv_tile('h', o, cb)
    if cut == 'p3':
        fin = P.val("s_stage")

        def sy_f3(sy, fin=fin):
            sy.wait_ge(P.sem("s_stage"), fin)
        P.emit("sync", sy_f3)
        return P.build()

    for pb in range(4):
        conv_tile('v', 0, pb)
    if cut == 'c0':
        fin = P.val("s_stage")

        def sy_f4(sy, fin=fin):
            sy.wait_ge(P.sem("s_stage"), fin)
        P.emit("sync", sy_f4)
        return P.build()

    for pb in range(4):
        conv_tile('u', 1, pb)
    fin = P.val("s_stage")

    def sy_fin(sy, fin=fin):
        sy.wait_ge(P.sem("s_stage"), fin)
    P.emit("sync", sy_fin)
    return P.build()
